# version 13
# baseline (speedup 1.0000x reference)
import math
import sys
import contextlib

for _p in ("/opt/trn_rl_repo",):
    if _p not in sys.path:
        sys.path.append(_p)

import numpy as np
import ml_dtypes
import concourse.bass as bass
import concourse.mybir as mybir
from concourse.bass_utils import run_bass_kernel_spmd

F32 = mybir.dt.float32
BF16 = mybir.dt.bfloat16
I32 = mybir.dt.int32
AF = mybir.ActivationFunctionType
ALU = mybir.AluOpType
AX = mybir.AxisListType.X

NCORE = 8
B, T, C = 2, 4096, 2048
H, DN, DR, D = 16, 128, 64, 192
QR, KVR = 1536, 512
TOK = 1024
NTC = TOK // 128
ISD = 1.0 / math.sqrt(D)
SHIFT = 12.0
EPS = 1e-5
GROUPS = [[0, 1, 2, 3], [4, 5, 6, 7]]
ENGS = ("sync", "scalar", "vector", "tensor", "gpsimd")
ENG_SEM = {"sync": "ld", "gpsimd": "ldg", "tensor": "tt", "vector": "vv",
           "scalar": "ss"}
ENG_INC = {"sync": 16, "gpsimd": 16, "tensor": 1, "vector": 1, "scalar": 1}


class Slot:
    __slots__ = ("w", "rs")
    def __init__(self):
        self.w = None
        self.rs = []


class Prog:

    def __init__(self):
        self.streams = {e: [] for e in ENGS}
        self.cnt = {"ld": 0, "st": 0, "st2": 0, "ldg": 0, "cc": 0,
                    "tt": 0, "vv": 0, "ss": 0,
                    "lb1": 0, "lw1": 0, "lb2": 0, "lw2": 0, "lw3": 0,
                    "lag": 0, "lag2": 0, "xa": 0, "xb": 0, "qa": 0, "qb": 0,
                    "wa": 0, "wb": 0, "wc": 0, "wd": 0, "sa": 0, "sb": 0,
                    "ka": 0, "kb": 0,
                    "la0": 0, "la1": 0, "sta": 0}
        self.waited = {e: {} for e in ENGS}
        self.sems = {}

    def op(self, eng, build, reads=(), writes=(), sem=None, inc=None,
           defer=False, raw=False):
        deps = {}
        for s in reads:
            if s.w:
                if deps.get(s.w[0], -1) < s.w[1]:
                    deps[s.w[0]] = s.w[1]
        for s in writes:
            toks = ([s.w] if s.w else []) + s.rs
            for (sm, v) in toks:
                if deps.get(sm, -1) < v:
                    deps[sm] = v
        if sem is None:
            sem = ENG_SEM[eng]
        if inc is None:
            inc = ENG_INC[eng]
        w = self.waited[eng]
        waits = []
        for sname, v in deps.items():
            if v > self.cnt[sname]:
                assert sname == sem and v <= self.cnt[sname] + inc, (
                    eng, sem, sname, v, self.cnt[sname], inc)
                continue
            if w.get(sname, 0) < v:
                waits.append((sname, v))
                w[sname] = v
        if defer:
            tok = (sem, self.cnt[sem] + inc)
            do_inc = False
        else:
            self.cnt[sem] += inc
            tok = (sem, self.cnt[sem])
            do_inc = True

        sems = self.sems

        def thunk(E, waits=waits, build=build, tok=tok, inc=inc,
                  do_inc=do_inc, raw=raw, sem=sem):
            for sname, v in waits:
                E.wait_ge(sems[sname], v)
            ins = build(E)
            if do_inc and not raw:
                ins.then_inc(sems[sem], inc)

        self.streams[eng].append(thunk)
        for s in reads:
            s.rs.append(tok)
        for s in writes:
            s.w = tok
            s.rs = []
        return tok

    def bulk_end(self, slots, sem):
        tok = (sem, self.cnt[sem])
        for s in slots:
            s.w = tok
            s.rs = []

    def wait_raw(self, eng, sem, v):
        w = self.waited[eng]
        if w.get(sem, 0) >= v:
            return
        w[eng if False else sem] = v
        sems = self.sems
        self.streams[eng].append(lambda E, s=sem, v=v: E.wait_ge(sems[s], v))

    def barrier(self, exclude=()):
        snap = dict(self.cnt)
        for e in ENGS:
            for s, v in snap.items():
                if v > 0 and s not in exclude:
                    self.wait_raw(e, s, v)

    def flush(self, block):
        for e in ENGS:
            fns = self.streams[e]
            if not fns:
                continue
            getattr(block, e)(lambda E, fns=tuple(fns): [f(E) for f in fns])
            self.streams[e] = []


def build_program(debug=False):
    nc = bass.Bass("TRN2", target_bir_lowering=False)
    p = Prog()

    def dram(name, shape, dtype, kind):
        return nc.dram_tensor(name, shape, dtype, kind=kind).ap()

    x_own = dram("x_own", [TOK, C], F32, "ExternalInput")
    wq_down = dram("wq_down", [C, QR], F32, "ExternalInput")
    wkv_down = dram("wkv_down", [C, KVR], F32, "ExternalInput")
    wq_up = dram("wq_up", [QR, 768], F32, "ExternalInput")
    wk_up = dram("wk_up", [KVR, 768], F32, "ExternalInput")
    wv_up = dram("wv_up", [KVR, 768], F32, "ExternalInput")
    wo_in = dram("wo", [H * D, C], F32, "ExternalInput")
    wt_aug = dram("wt_aug", [C + 1, C], F32, "ExternalInput")
    gate_aug = dram("gate_aug", [C + 1, 8], F32, "ExternalInput")
    lora_at = dram("lora_at", [C, 128], F32, "ExternalInput")
    lora_bc = dram("lora_bc", [128, C], F32, "ExternalInput")
    cdup_in = dram("cdup", [64, T], BF16, "ExternalInput")
    sdup_in = dram("sdup", [64, T], BF16, "ExternalInput")
    maskc = dram("maskc", [128, 4 * 512], BF16, "ExternalInput")
    ident_in = dram("ident", [128, 128], BF16, "ExternalInput")
    gidx = dram("gidx", [4, 1], I32, "ExternalInput")
    out = dram("out", [TOK, C], F32, "ExternalOutput")

    ikind = "ExternalOutput" if debug else "Internal"
    lat_in = dram("lat_in", [C, TOK], BF16, ikind)
    lat_out_kv = dram("lat_out_kv", [4 * KVR, TOK], BF16, ikind)
    lat_out_q = dram("lat_out_q", [4 * QR, TOK], BF16, ikind)
    at_in = [dram(f"at_in{pp}", [T, 384], BF16, ikind) for pp in range(2)]
    at_out = [dram(f"at_out{pp}", [4 * T, 384], BF16, ikind) for pp in range(2)]

    ctx = contextlib.ExitStack()
    with ctx:
        for sname in p.cnt:
            p.sems[sname] = ctx.enter_context(nc.semaphore(sname))
        block = ctx.enter_context(nc.Block())

        csb = ctx.enter_context(nc.sbuf_tensor("csb", [128, 4], F32))
        ident = ctx.enter_context(nc.sbuf_tensor("identsb", [128, 128], BF16))
        ones_row = ctx.enter_context(nc.sbuf_tensor("ones_row", [1, 128], BF16))
        S_csb, S_ident, S_ones = Slot(), Slot(), Slot()
        p.op("vector", lambda E: E.memset(csb[:, 0:1], ISD), writes=[S_csb],
             defer=True)
        p.op("vector", lambda E: E.memset(csb[:, 1:2], -SHIFT), writes=[S_csb])
        p.op("vector", lambda E: E.memset(ones_row[:, :], 1.0), writes=[S_ones])
        p.op("sync", lambda E: E.dma_start(out=ident[:, :], in_=ident_in[:, :]),
             writes=[S_ident], sem="lb1")
        p.bulk_end([S_ident], "lb1")

        def ln_chunk(xb, S_xb, st, S_st, xnb, S_xnb, xsq, S_xsq):
            p.op("vector", lambda E: E.scalar_tensor_tensor(
                xsq[:, :], xb[:, :], 1.0, xb[:, :], ALU.mult, ALU.mult,
                accum_out=st[:, 0:1]),
                reads=[S_xb], writes=[S_xsq, S_st[0]])
            p.op("vector", lambda E: E.reduce_sum(st[:, 1:2], xb[:, :], axis=AX),
                 reads=[S_xb], writes=[S_st[1]])
            p.op("vector", lambda E: E.tensor_scalar(
                st[:, 2:3], st[:, 1:2], 1.0 / C, None, ALU.mult),
                reads=[S_st[1]], writes=[S_st[2]])
            p.op("vector", lambda E: E.tensor_scalar(
                st[:, 3:4], st[:, 0:1], 1.0 / C, EPS, ALU.mult, ALU.add),
                reads=[S_st[0]], writes=[S_st[3]])
            p.op("vector", lambda E: E.scalar_tensor_tensor(
                st[:, 4:5], st[:, 2:3], -1.0, st[:, 2:3], ALU.mult, ALU.mult),
                reads=[S_st[2]], writes=[S_st[4]])
            p.op("vector", lambda E: E.tensor_add(
                st[:, 5:6], st[:, 3:4], st[:, 4:5]),
                reads=[S_st[3], S_st[4]], writes=[S_st[5]])
            p.op("scalar", lambda E: E.activation(st[:, 6:7], st[:, 5:6], AF.Sqrt),
                 reads=[S_st[5]], writes=[S_st[6]])
            p.op("vector", lambda E: E.reciprocal(st[:, 7:8], st[:, 6:7]),
                 reads=[S_st[6]], writes=[S_st[7]])
            p.op("vector", lambda E: E.tensor_mul(st[:, 4:5], st[:, 7:8], st[:, 7:8]),
                 reads=[S_st[7]], writes=[S_st[4]])
            p.op("vector", lambda E: E.tensor_mul(st[:, 4:5], st[:, 4:5], st[:, 5:6]),
                 reads=[S_st[4], S_st[5]], writes=[S_st[4]])
            p.op("vector", lambda E: E.tensor_scalar(
                st[:, 4:5], st[:, 4:5], -0.5, 1.5, ALU.mult, ALU.add),
                reads=[S_st[4]], writes=[S_st[4]])
            p.op("vector", lambda E: E.tensor_mul(st[:, 6:7], st[:, 7:8], st[:, 4:5]),
                 reads=[S_st[7], S_st[4]], writes=[S_st[6]])
            p.op("vector", lambda E: E.scalar_tensor_tensor(
                st[:, 3:4], st[:, 2:3], -1.0, st[:, 6:7], ALU.mult, ALU.mult),
                reads=[S_st[2], S_st[6]], writes=[S_st[3]])
            p.op("scalar", lambda E: E.activation(
                xnb[:, :], xb[:, :], AF.Identity, bias=st[:, 3:4],
                scale=st[:, 6:7]),
                reads=[S_xb, S_st[3], S_st[6]], writes=[S_xnb])

        def transpose128(src_ap, S_srcs, dst_ap, S_dst, ptb, S_ptb):
            p.op("tensor", lambda E: E.transpose(ptb[:, 0:128], src_ap,
                                                 ident[:, :]),
                 reads=list(S_srcs) + [S_ident], writes=[S_ptb])
            p.op("vector", lambda E: E.tensor_copy(dst_ap, ptb[:, 0:128]),
                 reads=[S_ptb], writes=[S_dst])

        S_lat_kv = Slot()
        S_lat_q = Slot()
        with contextlib.ExitStack() as ph1:
            xb = [ph1.enter_context(nc.sbuf_tensor(f"xb{i}", [128, C], F32))
                  for i in range(2)]
            xsq = [ph1.enter_context(nc.sbuf_tensor(f"xsq{i}", [128, C], BF16))
                   for i in range(2)]
            stt = [ph1.enter_context(nc.sbuf_tensor(f"stt{i}", [128, 8], F32))
                   for i in range(2)]
            xnb = [ph1.enter_context(nc.sbuf_tensor(f"xnb{i}", [128, C], BF16))
                   for i in range(2)]
            xn_fm = ph1.enter_context(nc.sbuf_tensor("xn_fm", [128, 16, TOK], BF16))
            wqd = ph1.enter_context(nc.sbuf_tensor("wqd", [128, 16, QR], BF16))
            wkvd = ph1.enter_context(nc.sbuf_tensor("wkvd", [128, 16, KVR], BF16))
            lat_sb = [ph1.enter_context(nc.sbuf_tensor(f"lat_sb{i}", [128, 512], BF16))
                      for i in range(2)]
            ptb1 = [ph1.enter_context(nc.psum_tensor(f"ptb1_{i}", [128, 1024], BF16))
                    for i in range(2)]
            pl1 = [ph1.enter_context(nc.psum_tensor(f"pl1_{i}", [128, 512], F32))
                   for i in range(2)]

            S_xb = [Slot(), Slot()]
            S_xsq = [Slot(), Slot()]
            S_stt = [[Slot() for _ in range(8)] for _ in range(2)]
            S_xnb = [Slot(), Slot()]
            S_xn = [[Slot() for _ in range(NTC)] for _ in range(16)]
            S_wqd = [Slot() for _ in range(16)]
            S_wkvd = [Slot() for _ in range(16)]
            S_latsb = [Slot(), Slot()]
            S_ptb1 = [Slot(), Slot()]
            S_pl1 = [Slot(), Slot()]

            for k in range(16):
                p.op("gpsimd", lambda E, k=k: E.dma_start(
                    out=wqd[:, k, :], in_=wq_down[k * 128:(k + 1) * 128, :]),
                    writes=[S_wqd[k]], sem="lw1")
                p.op("gpsimd", lambda E, k=k: E.dma_start(
                    out=wkvd[:, k, :], in_=wkv_down[k * 128:(k + 1) * 128, :]),
                    writes=[S_wkvd[k]], sem="lw1")
            p.bulk_end(S_wqd + S_wkvd, "lw1")

            def emit_lat(mi_list, nt):
                tiles = []
                for mi in mi_list:
                    bank = mi % 2
                    for k in range(16):
                        if mi < 12:
                            lhs = wqd[:, k, mi * 128:(mi + 1) * 128]
                            rsl = S_wqd[k]
                        else:
                            lhs = wkvd[:, k, (mi - 12) * 128:(mi - 11) * 128]
                            rsl = S_wkvd[k]
                        p.op("tensor",
                             lambda E, lhs=lhs, k=k, nt=nt, bank=bank:
                             E.matmul(pl1[bank][:, :], lhs,
                                      xn_fm[:, k, nt * 512:(nt + 1) * 512],
                                      start=(k == 0), stop=(k == 15)),
                             reads=[rsl] + [S_xn[k][nt * 4 + j]
                                            for j in range(4)],
                             writes=[S_pl1[bank]], defer=(k != 15))
                    sbuf = lat_sb[bank]
                    p.op("vector", lambda E, sbuf=sbuf, bank=bank:
                         E.tensor_copy(sbuf[:, :], pl1[bank][:, :]),
                         reads=[S_pl1[bank]], writes=[S_latsb[bank]])
                    S_tile = Slot()
                    tiles.append(S_tile)
                    p.op("sync", lambda E, sbuf=sbuf, mi=mi, nt=nt:
                         E.dma_start(
                             out=lat_in[mi * 128:(mi + 1) * 128,
                                        nt * 512:(nt + 1) * 512],
                             in_=sbuf[:, :]),
                         reads=[S_latsb[bank]], writes=[S_tile],
                         sem=("la0", "la1")[bank])
                return tiles

            def emit_ln(tcs):
                for tc in tcs:
                    i = tc % 2
                    p.op("sync", lambda E, tc=tc, i=i: E.dma_start(
                        out=xb[i][:, :], in_=x_own[tc * 128:(tc + 1) * 128, :]),
                        writes=[S_xb[i]], sem=("xa", "xb")[i])
                    ln_chunk(xb[i], S_xb[i], stt[i], S_stt[i], xnb[i], S_xnb[i],
                             xsq[i], S_xsq[i])
                    for k in range(16):
                        transpose128(xnb[i][:, k * 128:(k + 1) * 128], [S_xnb[i]],
                                     xn_fm[:, k, tc * 128:(tc + 1) * 128],
                                     S_xn[k][tc], ptb1[k % 2], S_ptb1[k % 2])

            emit_ln(range(0, 4))
            kv_tiles = emit_lat([12, 13, 14, 15], 0)
            emit_ln(range(4, 8))
            kv_tiles += emit_lat([12, 13, 14, 15], 1)
            p.op("gpsimd", lambda E: E.collective_compute(
                "AllGather", ALU.bypass, replica_groups=GROUPS,
                ins=[lat_in[QR:C, :]], outs=[lat_out_kv[:, :]]),
                reads=kv_tiles, writes=[S_lat_kv], sem="cc", inc=1)
            q_tiles = emit_lat(list(range(12)), 0) + emit_lat(list(range(12)), 1)
            p.op("gpsimd", lambda E: E.collective_compute(
                "AllGather", ALU.bypass, replica_groups=GROUPS,
                ins=[lat_in[0:QR, :]], outs=[lat_out_q[:, :]]),
                reads=q_tiles, writes=[S_lat_q], sem="cc", inc=1)

            p.barrier(exclude=("cc",))
            p.flush(block)

        S_at_in = [Slot(), Slot()]
        S_at_out = [Slot(), Slot()]
        with contextlib.ExitStack() as ph2:
            kvt = [ph2.enter_context(nc.sbuf_tensor(f"kvt{i}", [128, 4, 512], BF16))
                   for i in range(2)]
            qlb = [ph2.enter_context(nc.sbuf_tensor(f"qlb{i}", [128, 12, 512], BF16))
                   for i in range(2)]
            qc2 = ph2.enter_context(nc.sbuf_tensor("qc2", [128, 2, T], BF16))
            qr2 = ph2.enter_context(nc.sbuf_tensor("qr2", [128, T], BF16))
            kc2 = ph2.enter_context(nc.sbuf_tensor("kc2", [128, 2, T], BF16))
            kr2 = ph2.enter_context(nc.sbuf_tensor("kr2", [128, 2, T], BF16))
            v2 = ph2.enter_context(nc.sbuf_tensor("v2", [128, 32, 386], BF16))
            ath = ph2.enter_context(nc.sbuf_tensor("ath", [128, 32, 384], BF16))
            cds = ph2.enter_context(nc.sbuf_tensor("cds", [64, T], BF16))
            sns = ph2.enter_context(nc.sbuf_tensor("sns", [64, T], BF16))
            msk = ph2.enter_context(nc.sbuf_tensor("msk", [128, 4 * 512], BF16))
            wqu = ph2.enter_context(nc.sbuf_tensor("wqu", [128, 12, 768], BF16))
            wku = ph2.enter_context(nc.sbuf_tensor("wku", [128, 4, 768], BF16))
            wvu = ph2.enter_context(nc.sbuf_tensor("wvu", [128, 4, 768], BF16))
            ptile = [ph2.enter_context(nc.sbuf_tensor(f"ptile{i}", [128, 512], BF16))
                     for i in range(3)]
            rA = ph2.enter_context(nc.sbuf_tensor("rA", [128, 512], F32))
            rB = ph2.enter_context(nc.sbuf_tensor("rB", [128, 512], F32))
            rec = ph2.enter_context(nc.sbuf_tensor("rec", [128, 4], F32))
            psT = [ph2.enter_context(nc.psum_tensor(f"ps2_{i}", [128, 512], F32))
                   for i in range(8)]
            ps = [t[:, :] for t in psT]

            S_kvt = [Slot(), Slot()]
            S_qlb = [Slot(), Slot()]
            S_qc = [[Slot() for _ in range(8)] for _ in range(2)]
            S_qr = [Slot() for _ in range(8)]
            S_kc = [[Slot() for _ in range(8)] for _ in range(2)]
            S_kr = [Slot() for _ in range(8)]
            S_v = [[Slot() for _ in range(2)] for _ in range(32)]
            S_vones = Slot()
            S_ath = [[Slot() for _ in range(2)] for _ in range(32)]
            S_cds, S_sns, S_msk = Slot(), Slot(), Slot()
            S_wqu = [Slot() for _ in range(12)]
            S_wku = [Slot() for _ in range(4)]
            S_wvu = [Slot() for _ in range(4)]
            S_pt = [Slot() for _ in range(3)]
            S_rA, S_rB, S_rec = Slot(), Slot(), Slot()
            S_ps = [Slot() for _ in range(8)]

            p.op("sync", lambda E: E.dma_start(out=cds[:, :], in_=cdup_in[:, :]),
                 writes=[S_cds], sem="lb2")
            p.op("sync", lambda E: E.dma_start(out=sns[:, :], in_=sdup_in[:, :]),
                 writes=[S_sns], sem="lb2")
            p.op("sync", lambda E: E.dma_start(out=msk[:, :], in_=maskc[:, :]),
                 writes=[S_msk], sem="lb2")
            for k in range(12):
                p.op("gpsimd", lambda E, k=k: E.dma_start(
                    out=wqu[:, k, :], in_=wq_up[k * 128:(k + 1) * 128, :]),
                    writes=[S_wqu[k]], sem="lw2")
            for k in range(4):
                p.op("gpsimd", lambda E, k=k: E.dma_start(
                    out=wku[:, k, :], in_=wk_up[k * 128:(k + 1) * 128, :]),
                    writes=[S_wku[k]], sem="lw2")
                p.op("gpsimd", lambda E, k=k: E.dma_start(
                    out=wvu[:, k, :], in_=wv_up[k * 128:(k + 1) * 128, :]),
                    writes=[S_wvu[k]], sem="lw2")
            p.bulk_end(S_wqu + S_wku + S_wvu, "lw2")
            p.bulk_end([S_cds, S_sns, S_msk], "lb2")
            p.op("vector", lambda E: E.memset(v2[:, :, 192:193], 1.0),
                 writes=[S_vones], defer=True)
            p.op("vector", lambda E: E.memset(v2[:, :, 385:386], 1.0),
                 writes=[S_vones])
            S_krz = Slot()
            p.op("vector", lambda E: E.memset(kr2[64:128, 0, :], 0.0),
                 writes=[S_krz], defer=True)
            p.op("vector", lambda E: E.memset(kr2[0:64, 1, :], 0.0),
                 writes=[S_krz])

            def rope_store(pbank, S_pbank, t0, dstq, S_dst, dsts=None):
                n = 512
                p.op("vector", lambda E: E.tensor_mul(
                    rA[0:64, :], pbank[0:64, :], cds[:, t0:t0 + n]),
                    reads=[S_pbank, S_cds], writes=[S_rA], defer=True)
                p.op("vector", lambda E: E.tensor_mul(
                    rA[64:128, :], pbank[64:128, :], cds[:, t0:t0 + n]),
                    reads=[S_pbank, S_cds], writes=[S_rA])
                p.op("vector", lambda E: E.tensor_mul(
                    rB[0:64, :], pbank[64:128, :], sns[:, t0:t0 + n]),
                    reads=[S_pbank, S_sns], writes=[S_rB], defer=True)
                p.op("vector", lambda E: E.tensor_mul(
                    rB[64:128, :], pbank[0:64, :], sns[:, t0:t0 + n]),
                    reads=[S_pbank, S_sns], writes=[S_rB])
                for hh in range(2):
                    d1 = (dstq[hh * 64:hh * 64 + 32, :] if dsts is None
                          else dsts[hh][0])
                    d2 = (dstq[hh * 64 + 32:hh * 64 + 64, :] if dsts is None
                          else dsts[hh][1])
                    p.op("vector", lambda E, hh=hh, d1=d1: E.tensor_sub(
                        d1, rA[hh * 32:hh * 32 + 32, :],
                        rB[hh * 32:hh * 32 + 32, :]),
                        reads=[S_rA, S_rB], writes=[S_dst], defer=True)
                    p.op("vector", lambda E, hh=hh, d2=d2: E.tensor_add(
                        d2, rB[64 + hh * 32:96 + hh * 32, :],
                        rA[64 + hh * 32:96 + hh * 32, :]),
                        reads=[S_rA, S_rB], writes=[S_dst],
                        defer=(hh == 0))

            for pp in range(2):
                for r in range(4):
                    for nh in range(2):
                        i = (r * 2 + nh) % 2
                        p.op("sync", lambda E, r=r, nh=nh, i=i: E.dma_start(
                            out=qlb[i][:, :, :],
                            in_=lat_out_q[r * QR:(r + 1) * QR,
                                          nh * 512:(nh + 1) * 512].rearrange(
                                "(kq pq) n -> pq kq n", pq=128)),
                            reads=[S_lat_q], writes=[S_qlb[i]],
                            sem=("qa", "qb")[i])
                        t0 = r * TOK + nh * 512
                        for mj, mi in enumerate((2 * pp, 2 * pp + 1, 4 + pp)):
                            for k in range(12):
                                p.op("tensor",
                                     lambda E, mi=mi, k=k, i=i, mj=mj:
                                     E.matmul(ps[mj],
                                              wqu[:, k, mi * 128:(mi + 1) * 128],
                                              qlb[i][:, k, :],
                                              start=(k == 0), stop=(k == 11)),
                                     reads=[S_wqu[k], S_qlb[i]],
                                     writes=[S_ps[mj]], defer=(k != 11))
                        for mj in range(2):
                            p.op("vector",
                                 lambda E, mj=mj, t0=t0: E.tensor_copy(
                                     qc2[:, mj, t0:t0 + 512], ps[mj]),
                                 reads=[S_ps[mj]], writes=[S_qc[mj][t0 // 512]])
                        rope_store(ps[2], S_ps[2], t0,
                                   qr2[:, t0:t0 + 512], S_qr[t0 // 512])

                for nt in range(8):
                    rr = nt // 2
                    ch = nt % 2
                    i = nt % 2
                    p.op("sync", lambda E, rr=rr, ch=ch, i=i: E.dma_start(
                        out=kvt[i][:, :, :],
                        in_=lat_out_kv[rr * KVR:(rr + 1) * KVR,
                                       ch * 512:(ch + 1) * 512].rearrange(
                            "(kk pk) n -> pk kk n", pk=128)),
                        reads=[S_lat_kv], writes=[S_kvt[i]],
                        sem=("ka", "kb")[i])
                    for mj, mi in enumerate((2 * pp, 2 * pp + 1, 4 + pp)):
                        for kk in range(4):
                            p.op("tensor",
                                 lambda E, mi=mi, kk=kk, i=i, mj=mj: E.matmul(
                                     ps[mj],
                                     wku[:, kk, mi * 128:(mi + 1) * 128],
                                     kvt[i][:, kk, :],
                                     start=(kk == 0), stop=(kk == 3)),
                                 reads=[S_wku[kk], S_kvt[i]],
                                 writes=[S_ps[mj]], defer=(kk != 3))
                    for mj in range(2):
                        p.op("vector", lambda E, mj=mj, nt=nt: E.tensor_copy(
                            kc2[:, mj, nt * 512:(nt + 1) * 512], ps[mj]),
                            reads=[S_ps[mj]], writes=[S_kc[mj][nt]])
                    rope_store(
                        ps[2], S_ps[2], nt * 512, None, S_kr[nt],
                        dsts=[(kr2[0:32, 0, nt * 512:(nt + 1) * 512],
                               kr2[32:64, 0, nt * 512:(nt + 1) * 512]),
                              (kr2[64:96, 1, nt * 512:(nt + 1) * 512],
                               kr2[96:128, 1, nt * 512:(nt + 1) * 512])])
                    for tcl in range(4):
                        tc = nt * 4 + tcl
                        bank = (3, 4, 5, 7)[tcl]
                        for kk in range(4):
                            p.op("tensor",
                                 lambda E, kk=kk, tcl=tcl, pp=pp, i=i, bank=bank:
                                 E.matmul(
                                     ps[bank][:, 0:384],
                                     kvt[i][:, kk, tcl * 128:(tcl + 1) * 128],
                                     wvu[:, kk, 2 * pp * 192:2 * pp * 192 + 384],
                                     start=(kk == 0), stop=(kk == 3)),
                                 reads=[S_wvu[kk], S_kvt[i]],
                                 writes=[S_ps[bank]], defer=(kk != 3))
                        p.op("vector",
                             lambda E, tc=tc, bank=bank: E.tensor_copy(
                                 v2[:, tc, 0:386].rearrange(
                                     "p (h x) -> p h x", h=2)[:, :, 0:192],
                                 ps[bank][:, 0:384].rearrange(
                                     "p (h x) -> p h x", h=2)),
                             reads=[S_ps[bank]],
                             writes=[S_v[tc][0], S_v[tc][1]])

                for hl in range(2):
                    for qt in range(8):
                        nsc = (qt + 1) * 4
                        q0 = qt * 512

                        def emit_scores(sc, hl=hl, q0=q0):
                            sb = 4 + (sc % 3)
                            p.op("tensor",
                                 lambda E, hl=hl, sc=sc, q0=q0, sb=sb: E.matmul(
                                     ps[sb],
                                     kc2[:, hl, sc * 128:(sc + 1) * 128],
                                     qc2[:, hl, q0:q0 + 512],
                                     start=True, stop=False),
                                 reads=[S_kc[hl][sc // 4], S_qc[hl][q0 // 512]],
                                 writes=[S_ps[sb]], defer=True)
                            p.op("tensor",
                                 lambda E, hl=hl, sc=sc, q0=q0, sb=sb: E.matmul(
                                     ps[sb],
                                     kr2[:, hl, sc * 128:(sc + 1) * 128],
                                     qr2[:, q0:q0 + 512],
                                     start=False, stop=True),
                                 reads=[S_kr[sc // 4], S_qr[q0 // 512], S_krz],
                                 writes=[S_ps[sb]])

                        def emit_exp(sc, qt=qt):
                            sb = 4 + (sc % 3)
                            pt = ptile[sc % 3]
                            ipt = sc % 3
                            p.op("scalar", lambda E, sb=sb, pt=pt: E.activation(
                                pt[:, :], ps[sb], AF.Exp,
                                bias=csb[:, 1:2], scale=csb[:, 0:1]),
                                reads=[S_ps[sb], S_csb], writes=[S_pt[ipt]])
                            if sc >= qt * 4:
                                off = (sc - qt * 4) * 512
                                p.op("vector", lambda E, pt=pt, off=off:
                                     E.tensor_mul(pt[:, :], pt[:, :],
                                                  msk[:, off:off + 512]),
                                     reads=[S_pt[ipt], S_msk],
                                     writes=[S_pt[ipt]])

                        def emit_pv(sc, hl=hl, nsc=nsc):
                            pt = ptile[sc % 3]
                            ipt = sc % 3
                            for qc4 in range(4):
                                p.op("tensor",
                                     lambda E, pt=pt, qc4=qc4, sc=sc, hl=hl,
                                     nsc=nsc: E.matmul(
                                         ps[qc4][:, 0:193],
                                         pt[:, qc4 * 128:(qc4 + 1) * 128],
                                         v2[:, sc, hl * 193:(hl + 1) * 193],
                                         start=(sc == 0), stop=(sc == nsc - 1),
                                         skip_group_check=True),
                                     reads=[S_pt[ipt], S_v[sc][hl], S_vones],
                                     writes=[S_ps[qc4]], defer=(qc4 != 3))

                        emit_scores(0)
                        if nsc > 1:
                            emit_scores(1)
                        for sc in range(nsc):
                            emit_exp(sc)
                            if sc + 2 < nsc:
                                emit_scores(sc + 2)
                            emit_pv(sc)
                        for qc4 in range(4):
                            tc = qt * 4 + qc4
                            p.op("vector", lambda E, qc4=qc4: E.reciprocal(
                                rec[:, qc4:qc4 + 1], ps[qc4][:, 192:193]),
                                reads=[S_ps[qc4]], writes=[S_rec])
                            p.op("vector",
                                 lambda E, qc4=qc4, tc=tc, hl=hl: E.tensor_scalar(
                                     ath[:, tc, hl * 192:(hl + 1) * 192],
                                     ps[qc4][:, 0:192], rec[:, qc4:qc4 + 1],
                                     None, ALU.mult),
                                 reads=[S_ps[qc4], S_rec], writes=[S_ath[tc][hl]])

                ath_deps = [S_ath[tc][hl] for tc in range(32) for hl in range(2)]
                S_store = Slot()
                for q4 in range(4):
                    p.op("sync", lambda E, q4=q4, pp=pp: E.dma_start(
                        out=at_in[pp][q4 * 1024:(q4 + 1) * 1024, :].rearrange(
                            "(c p) f -> p c f", p=128),
                        in_=ath[:, q4 * 8:(q4 + 1) * 8, :]),
                        reads=ath_deps, writes=[S_store], sem="sta")
                p.op("gpsimd", lambda E, pp=pp: E.collective_compute(
                    "AllGather", ALU.bypass, replica_groups=GROUPS,
                    ins=[at_in[pp][:, :]], outs=[at_out[pp][:, :]]),
                    reads=[S_store, S_at_in[pp]], writes=[S_at_out[pp]],
                    sem="cc", inc=1)

            p.barrier(exclude=("cc",))
            p.flush(block)

        with contextlib.ExitStack() as ph3:
            xn3 = ph3.enter_context(nc.sbuf_tensor("xn3", [128, 16, TOK], BF16))
            afm = ph3.enter_context(nc.sbuf_tensor("afm", [128, 24, TOK], BF16))
            gsb = ph3.enter_context(nc.sbuf_tensor("gsb", [128, 17, 8], BF16))
            lgs = ph3.enter_context(nc.sbuf_tensor("lgs", [128, 8], F32))
            t2 = ph3.enter_context(nc.sbuf_tensor("t2", [128, 32], F32))
            wdn = ph3.enter_context(nc.sbuf_tensor("wdn", [128, 8, 8], F32))
            lat3 = ph3.enter_context(nc.sbuf_tensor("lat3", [128, 16, 128], BF16))
            lbc = ph3.enter_context(nc.sbuf_tensor("lbc", [128, C], BF16))
            xag = ph3.enter_context(nc.sbuf_tensor("xag", [128, 128], BF16))
            xafm = ph3.enter_context(nc.sbuf_tensor("xafm", [128, TOK], BF16))
            bias_sb = ph3.enter_context(nc.sbuf_tensor("bias_sb", [1, C], BF16))
            xsl = [ph3.enter_context(nc.sbuf_tensor(f"xsl{i}", [128, 512], F32))
                   for i in range(2)]
            wst = [ph3.enter_context(nc.sbuf_tensor(f"wst{i}", [128, 512], F32))
                   for i in range(4)]
            osb = [ph3.enter_context(nc.sbuf_tensor(f"osb{i}", [128, 512], F32))
                   for i in range(2)]
            ptb3 = [ph3.enter_context(nc.psum_tensor(f"ptb3_{i}", [128, 1024], BF16))
                    for i in range(2)]
            pf = [ph3.enter_context(nc.psum_tensor(f"pf{i}", [128, 512], F32))
                  for i in range(4)]
            ph3a = contextlib.ExitStack()
            aown = ph3a.enter_context(nc.sbuf_tensor("aown", [128, 8, 3072], BF16))
            xb3 = [ph3a.enter_context(nc.sbuf_tensor(f"xb3_{i}", [128, C], F32))
                   for i in range(2)]
            xsq3 = [ph3a.enter_context(nc.sbuf_tensor(f"xsq3_{i}", [128, C], BF16))
                    for i in range(2)]
            st3 = [ph3a.enter_context(nc.sbuf_tensor(f"st3_{i}", [128, 8], F32))
                   for i in range(2)]
            xnb3 = [ph3a.enter_context(nc.sbuf_tensor(f"xnb3_{i}", [128, C], BF16))
                    for i in range(2)]

            S_aownc = [Slot() for _ in range(8)]
            S_xb3 = [Slot(), Slot()]
            S_xsq3 = [Slot(), Slot()]
            S_st3 = [[Slot() for _ in range(8)] for _ in range(2)]
            S_xnb3 = [Slot(), Slot()]
            S_xn3 = [[Slot() for _ in range(NTC)] for _ in range(16)]
            S_afm = [[Slot() for _ in range(NTC)] for _ in range(24)]
            S_gsb = Slot()
            S_lgs = Slot()
            S_t2 = Slot()
            S_wdn = [Slot() for _ in range(8)]
            S_lat3 = [Slot() for _ in range(16)]
            S_lbc = Slot()
            S_xag = Slot()
            S_xafm = [Slot() for _ in range(8)]
            S_bias = Slot()
            S_wnt = [Slot(), Slot()]
            S_wst = [Slot() for _ in range(4)]
            S_xsl = [Slot(), Slot()]
            S_osb = [Slot(), Slot()]
            S_ptb3 = [Slot(), Slot()]
            S_pf = [Slot() for _ in range(4)]
            S_out = Slot()

            for k in range(16):
                p.op("gpsimd", lambda E, k=k: E.dma_start(
                    out=gsb[:, k, :], in_=gate_aug[k * 128:(k + 1) * 128, :]),
                    writes=[S_gsb], sem="lw3")
            p.op("gpsimd", lambda E: E.dma_start(
                out=gsb[0:1, 16, :], in_=gate_aug[C:C + 1, :]),
                writes=[S_gsb], sem="lw3")
            p.op("gpsimd", lambda E: E.dma_start(
                out=lat3[:, :, :],
                in_=lora_at[:, :].rearrange("(kq p) r -> p kq r", p=128)),
                writes=S_lat3, sem="lw3")
            p.op("gpsimd", lambda E: E.dma_start(out=lbc[:, :], in_=lora_bc[:, :]),
                 writes=[S_lbc], sem="lw3")
            p.op("gpsimd", lambda E: E.dma_start(
                out=bias_sb[:, :], in_=wt_aug[C:C + 1, :]),
                writes=[S_bias], sem="lw3")
            p.bulk_end([S_gsb, S_lbc, S_bias] + S_lat3, "lw3")

            cc_toks = [S_at_out[0].w, S_at_out[1].w]

            def aown_dmas(E):
                sems = p.sems
                last = None
                with E.register("goff") as rg:
                    offs = []
                    for r in range(4):
                        E.reg_load(rg, gidx[r:r + 1, 0:1])
                        offs.append(E.snap(rg, min_val=0,
                                           max_val=3 * T + 3 * TOK))
                    for ppx in range(2):
                        E.wait_ge(sems[cc_toks[ppx][0]], cc_toks[ppx][1])
                        for r in range(4):
                            col = r * 768 + ppx * 384
                            last = E.dma_start(
                                out=aown[:, :, col:col + 384],
                                in_=at_out[ppx][bass.ds(offs[r], TOK), :].rearrange(
                                    "(c p) f -> p c f", p=128))
                            last.then_inc(sems[("lag", "lag2")[ppx]], 16)
                return last

            p.op("gpsimd", aown_dmas, reads=[],
                 writes=S_aownc, sem="lag", inc=4 * 16, raw=True)
            p.cnt["lag2"] += 4 * 16
            for r in range(4):
                S_aownc[2 * r + 0].w = ("lag", p.cnt["lag"])
                S_aownc[2 * r + 1].w = ("lag2", p.cnt["lag2"])

            for tc in range(NTC):
                i = tc % 2
                p.op("sync", lambda E, tc=tc, i=i: E.dma_start(
                    out=xb3[i][:, :], in_=x_own[tc * 128:(tc + 1) * 128, :]),
                    writes=[S_xb3[i]], sem=("xa", "xb")[i])
                ln_chunk(xb3[i], S_xb3[i], st3[i], S_st3[i], xnb3[i],
                         S_xnb3[i], xsq3[i], S_xsq3[i])
                for k in range(16):
                    transpose128(xnb3[i][:, k * 128:(k + 1) * 128], [S_xnb3[i]],
                                 xn3[:, k, tc * 128:(tc + 1) * 128],
                                 S_xn3[k][tc], ptb3[k % 2], S_ptb3[k % 2])

            for qc in range(8):
                for k in range(16):
                    p.op("tensor", lambda E, k=k, qc=qc: E.matmul(
                        pf[0][:, 0:8], xn3[:, k, qc * 128:(qc + 1) * 128],
                        gsb[:, k, :], start=(k == 0), stop=False),
                        reads=[S_xn3[k][qc], S_gsb], writes=[S_pf[0]],
                        defer=True)
                p.op("tensor", lambda E: E.matmul(
                    pf[0][:, 0:8], ones_row[:, :], gsb[0:1, 16, :],
                    start=False, stop=True),
                    reads=[S_ones, S_gsb], writes=[S_pf[0]])
                p.op("vector", lambda E: E.tensor_copy(lgs[:, :], pf[0][:, 0:8]),
                     reads=[S_pf[0]], writes=[S_lgs])
                p.op("vector", lambda E: E.reduce_max(t2[:, 8:9], lgs[:, :], axis=AX),
                     reads=[S_lgs], writes=[S_t2])
                p.op("vector", lambda E, qc=qc: E.tensor_tensor(
                    wdn[:, qc, :], lgs[:, :],
                    t2[:, 8:9].to_broadcast((128, 8)), ALU.is_ge),
                    reads=[S_lgs, S_t2], writes=[S_wdn[qc]])
                p.op("vector", lambda E, qc=qc: E.scalar_tensor_tensor(
                    t2[:, 16:24], wdn[:, qc, :], -1e30, lgs[:, :],
                    ALU.mult, ALU.add),
                    reads=[S_lgs, S_wdn[qc]], writes=[S_t2])
                p.op("vector", lambda E: E.reduce_max(
                    t2[:, 9:10], t2[:, 16:24], axis=AX),
                    reads=[S_t2], writes=[S_t2])
                p.op("vector", lambda E: E.tensor_tensor(
                    t2[:, 24:32], t2[:, 16:24],
                    t2[:, 9:10].to_broadcast((128, 8)), ALU.is_ge),
                    reads=[S_t2], writes=[S_t2])
                p.op("vector", lambda E: E.tensor_sub(
                    t2[:, 10:11], t2[:, 8:9], t2[:, 9:10]),
                    reads=[S_t2], writes=[S_t2])
                p.op("scalar", lambda E: E.activation(
                    t2[:, 11:12], t2[:, 10:11], AF.Sigmoid),
                    reads=[S_t2], writes=[S_t2])
                p.op("scalar", lambda E: E.activation(
                    t2[:, 12:13], t2[:, 10:11], AF.Sigmoid, scale=-1.0),
                    reads=[S_t2], writes=[S_t2])
                p.op("vector", lambda E, qc=qc: E.tensor_scalar(
                    wdn[:, qc, :], wdn[:, qc, :], t2[:, 11:12], None, ALU.mult),
                    reads=[S_wdn[qc], S_t2], writes=[S_wdn[qc]])
                p.op("vector", lambda E, qc=qc: E.scalar_tensor_tensor(
                    wdn[:, qc, :], t2[:, 24:32], t2[:, 12:13], wdn[:, qc, :],
                    ALU.mult, ALU.add),
                    reads=[S_t2, S_wdn[qc]], writes=[S_wdn[qc]])
                for k in range(16):
                    p.op("tensor", lambda E, k=k, qc=qc: E.matmul(
                        pf[1][:, 0:128], xn3[:, k, qc * 128:(qc + 1) * 128],
                        lat3[:, k, :], start=(k == 0), stop=(k == 15)),
                        reads=[S_xn3[k][qc], S_lat3[k]], writes=[S_pf[1]],
                        defer=(k != 15))
                p.op("vector", lambda E, qc=qc: E.tensor_tensor(
                    xag[:, :].rearrange("p (e r) -> p e r", r=16),
                    pf[1][:, 0:128].rearrange("p (e r) -> p e r", r=16),
                    wdn[:, qc, :][:, :, None].to_broadcast((128, 8, 16)),
                    ALU.mult),
                    reads=[S_pf[1], S_wdn[qc]], writes=[S_xag])
                transpose128(xag[:, :], [S_xag], xafm[:, qc * 128:(qc + 1) * 128],
                             S_xafm[qc], ptb3[0], S_ptb3[0])


            for tc in range(NTC):
                for k in range(24):
                    transpose128(aown[:, tc, k * 128:(k + 1) * 128],
                                 [S_aownc[k // 3]],
                                 afm[:, k, tc * 128:(tc + 1) * 128],
                                 S_afm[k][tc], ptb3[k % 2], S_ptb3[k % 2])

            for k in range(4):
                srcap = wo_in[k * 128:(k + 1) * 128, 0:512]
                p.op("sync", lambda E, srcap=srcap, k=k: E.dma_start(
                    out=wst[k][:, :], in_=srcap),
                    writes=[S_wst[k]], sem=("wa", "wb", "wc", "wd")[k])
            p.barrier(exclude=("cc", "wa", "wb", "wc", "wd"))
            ph3a.close()
            wnt = [ph3.enter_context(nc.sbuf_tensor(f"wnt{i}", [128, 41, 512], BF16))
                   for i in range(2)]

            def emit_w(nt):
                wb = nt % 2
                wslot = S_wnt[wb]
                for k in range(41):
                    if k < 24:
                        srcap = wo_in[k * 128:(k + 1) * 128,
                                      nt * 512:(nt + 1) * 512]
                    elif k < 40:
                        srcap = wt_aug[(k - 24) * 128:(k - 23) * 128,
                                       nt * 512:(nt + 1) * 512]
                    else:
                        srcap = lora_bc[:, nt * 512:(nt + 1) * 512]
                    j = k % 4
                    if nt > 0 or k >= 4:
                        p.op("sync", lambda E, srcap=srcap, j=j: E.dma_start(
                            out=wst[j][:, :], in_=srcap),
                            writes=[S_wst[j]], sem=("wa", "wb", "wc", "wd")[j])
                    p.op("vector", lambda E, k=k, wb=wb, j=j: E.tensor_copy(
                        wnt[wb][:, k, :], wst[j][:, :]),
                        reads=[S_wst[j]], writes=[wslot])

            emit_w(0)
            for nt in range(4):
                wb = nt % 2
                wslot = S_wnt[wb]
                if nt + 1 < 4:
                    emit_w(nt + 1)
                for qc in range(8):
                    bank = 2 + qc % 2
                    for k in range(24):
                        p.op("tensor", lambda E, k=k, qc=qc, wb=wb, bank=bank:
                             E.matmul(pf[bank][:, :],
                                      afm[:, k, qc * 128:(qc + 1) * 128],
                                      wnt[wb][:, k, :],
                                      start=(k == 0), stop=False),
                             reads=[S_afm[k][qc], wslot], writes=[S_pf[bank]],
                             defer=True)
                    for k in range(16):
                        p.op("tensor", lambda E, k=k, qc=qc, wb=wb, bank=bank:
                             E.matmul(pf[bank][:, :],
                                      xn3[:, k, qc * 128:(qc + 1) * 128],
                                      wnt[wb][:, 24 + k, :],
                                      start=False, stop=False),
                             reads=[S_xn3[k][qc], wslot], writes=[S_pf[bank]],
                             defer=True)
                    p.op("tensor", lambda E, qc=qc, nt=nt, bank=bank: E.matmul(
                        pf[bank][:, :], ones_row[:, :],
                        bias_sb[:, nt * 512:(nt + 1) * 512],
                        start=False, stop=False),
                        reads=[S_ones, S_bias], writes=[S_pf[bank]], defer=True)
                    p.op("tensor", lambda E, qc=qc, wb=wb, bank=bank: E.matmul(
                        pf[bank][:, :], xafm[:, qc * 128:(qc + 1) * 128],
                        wnt[wb][:, 40, :], start=False, stop=True),
                        reads=[S_xafm[qc], wslot], writes=[S_pf[bank]])
                    i = qc % 2
                    p.op("sync", lambda E, qc=qc, nt=nt, i=i: E.dma_start(
                        out=xsl[i][:, :],
                        in_=x_own[qc * 128:(qc + 1) * 128,
                                  nt * 512:(nt + 1) * 512]),
                        writes=[S_xsl[i]], sem=("sa", "sb")[i])
                    p.op("vector", lambda E, i=i, bank=bank: E.tensor_add(
                        osb[i][:, :], pf[bank][:, :], xsl[i][:, :]),
                        reads=[S_pf[bank], S_xsl[i]], writes=[S_osb[i]])
                    p.op("sync", lambda E, qc=qc, nt=nt, i=i: E.dma_start(
                        out=out[qc * 128:(qc + 1) * 128,
                                nt * 512:(nt + 1) * 512],
                        in_=osb[i][:, :]),
                        reads=[S_osb[i]], writes=[S_out], sem=("st", "st2")[i])

            p.wait_raw("sync", "st", p.cnt["st"])
            p.wait_raw("sync", "st2", p.cnt["st2"])
            p.flush(block)

    return nc


_CACHE = {}


def _host_const_tables():
    inv = 1.0 / (10000.0 ** (np.arange(0, DR, 2, dtype=np.float64) / DR))
    t = np.arange(T, dtype=np.float64)
    fr = np.outer(inv, t)
    cos = np.cos(fr).astype(np.float32)
    sin = np.sin(fr).astype(np.float32)
    cdup = np.concatenate([cos, cos], axis=0).astype(ml_dtypes.bfloat16)
    sdup = np.concatenate([sin, sin], axis=0).astype(ml_dtypes.bfloat16)
    qf = np.arange(512)
    sl = np.arange(128)
    msk = np.concatenate(
        [(qf[None, :] >= (v * 128 + sl[:, None])) for v in range(4)],
        axis=1).astype(ml_dtypes.bfloat16)
    ident = np.eye(128, dtype=ml_dtypes.bfloat16)
    return cdup, sdup, msk, ident


def _prep_shared(I):
    W = np.asarray(I["L0_ffn_W"], dtype=np.float32)
    b = np.asarray(I["L0_ffn_b"], dtype=np.float32)
    wt_aug = np.ascontiguousarray(
        np.concatenate([W.T, b[None, :]], axis=0), dtype=np.float32)
    gw = np.asarray(I["L0_ffn_gate_w"], dtype=np.float32)
    gb = np.asarray(I["L0_ffn_gate_b"], dtype=np.float32)
    gate_aug = np.ascontiguousarray(
        np.concatenate([gw.T, gb[None, :]], axis=0), dtype=np.float32)
    lA = np.asarray(I["L0_ffn_lora_A"], dtype=np.float32)
    lora_at = np.ascontiguousarray(lA.transpose(2, 0, 1).reshape(C, 128))
    lB = np.asarray(I["L0_ffn_lora_B"], dtype=np.float32)
    lora_bc = np.ascontiguousarray(lB.transpose(0, 2, 1).reshape(128, C))
    return wt_aug, gate_aug, lora_at, lora_bc


def _head_cols_up(wn, wr, heads):
    cols = []
    for h in heads:
        cols.append(wn[:, h * DN:(h + 1) * DN])
    for pp in range(2):
        hA, hB = heads[2 * pp], heads[2 * pp + 1]
        evA = wr[:, hA * DR:(hA + 1) * DR][:, 0::2]
        evB = wr[:, hB * DR:(hB + 1) * DR][:, 0::2]
        odA = wr[:, hA * DR:(hA + 1) * DR][:, 1::2]
        odB = wr[:, hB * DR:(hB + 1) * DR][:, 1::2]
        cols.append(np.concatenate([evA, evB, odA, odB], axis=1))
    return np.ascontiguousarray(np.concatenate(cols, axis=1), dtype=np.float32)


def prep_in_maps(inputs):
    I = inputs
    x = np.asarray(I["L0_attn_x"], dtype=np.float32)
    wt_aug, gate_aug, lora_at, lora_bc = _prep_shared(I)
    cdup, sdup, msk, ident = _host_const_tables()
    wq_down = np.ascontiguousarray(np.asarray(I["L0_attn_wq_down"], dtype=np.float32))
    wkv_down = np.ascontiguousarray(np.asarray(I["L0_attn_wkv_down"], dtype=np.float32))
    wq_nope = np.asarray(I["L0_attn_wq_nope"], dtype=np.float32)
    wq_rope = np.asarray(I["L0_attn_wq_rope"], dtype=np.float32)
    wk_nope = np.asarray(I["L0_attn_wk_nope"], dtype=np.float32)
    wk_rope = np.asarray(I["L0_attn_wk_rope"], dtype=np.float32)
    wv_up = np.asarray(I["L0_attn_wv_up"], dtype=np.float32)
    wo = np.ascontiguousarray(np.asarray(I["L0_attn_wo"], dtype=np.float32))

    in_maps = []
    for c in range(NCORE):
        bidx, g = c // 4, c % 4
        heads = [4 * g + j for j in range(4)]
        wv_cols = np.ascontiguousarray(np.concatenate(
            [wv_up[:, h * D:(h + 1) * D] for h in heads], axis=1),
            dtype=np.float32)
        m = {
            "x_own": np.ascontiguousarray(x[bidx, g * TOK:(g + 1) * TOK, :]),
            "wq_down": wq_down,
            "wkv_down": wkv_down,
            "wq_up": _head_cols_up(wq_nope, wq_rope, heads),
            "wk_up": _head_cols_up(wk_nope, wk_rope, heads),
            "wv_up": wv_cols,
            "wo": wo,
            "wt_aug": wt_aug,
            "gate_aug": gate_aug,
            "lora_at": lora_at,
            "lora_bc": lora_bc,
            "cdup": cdup,
            "sdup": sdup,
            "maskc": msk,
            "ident": ident,
            "gidx": np.asarray([[r * T + g * TOK] for r in range(4)],
                               dtype=np.int32),
        }
        in_maps.append(m)
    return in_maps


def kernel(**inputs):
    if "nc" not in _CACHE:
        _CACHE["nc"] = build_program()
    nc = _CACHE["nc"]
    in_maps = prep_in_maps(inputs)
    res = run_bass_kernel_spmd(nc, in_maps, core_ids=list(range(NCORE)))
    outp = np.empty((B, T, C), dtype=np.float32)
    for c in range(NCORE):
        bidx, g = c // 4, c % 4
        outp[bidx, g * TOK:(g + 1) * TOK, :] = res.results[c]["out"]
    return outp


# version 14
# speedup vs baseline: 1.0850x; 1.0850x over previous
import math
import sys
import contextlib

for _p in ("/opt/trn_rl_repo",):
    if _p not in sys.path:
        sys.path.append(_p)

import numpy as np
import ml_dtypes
import concourse.bass as bass
import concourse.mybir as mybir
from concourse.bass_utils import run_bass_kernel_spmd

F32 = mybir.dt.float32
BF16 = mybir.dt.bfloat16
I32 = mybir.dt.int32
AF = mybir.ActivationFunctionType
ALU = mybir.AluOpType
AX = mybir.AxisListType.X

NCORE = 8
B, T, C = 2, 4096, 2048
H, DN, DR, D = 16, 128, 64, 192
QR, KVR = 1536, 512
TOK = 1024
NTC = TOK // 128
ISD = 1.0 / math.sqrt(D)
SHIFT = 12.0
EPS = 1e-5
GROUPS = [[0, 1, 2, 3], [4, 5, 6, 7]]
ENGS = ("sync", "scalar", "vector", "tensor", "gpsimd")
ENG_SEM = {"sync": "ld", "gpsimd": "ldg", "tensor": "tt", "vector": "vv",
           "scalar": "ss"}
ENG_INC = {"sync": 16, "gpsimd": 16, "tensor": 1, "vector": 1, "scalar": 1}


class Slot:
    __slots__ = ("w", "rs")
    def __init__(self):
        self.w = None
        self.rs = []


class Prog:

    def __init__(self):
        self.streams = {e: [] for e in ENGS}
        self.cnt = {"ld": 0, "st": 0, "st2": 0, "ldg": 0, "cc": 0,
                    "tt": 0, "vv": 0, "ss": 0,
                    "lb1": 0, "lw1": 0, "lb2": 0, "lw2": 0, "lw3": 0,
                    "lag": 0, "lag2": 0, "xa": 0, "xb": 0, "qa": 0, "qb": 0,
                    "wa": 0, "wb": 0, "wc": 0, "wd": 0, "sa": 0, "sb": 0,
                    "ka": 0, "kb": 0, "xc": 0, "xd": 0, "qc": 0, "qd": 0,
                    "la0": 0, "la1": 0, "sta": 0}
        self.waited = {e: {} for e in ENGS}
        self.sems = {}

    def op(self, eng, build, reads=(), writes=(), sem=None, inc=None,
           defer=False, raw=False):
        deps = {}
        for s in reads:
            if s.w:
                if deps.get(s.w[0], -1) < s.w[1]:
                    deps[s.w[0]] = s.w[1]
        for s in writes:
            toks = ([s.w] if s.w else []) + s.rs
            for (sm, v) in toks:
                if deps.get(sm, -1) < v:
                    deps[sm] = v
        if sem is None:
            sem = ENG_SEM[eng]
        if inc is None:
            inc = ENG_INC[eng]
        w = self.waited[eng]
        waits = []
        for sname, v in deps.items():
            if v > self.cnt[sname]:
                assert sname == sem and v <= self.cnt[sname] + inc, (
                    eng, sem, sname, v, self.cnt[sname], inc)
                continue
            if w.get(sname, 0) < v:
                waits.append((sname, v))
                w[sname] = v
        if defer:
            tok = (sem, self.cnt[sem] + inc)
            do_inc = False
        else:
            self.cnt[sem] += inc
            tok = (sem, self.cnt[sem])
            do_inc = True

        sems = self.sems

        def thunk(E, waits=waits, build=build, tok=tok, inc=inc,
                  do_inc=do_inc, raw=raw, sem=sem):
            for sname, v in waits:
                E.wait_ge(sems[sname], v)
            ins = build(E)
            if do_inc and not raw:
                ins.then_inc(sems[sem], inc)

        self.streams[eng].append(thunk)
        for s in reads:
            s.rs.append(tok)
        for s in writes:
            s.w = tok
            s.rs = []
        return tok

    def bulk_end(self, slots, sem):
        tok = (sem, self.cnt[sem])
        for s in slots:
            s.w = tok
            s.rs = []

    def wait_raw(self, eng, sem, v):
        w = self.waited[eng]
        if w.get(sem, 0) >= v:
            return
        w[eng if False else sem] = v
        sems = self.sems
        self.streams[eng].append(lambda E, s=sem, v=v: E.wait_ge(sems[s], v))

    def barrier(self, exclude=()):
        snap = dict(self.cnt)
        for e in ENGS:
            for s, v in snap.items():
                if v > 0 and s not in exclude:
                    self.wait_raw(e, s, v)

    def flush(self, block):
        for e in ENGS:
            fns = self.streams[e]
            if not fns:
                continue
            getattr(block, e)(lambda E, fns=tuple(fns): [f(E) for f in fns])
            self.streams[e] = []


def build_program(debug=False):
    nc = bass.Bass("TRN2", target_bir_lowering=False)
    p = Prog()

    def dram(name, shape, dtype, kind):
        return nc.dram_tensor(name, shape, dtype, kind=kind).ap()

    x_own = dram("x_own", [TOK, C], F32, "ExternalInput")
    wq_down = dram("wq_down", [C, QR], F32, "ExternalInput")
    wkv_down = dram("wkv_down", [C, KVR], F32, "ExternalInput")
    wq_up = dram("wq_up", [QR, 768], F32, "ExternalInput")
    wk_up = dram("wk_up", [KVR, 768], F32, "ExternalInput")
    wv_up = dram("wv_up", [KVR, 768], F32, "ExternalInput")
    wo_in = dram("wo", [H * D, C], F32, "ExternalInput")
    wt_aug = dram("wt_aug", [C + 1, C], F32, "ExternalInput")
    gate_aug = dram("gate_aug", [C + 1, 8], F32, "ExternalInput")
    lora_at = dram("lora_at", [C, 128], F32, "ExternalInput")
    lora_bc = dram("lora_bc", [128, C], F32, "ExternalInput")
    cdup_in = dram("cdup", [64, T], BF16, "ExternalInput")
    sdup_in = dram("sdup", [64, T], BF16, "ExternalInput")
    maskc = dram("maskc", [128, 4 * 512], BF16, "ExternalInput")
    ident_in = dram("ident", [128, 128], BF16, "ExternalInput")
    gidx = dram("gidx", [4, 1], I32, "ExternalInput")
    out = dram("out", [TOK, C], F32, "ExternalOutput")

    ikind = "ExternalOutput" if debug else "Internal"
    lat_in = dram("lat_in", [C, TOK], BF16, ikind)
    lat_out_kv = dram("lat_out_kv", [4 * KVR, TOK], BF16, ikind)
    lat_out_q = [dram(f"lat_out_q{h}", [4 * (QR // 2), TOK], BF16, ikind)
                 for h in range(2)]
    at_in = [dram(f"at_in{pp}", [T, 384], BF16, ikind) for pp in range(2)]
    at_out = [dram(f"at_out{pp}", [4 * T, 384], BF16, ikind) for pp in range(2)]

    ctx = contextlib.ExitStack()
    with ctx:
        for sname in p.cnt:
            p.sems[sname] = ctx.enter_context(nc.semaphore(sname))
        block = ctx.enter_context(nc.Block())

        csb = ctx.enter_context(nc.sbuf_tensor("csb", [128, 4], F32))
        ident = ctx.enter_context(nc.sbuf_tensor("identsb", [128, 128], BF16))
        ones_row = ctx.enter_context(nc.sbuf_tensor("ones_row", [1, 128], BF16))
        S_csb, S_ident, S_ones = Slot(), Slot(), Slot()
        p.op("vector", lambda E: E.memset(csb[:, 0:1], ISD), writes=[S_csb],
             defer=True)
        p.op("vector", lambda E: E.memset(csb[:, 1:2], -SHIFT), writes=[S_csb])
        p.op("vector", lambda E: E.memset(ones_row[:, :], 1.0), writes=[S_ones])
        p.op("sync", lambda E: E.dma_start(out=ident[:, :], in_=ident_in[:, :]),
             writes=[S_ident], sem="lb1")
        p.bulk_end([S_ident], "lb1")

        def ln_chunk(xb, S_xb, st, S_st, xnb, S_xnb, xsq, S_xsq):
            p.op("vector", lambda E: E.scalar_tensor_tensor(
                xsq[:, :], xb[:, :], 1.0, xb[:, :], ALU.mult, ALU.mult,
                accum_out=st[:, 0:1]),
                reads=[S_xb], writes=[S_xsq, S_st[0]])
            p.op("vector", lambda E: E.reduce_sum(st[:, 1:2], xb[:, :], axis=AX),
                 reads=[S_xb], writes=[S_st[1]])
            p.op("vector", lambda E: E.tensor_scalar(
                st[:, 2:3], st[:, 1:2], 1.0 / C, None, ALU.mult),
                reads=[S_st[1]], writes=[S_st[2]])
            p.op("vector", lambda E: E.tensor_scalar(
                st[:, 3:4], st[:, 0:1], 1.0 / C, EPS, ALU.mult, ALU.add),
                reads=[S_st[0]], writes=[S_st[3]])
            p.op("vector", lambda E: E.scalar_tensor_tensor(
                st[:, 4:5], st[:, 2:3], -1.0, st[:, 2:3], ALU.mult, ALU.mult),
                reads=[S_st[2]], writes=[S_st[4]])
            p.op("vector", lambda E: E.tensor_add(
                st[:, 5:6], st[:, 3:4], st[:, 4:5]),
                reads=[S_st[3], S_st[4]], writes=[S_st[5]])
            p.op("scalar", lambda E: E.activation(st[:, 6:7], st[:, 5:6], AF.Sqrt),
                 reads=[S_st[5]], writes=[S_st[6]])
            p.op("vector", lambda E: E.reciprocal(st[:, 7:8], st[:, 6:7]),
                 reads=[S_st[6]], writes=[S_st[7]])
            p.op("vector", lambda E: E.tensor_mul(st[:, 4:5], st[:, 7:8], st[:, 7:8]),
                 reads=[S_st[7]], writes=[S_st[4]])
            p.op("vector", lambda E: E.tensor_mul(st[:, 4:5], st[:, 4:5], st[:, 5:6]),
                 reads=[S_st[4], S_st[5]], writes=[S_st[4]])
            p.op("vector", lambda E: E.tensor_scalar(
                st[:, 4:5], st[:, 4:5], -0.5, 1.5, ALU.mult, ALU.add),
                reads=[S_st[4]], writes=[S_st[4]])
            p.op("vector", lambda E: E.tensor_mul(st[:, 6:7], st[:, 7:8], st[:, 4:5]),
                 reads=[S_st[7], S_st[4]], writes=[S_st[6]])
            p.op("vector", lambda E: E.scalar_tensor_tensor(
                st[:, 3:4], st[:, 2:3], -1.0, st[:, 6:7], ALU.mult, ALU.mult),
                reads=[S_st[2], S_st[6]], writes=[S_st[3]])
            p.op("scalar", lambda E: E.activation(
                xnb[:, :], xb[:, :], AF.Identity, bias=st[:, 3:4],
                scale=st[:, 6:7]),
                reads=[S_xb, S_st[3], S_st[6]], writes=[S_xnb])

        def transpose128(src_ap, S_srcs, dst_ap, S_dst, ptb, S_ptb):
            p.op("tensor", lambda E: E.transpose(ptb[:, 0:128], src_ap,
                                                 ident[:, :]),
                 reads=list(S_srcs) + [S_ident], writes=[S_ptb])
            p.op("vector", lambda E: E.tensor_copy(dst_ap, ptb[:, 0:128]),
                 reads=[S_ptb], writes=[S_dst])

        S_lat_kv = Slot()
        S_lat_q = [Slot(), Slot()]
        with contextlib.ExitStack() as ph1:
            xb = [ph1.enter_context(nc.sbuf_tensor(f"xb{i}", [128, C], F32))
                  for i in range(4)]
            xsq = [ph1.enter_context(nc.sbuf_tensor(f"xsq{i}", [128, C], BF16))
                   for i in range(2)]
            stt = [ph1.enter_context(nc.sbuf_tensor(f"stt{i}", [128, 8], F32))
                   for i in range(2)]
            xnb = [ph1.enter_context(nc.sbuf_tensor(f"xnb{i}", [128, C], BF16))
                   for i in range(2)]
            xn_fm = ph1.enter_context(nc.sbuf_tensor("xn_fm", [128, 16, TOK], BF16))
            wqd = ph1.enter_context(nc.sbuf_tensor("wqd", [128, 16, QR], BF16))
            wkvd = ph1.enter_context(nc.sbuf_tensor("wkvd", [128, 16, KVR], BF16))
            lat_sb = [ph1.enter_context(nc.sbuf_tensor(f"lat_sb{i}", [128, 512], BF16))
                      for i in range(2)]
            ptb1 = [ph1.enter_context(nc.psum_tensor(f"ptb1_{i}", [128, 1024], BF16))
                    for i in range(2)]
            pl1 = [ph1.enter_context(nc.psum_tensor(f"pl1_{i}", [128, 512], F32))
                   for i in range(2)]

            S_xb = [Slot() for _ in range(4)]
            S_xsq = [Slot(), Slot()]
            S_stt = [[Slot() for _ in range(8)] for _ in range(2)]
            S_xnb = [Slot(), Slot()]
            S_xn = [[Slot() for _ in range(NTC)] for _ in range(16)]
            S_wqd = [Slot() for _ in range(16)]
            S_wkvd = [Slot() for _ in range(16)]
            S_latsb = [Slot(), Slot()]
            S_ptb1 = [Slot(), Slot()]
            S_pl1 = [Slot(), Slot()]

            for k in range(16):
                p.op("gpsimd", lambda E, k=k: E.dma_start(
                    out=wqd[:, k, :], in_=wq_down[k * 128:(k + 1) * 128, :]),
                    writes=[S_wqd[k]], sem="lw1")
                p.op("gpsimd", lambda E, k=k: E.dma_start(
                    out=wkvd[:, k, :], in_=wkv_down[k * 128:(k + 1) * 128, :]),
                    writes=[S_wkvd[k]], sem="lw1")
            p.bulk_end(S_wqd + S_wkvd, "lw1")

            def emit_lat(mi_list, nt):
                tiles = []
                for mi in mi_list:
                    bank = mi % 2
                    for k in range(16):
                        if mi < 12:
                            lhs = wqd[:, k, mi * 128:(mi + 1) * 128]
                            rsl = S_wqd[k]
                        else:
                            lhs = wkvd[:, k, (mi - 12) * 128:(mi - 11) * 128]
                            rsl = S_wkvd[k]
                        p.op("tensor",
                             lambda E, lhs=lhs, k=k, nt=nt, bank=bank:
                             E.matmul(pl1[bank][:, :], lhs,
                                      xn_fm[:, k, nt * 512:(nt + 1) * 512],
                                      start=(k == 0), stop=(k == 15)),
                             reads=[rsl] + [S_xn[k][nt * 4 + j]
                                            for j in range(4)],
                             writes=[S_pl1[bank]], defer=(k != 15))
                    sbuf = lat_sb[bank]
                    p.op("vector", lambda E, sbuf=sbuf, bank=bank:
                         E.tensor_copy(sbuf[:, :], pl1[bank][:, :]),
                         reads=[S_pl1[bank]], writes=[S_latsb[bank]])
                    S_tile = Slot()
                    tiles.append(S_tile)
                    p.op("sync", lambda E, sbuf=sbuf, mi=mi, nt=nt:
                         E.dma_start(
                             out=lat_in[mi * 128:(mi + 1) * 128,
                                        nt * 512:(nt + 1) * 512],
                             in_=sbuf[:, :]),
                         reads=[S_latsb[bank]], writes=[S_tile],
                         sem=("la0", "la1")[bank])
                return tiles

            def emit_ln(tcs):
                for tc in tcs:
                    i = tc % 4
                    j = tc % 2
                    p.op("sync", lambda E, tc=tc, i=i: E.dma_start(
                        out=xb[i][:, :], in_=x_own[tc * 128:(tc + 1) * 128, :]),
                        writes=[S_xb[i]], sem=("xa", "xb", "xc", "xd")[i])
                    ln_chunk(xb[i], S_xb[i], stt[j], S_stt[j], xnb[j], S_xnb[j],
                             xsq[j], S_xsq[j])
                    for k in range(16):
                        transpose128(xnb[j][:, k * 128:(k + 1) * 128], [S_xnb[j]],
                                     xn_fm[:, k, tc * 128:(tc + 1) * 128],
                                     S_xn[k][tc], ptb1[k % 2], S_ptb1[k % 2])

            emit_ln(range(0, 4))
            kv_tiles = emit_lat([12, 13, 14, 15], 0)
            emit_ln(range(4, 8))
            kv_tiles += emit_lat([12, 13, 14, 15], 1)
            p.op("gpsimd", lambda E: E.collective_compute(
                "AllGather", ALU.bypass, replica_groups=GROUPS,
                ins=[lat_in[QR:C, :]], outs=[lat_out_kv[:, :]]),
                reads=kv_tiles, writes=[S_lat_kv], sem="cc", inc=1)
            q_tiles = emit_lat(list(range(6)), 0) + emit_lat(list(range(6)), 1)
            p.op("gpsimd", lambda E: E.collective_compute(
                "AllGather", ALU.bypass, replica_groups=GROUPS,
                ins=[lat_in[0:QR // 2, :]], outs=[lat_out_q[0][:, :]]),
                reads=q_tiles, writes=[S_lat_q[0]], sem="cc", inc=1)
            q_tiles = emit_lat(list(range(6, 12)), 0) + emit_lat(list(range(6, 12)), 1)
            p.op("gpsimd", lambda E: E.collective_compute(
                "AllGather", ALU.bypass, replica_groups=GROUPS,
                ins=[lat_in[QR // 2:QR, :]], outs=[lat_out_q[1][:, :]]),
                reads=q_tiles, writes=[S_lat_q[1]], sem="cc", inc=1)

            p.barrier(exclude=("cc",))
            p.flush(block)

        S_at_in = [Slot(), Slot()]
        S_at_out = [Slot(), Slot()]
        with contextlib.ExitStack() as ph2:
            kvt = [ph2.enter_context(nc.sbuf_tensor(f"kvt{i}", [128, 4, 512], BF16))
                   for i in range(2)]
            qlb = [ph2.enter_context(nc.sbuf_tensor(f"qlb{i}", [128, 12, 512], BF16))
                   for i in range(2)]
            qc2 = ph2.enter_context(nc.sbuf_tensor("qc2", [128, 2, T], BF16))
            qr2 = ph2.enter_context(nc.sbuf_tensor("qr2", [128, T], BF16))
            kc2 = ph2.enter_context(nc.sbuf_tensor("kc2", [128, 2, T], BF16))
            kr2 = ph2.enter_context(nc.sbuf_tensor("kr2", [128, 2, T], BF16))
            v2 = ph2.enter_context(nc.sbuf_tensor("v2", [128, 32, 386], BF16))
            ath = ph2.enter_context(nc.sbuf_tensor("ath", [128, 32, 384], BF16))
            cds = ph2.enter_context(nc.sbuf_tensor("cds", [64, T], BF16))
            sns = ph2.enter_context(nc.sbuf_tensor("sns", [64, T], BF16))
            msk = ph2.enter_context(nc.sbuf_tensor("msk", [128, 4 * 512], BF16))
            wqu = ph2.enter_context(nc.sbuf_tensor("wqu", [128, 12, 768], BF16))
            wku = ph2.enter_context(nc.sbuf_tensor("wku", [128, 4, 768], BF16))
            wvu = ph2.enter_context(nc.sbuf_tensor("wvu", [128, 4, 768], BF16))
            ptile = [ph2.enter_context(nc.sbuf_tensor(f"ptile{i}", [128, 512], BF16))
                     for i in range(3)]
            rA = ph2.enter_context(nc.sbuf_tensor("rA", [128, 512], F32))
            rB = ph2.enter_context(nc.sbuf_tensor("rB", [128, 512], F32))
            rec = ph2.enter_context(nc.sbuf_tensor("rec", [128, 4], F32))
            psT = [ph2.enter_context(nc.psum_tensor(f"ps2_{i}", [128, 512], F32))
                   for i in range(8)]
            ps = [t[:, :] for t in psT]

            S_kvt = [Slot(), Slot()]
            S_qlb = [[Slot(), Slot()] for _ in range(2)]
            S_qc = [[Slot() for _ in range(8)] for _ in range(2)]
            S_qr = [Slot() for _ in range(8)]
            S_kc = [[Slot() for _ in range(8)] for _ in range(2)]
            S_kr = [Slot() for _ in range(8)]
            S_v = [[Slot() for _ in range(2)] for _ in range(32)]
            S_vones = Slot()
            S_ath = [[Slot() for _ in range(2)] for _ in range(32)]
            S_cds, S_sns, S_msk = Slot(), Slot(), Slot()
            S_wqu = [Slot() for _ in range(12)]
            S_wku = [Slot() for _ in range(4)]
            S_wvu = [Slot() for _ in range(4)]
            S_pt = [Slot() for _ in range(3)]
            S_rA, S_rB, S_rec = Slot(), Slot(), Slot()
            S_ps = [Slot() for _ in range(8)]

            p.op("sync", lambda E: E.dma_start(out=cds[:, :], in_=cdup_in[:, :]),
                 writes=[S_cds], sem="lb2")
            p.op("sync", lambda E: E.dma_start(out=sns[:, :], in_=sdup_in[:, :]),
                 writes=[S_sns], sem="lb2")
            p.op("sync", lambda E: E.dma_start(out=msk[:, :], in_=maskc[:, :]),
                 writes=[S_msk], sem="lb2")
            for k in range(12):
                p.op("gpsimd", lambda E, k=k: E.dma_start(
                    out=wqu[:, k, :], in_=wq_up[k * 128:(k + 1) * 128, :]),
                    writes=[S_wqu[k]], sem="lw2")
            for k in range(4):
                p.op("gpsimd", lambda E, k=k: E.dma_start(
                    out=wku[:, k, :], in_=wk_up[k * 128:(k + 1) * 128, :]),
                    writes=[S_wku[k]], sem="lw2")
                p.op("gpsimd", lambda E, k=k: E.dma_start(
                    out=wvu[:, k, :], in_=wv_up[k * 128:(k + 1) * 128, :]),
                    writes=[S_wvu[k]], sem="lw2")
            p.bulk_end(S_wqu + S_wku + S_wvu, "lw2")
            p.bulk_end([S_cds, S_sns, S_msk], "lb2")
            p.op("vector", lambda E: E.memset(v2[:, :, 192:193], 1.0),
                 writes=[S_vones], defer=True)
            p.op("vector", lambda E: E.memset(v2[:, :, 385:386], 1.0),
                 writes=[S_vones])
            S_krz = Slot()
            p.op("vector", lambda E: E.memset(kr2[64:128, 0, :], 0.0),
                 writes=[S_krz], defer=True)
            p.op("vector", lambda E: E.memset(kr2[0:64, 1, :], 0.0),
                 writes=[S_krz])

            def rope_store(pbank, S_pbank, t0, dstq, S_dst, dsts=None):
                n = 512
                p.op("vector", lambda E: E.tensor_mul(
                    rA[0:64, :], pbank[0:64, :], cds[:, t0:t0 + n]),
                    reads=[S_pbank, S_cds], writes=[S_rA], defer=True)
                p.op("vector", lambda E: E.tensor_mul(
                    rA[64:128, :], pbank[64:128, :], cds[:, t0:t0 + n]),
                    reads=[S_pbank, S_cds], writes=[S_rA])
                p.op("vector", lambda E: E.tensor_mul(
                    rB[0:64, :], pbank[64:128, :], sns[:, t0:t0 + n]),
                    reads=[S_pbank, S_sns], writes=[S_rB], defer=True)
                p.op("vector", lambda E: E.tensor_mul(
                    rB[64:128, :], pbank[0:64, :], sns[:, t0:t0 + n]),
                    reads=[S_pbank, S_sns], writes=[S_rB])
                for hh in range(2):
                    d1 = (dstq[hh * 64:hh * 64 + 32, :] if dsts is None
                          else dsts[hh][0])
                    d2 = (dstq[hh * 64 + 32:hh * 64 + 64, :] if dsts is None
                          else dsts[hh][1])
                    p.op("vector", lambda E, hh=hh, d1=d1: E.tensor_sub(
                        d1, rA[hh * 32:hh * 32 + 32, :],
                        rB[hh * 32:hh * 32 + 32, :]),
                        reads=[S_rA, S_rB], writes=[S_dst], defer=True)
                    p.op("vector", lambda E, hh=hh, d2=d2: E.tensor_add(
                        d2, rB[64 + hh * 32:96 + hh * 32, :],
                        rA[64 + hh * 32:96 + hh * 32, :]),
                        reads=[S_rA, S_rB], writes=[S_dst],
                        defer=(hh == 0))

            for pp in range(2):
                for r in range(4):
                    for nh in range(2):
                        i = (r * 2 + nh) % 2
                        for h in range(2):
                            p.op("sync", lambda E, r=r, nh=nh, i=i, h=h:
                                 E.dma_start(
                                     out=qlb[i][:, 6 * h:6 * h + 6, :],
                                     in_=lat_out_q[h][
                                         r * (QR // 2):(r + 1) * (QR // 2),
                                         nh * 512:(nh + 1) * 512].rearrange(
                                         "(kq pq) n -> pq kq n", pq=128)),
                                 reads=[S_lat_q[h]], writes=[S_qlb[i][h]],
                                 sem=("qa", "qb", "qc", "qd")[i * 2 + h])
                        t0 = r * TOK + nh * 512
                        for mj, mi in enumerate((2 * pp, 2 * pp + 1, 4 + pp)):
                            for k in range(12):
                                p.op("tensor",
                                     lambda E, mi=mi, k=k, i=i, mj=mj:
                                     E.matmul(ps[mj],
                                              wqu[:, k, mi * 128:(mi + 1) * 128],
                                              qlb[i][:, k, :],
                                              start=(k == 0), stop=(k == 11)),
                                     reads=[S_wqu[k], S_qlb[i][k // 6]],
                                     writes=[S_ps[mj]], defer=(k != 11))
                        for mj in range(2):
                            p.op("vector",
                                 lambda E, mj=mj, t0=t0: E.tensor_copy(
                                     qc2[:, mj, t0:t0 + 512], ps[mj]),
                                 reads=[S_ps[mj]], writes=[S_qc[mj][t0 // 512]])
                        rope_store(ps[2], S_ps[2], t0,
                                   qr2[:, t0:t0 + 512], S_qr[t0 // 512])

                for nt in range(8):
                    rr = nt // 2
                    ch = nt % 2
                    i = nt % 2
                    p.op("sync", lambda E, rr=rr, ch=ch, i=i: E.dma_start(
                        out=kvt[i][:, :, :],
                        in_=lat_out_kv[rr * KVR:(rr + 1) * KVR,
                                       ch * 512:(ch + 1) * 512].rearrange(
                            "(kk pk) n -> pk kk n", pk=128)),
                        reads=[S_lat_kv], writes=[S_kvt[i]],
                        sem=("ka", "kb")[i])
                    for mj, mi in enumerate((2 * pp, 2 * pp + 1, 4 + pp)):
                        for kk in range(4):
                            p.op("tensor",
                                 lambda E, mi=mi, kk=kk, i=i, mj=mj: E.matmul(
                                     ps[mj],
                                     wku[:, kk, mi * 128:(mi + 1) * 128],
                                     kvt[i][:, kk, :],
                                     start=(kk == 0), stop=(kk == 3)),
                                 reads=[S_wku[kk], S_kvt[i]],
                                 writes=[S_ps[mj]], defer=(kk != 3))
                    for mj in range(2):
                        p.op("vector", lambda E, mj=mj, nt=nt: E.tensor_copy(
                            kc2[:, mj, nt * 512:(nt + 1) * 512], ps[mj]),
                            reads=[S_ps[mj]], writes=[S_kc[mj][nt]])
                    rope_store(
                        ps[2], S_ps[2], nt * 512, None, S_kr[nt],
                        dsts=[(kr2[0:32, 0, nt * 512:(nt + 1) * 512],
                               kr2[32:64, 0, nt * 512:(nt + 1) * 512]),
                              (kr2[64:96, 1, nt * 512:(nt + 1) * 512],
                               kr2[96:128, 1, nt * 512:(nt + 1) * 512])])
                    for tcl in range(4):
                        tc = nt * 4 + tcl
                        bank = (3, 4, 5, 7)[tcl]
                        for kk in range(4):
                            p.op("tensor",
                                 lambda E, kk=kk, tcl=tcl, pp=pp, i=i, bank=bank:
                                 E.matmul(
                                     ps[bank][:, 0:384],
                                     kvt[i][:, kk, tcl * 128:(tcl + 1) * 128],
                                     wvu[:, kk, 2 * pp * 192:2 * pp * 192 + 384],
                                     start=(kk == 0), stop=(kk == 3)),
                                 reads=[S_wvu[kk], S_kvt[i]],
                                 writes=[S_ps[bank]], defer=(kk != 3))
                        p.op("vector",
                             lambda E, tc=tc, bank=bank: E.tensor_copy(
                                 v2[:, tc, 0:386].rearrange(
                                     "p (h x) -> p h x", h=2)[:, :, 0:192],
                                 ps[bank][:, 0:384].rearrange(
                                     "p (h x) -> p h x", h=2)),
                             reads=[S_ps[bank]],
                             writes=[S_v[tc][0], S_v[tc][1]])

                for hl in range(2):
                    for qt in range(8):
                        nsc = (qt + 1) * 4
                        q0 = qt * 512

                        def emit_scores(sc, hl=hl, q0=q0):
                            sb = 4 + (sc % 3)
                            p.op("tensor",
                                 lambda E, hl=hl, sc=sc, q0=q0, sb=sb: E.matmul(
                                     ps[sb],
                                     kc2[:, hl, sc * 128:(sc + 1) * 128],
                                     qc2[:, hl, q0:q0 + 512],
                                     start=True, stop=False),
                                 reads=[S_kc[hl][sc // 4], S_qc[hl][q0 // 512]],
                                 writes=[S_ps[sb]], defer=True)
                            p.op("tensor",
                                 lambda E, hl=hl, sc=sc, q0=q0, sb=sb: E.matmul(
                                     ps[sb],
                                     kr2[:, hl, sc * 128:(sc + 1) * 128],
                                     qr2[:, q0:q0 + 512],
                                     start=False, stop=True),
                                 reads=[S_kr[sc // 4], S_qr[q0 // 512], S_krz],
                                 writes=[S_ps[sb]])

                        def emit_exp(sc, qt=qt):
                            sb = 4 + (sc % 3)
                            pt = ptile[sc % 3]
                            ipt = sc % 3
                            p.op("scalar", lambda E, sb=sb, pt=pt: E.activation(
                                pt[:, :], ps[sb], AF.Exp,
                                bias=csb[:, 1:2], scale=csb[:, 0:1]),
                                reads=[S_ps[sb], S_csb], writes=[S_pt[ipt]])
                            if sc >= qt * 4:
                                off = (sc - qt * 4) * 512
                                p.op("vector", lambda E, pt=pt, off=off:
                                     E.tensor_mul(pt[:, :], pt[:, :],
                                                  msk[:, off:off + 512]),
                                     reads=[S_pt[ipt], S_msk],
                                     writes=[S_pt[ipt]])

                        def emit_pv(sc, hl=hl, nsc=nsc):
                            pt = ptile[sc % 3]
                            ipt = sc % 3
                            for qc4 in range(4):
                                p.op("tensor",
                                     lambda E, pt=pt, qc4=qc4, sc=sc, hl=hl,
                                     nsc=nsc: E.matmul(
                                         ps[qc4][:, 0:193],
                                         pt[:, qc4 * 128:(qc4 + 1) * 128],
                                         v2[:, sc, hl * 193:(hl + 1) * 193],
                                         start=(sc == 0), stop=(sc == nsc - 1),
                                         skip_group_check=True),
                                     reads=[S_pt[ipt], S_v[sc][hl], S_vones],
                                     writes=[S_ps[qc4]], defer=(qc4 != 3))

                        emit_scores(0)
                        if nsc > 1:
                            emit_scores(1)
                        for sc in range(nsc):
                            emit_exp(sc)
                            if sc + 2 < nsc:
                                emit_scores(sc + 2)
                            emit_pv(sc)
                        for qc4 in range(4):
                            tc = qt * 4 + qc4
                            p.op("vector", lambda E, qc4=qc4: E.reciprocal(
                                rec[:, qc4:qc4 + 1], ps[qc4][:, 192:193]),
                                reads=[S_ps[qc4]], writes=[S_rec])
                            p.op("vector",
                                 lambda E, qc4=qc4, tc=tc, hl=hl: E.tensor_scalar(
                                     ath[:, tc, hl * 192:(hl + 1) * 192],
                                     ps[qc4][:, 0:192], rec[:, qc4:qc4 + 1],
                                     None, ALU.mult),
                                 reads=[S_ps[qc4], S_rec], writes=[S_ath[tc][hl]])

                ath_deps = [S_ath[tc][hl] for tc in range(32) for hl in range(2)]
                S_store = Slot()
                for q4 in range(4):
                    p.op("sync", lambda E, q4=q4, pp=pp: E.dma_start(
                        out=at_in[pp][q4 * 1024:(q4 + 1) * 1024, :].rearrange(
                            "(c p) f -> p c f", p=128),
                        in_=ath[:, q4 * 8:(q4 + 1) * 8, :]),
                        reads=ath_deps, writes=[S_store], sem="sta")
                p.op("gpsimd", lambda E, pp=pp: E.collective_compute(
                    "AllGather", ALU.bypass, replica_groups=GROUPS,
                    ins=[at_in[pp][:, :]], outs=[at_out[pp][:, :]]),
                    reads=[S_store, S_at_in[pp]], writes=[S_at_out[pp]],
                    sem="cc", inc=1)

            p.barrier(exclude=("cc",))
            p.flush(block)

        with contextlib.ExitStack() as ph3:
            xn3 = ph3.enter_context(nc.sbuf_tensor("xn3", [128, 16, TOK], BF16))
            afm = ph3.enter_context(nc.sbuf_tensor("afm", [128, 24, TOK], BF16))
            gsb = ph3.enter_context(nc.sbuf_tensor("gsb", [128, 17, 8], BF16))
            lgs = ph3.enter_context(nc.sbuf_tensor("lgs", [128, 8], F32))
            t2 = ph3.enter_context(nc.sbuf_tensor("t2", [128, 32], F32))
            wdn = ph3.enter_context(nc.sbuf_tensor("wdn", [128, 8, 8], F32))
            lat3 = ph3.enter_context(nc.sbuf_tensor("lat3", [128, 16, 128], BF16))
            lbc = ph3.enter_context(nc.sbuf_tensor("lbc", [128, C], BF16))
            xag = ph3.enter_context(nc.sbuf_tensor("xag", [128, 128], BF16))
            xafm = ph3.enter_context(nc.sbuf_tensor("xafm", [128, TOK], BF16))
            bias_sb = ph3.enter_context(nc.sbuf_tensor("bias_sb", [1, C], BF16))
            xsl = [ph3.enter_context(nc.sbuf_tensor(f"xsl{i}", [128, 512], F32))
                   for i in range(2)]
            wst = [ph3.enter_context(nc.sbuf_tensor(f"wst{i}", [128, 512], F32))
                   for i in range(4)]
            osb = [ph3.enter_context(nc.sbuf_tensor(f"osb{i}", [128, 512], F32))
                   for i in range(2)]
            ptb3 = [ph3.enter_context(nc.psum_tensor(f"ptb3_{i}", [128, 1024], BF16))
                    for i in range(2)]
            pf = [ph3.enter_context(nc.psum_tensor(f"pf{i}", [128, 512], F32))
                  for i in range(4)]
            ph3a = contextlib.ExitStack()
            aown = ph3a.enter_context(nc.sbuf_tensor("aown", [128, 8, 3072], BF16))
            xb3 = [ph3a.enter_context(nc.sbuf_tensor(f"xb3_{i}", [128, C], F32))
                   for i in range(2)]
            xsq3 = [ph3a.enter_context(nc.sbuf_tensor(f"xsq3_{i}", [128, C], BF16))
                    for i in range(2)]
            st3 = [ph3a.enter_context(nc.sbuf_tensor(f"st3_{i}", [128, 8], F32))
                   for i in range(2)]
            xnb3 = [ph3a.enter_context(nc.sbuf_tensor(f"xnb3_{i}", [128, C], BF16))
                    for i in range(2)]

            S_aownc = [Slot() for _ in range(8)]
            S_xb3 = [Slot(), Slot()]
            S_xsq3 = [Slot(), Slot()]
            S_st3 = [[Slot() for _ in range(8)] for _ in range(2)]
            S_xnb3 = [Slot(), Slot()]
            S_xn3 = [[Slot() for _ in range(NTC)] for _ in range(16)]
            S_afm = [[Slot() for _ in range(NTC)] for _ in range(24)]
            S_gsb = Slot()
            S_lgs = Slot()
            S_t2 = Slot()
            S_wdn = [Slot() for _ in range(8)]
            S_lat3 = [Slot() for _ in range(16)]
            S_lbc = Slot()
            S_xag = Slot()
            S_xafm = [Slot() for _ in range(8)]
            S_bias = Slot()
            S_wnt = [Slot(), Slot()]
            S_wst = [Slot() for _ in range(4)]
            S_xsl = [Slot(), Slot()]
            S_osb = [Slot(), Slot()]
            S_ptb3 = [Slot(), Slot()]
            S_pf = [Slot() for _ in range(4)]
            S_out = Slot()

            for k in range(16):
                p.op("gpsimd", lambda E, k=k: E.dma_start(
                    out=gsb[:, k, :], in_=gate_aug[k * 128:(k + 1) * 128, :]),
                    writes=[S_gsb], sem="lw3")
            p.op("gpsimd", lambda E: E.dma_start(
                out=gsb[0:1, 16, :], in_=gate_aug[C:C + 1, :]),
                writes=[S_gsb], sem="lw3")
            p.op("gpsimd", lambda E: E.dma_start(
                out=lat3[:, :, :],
                in_=lora_at[:, :].rearrange("(kq p) r -> p kq r", p=128)),
                writes=S_lat3, sem="lw3")
            p.op("gpsimd", lambda E: E.dma_start(out=lbc[:, :], in_=lora_bc[:, :]),
                 writes=[S_lbc], sem="lw3")
            p.op("gpsimd", lambda E: E.dma_start(
                out=bias_sb[:, :], in_=wt_aug[C:C + 1, :]),
                writes=[S_bias], sem="lw3")
            p.bulk_end([S_gsb, S_lbc, S_bias] + S_lat3, "lw3")

            cc_toks = [S_at_out[0].w, S_at_out[1].w]

            def aown_dmas(E):
                sems = p.sems
                last = None
                with E.register("goff") as rg:
                    offs = []
                    for r in range(4):
                        E.reg_load(rg, gidx[r:r + 1, 0:1])
                        offs.append(E.snap(rg, min_val=0,
                                           max_val=3 * T + 3 * TOK))
                    for ppx in range(2):
                        E.wait_ge(sems[cc_toks[ppx][0]], cc_toks[ppx][1])
                        for r in range(4):
                            col = r * 768 + ppx * 384
                            last = E.dma_start(
                                out=aown[:, :, col:col + 384],
                                in_=at_out[ppx][bass.ds(offs[r], TOK), :].rearrange(
                                    "(c p) f -> p c f", p=128))
                            last.then_inc(sems[("lag", "lag2")[ppx]], 16)
                return last

            p.op("gpsimd", aown_dmas, reads=[],
                 writes=S_aownc, sem="lag", inc=4 * 16, raw=True)
            p.cnt["lag2"] += 4 * 16
            for r in range(4):
                S_aownc[2 * r + 0].w = ("lag", p.cnt["lag"])
                S_aownc[2 * r + 1].w = ("lag2", p.cnt["lag2"])

            for tc in range(NTC):
                i = tc % 2
                p.op("sync", lambda E, tc=tc, i=i: E.dma_start(
                    out=xb3[i][:, :], in_=x_own[tc * 128:(tc + 1) * 128, :]),
                    writes=[S_xb3[i]], sem=("xa", "xb")[i])
                ln_chunk(xb3[i], S_xb3[i], st3[i], S_st3[i], xnb3[i],
                         S_xnb3[i], xsq3[i], S_xsq3[i])
                for k in range(16):
                    transpose128(xnb3[i][:, k * 128:(k + 1) * 128], [S_xnb3[i]],
                                 xn3[:, k, tc * 128:(tc + 1) * 128],
                                 S_xn3[k][tc], ptb3[k % 2], S_ptb3[k % 2])

            for qc in range(8):
                for k in range(16):
                    p.op("tensor", lambda E, k=k, qc=qc: E.matmul(
                        pf[0][:, 0:8], xn3[:, k, qc * 128:(qc + 1) * 128],
                        gsb[:, k, :], start=(k == 0), stop=False),
                        reads=[S_xn3[k][qc], S_gsb], writes=[S_pf[0]],
                        defer=True)
                p.op("tensor", lambda E: E.matmul(
                    pf[0][:, 0:8], ones_row[:, :], gsb[0:1, 16, :],
                    start=False, stop=True),
                    reads=[S_ones, S_gsb], writes=[S_pf[0]])
                p.op("vector", lambda E: E.tensor_copy(lgs[:, :], pf[0][:, 0:8]),
                     reads=[S_pf[0]], writes=[S_lgs])
                p.op("vector", lambda E: E.reduce_max(t2[:, 8:9], lgs[:, :], axis=AX),
                     reads=[S_lgs], writes=[S_t2])
                p.op("vector", lambda E, qc=qc: E.tensor_tensor(
                    wdn[:, qc, :], lgs[:, :],
                    t2[:, 8:9].to_broadcast((128, 8)), ALU.is_ge),
                    reads=[S_lgs, S_t2], writes=[S_wdn[qc]])
                p.op("vector", lambda E, qc=qc: E.scalar_tensor_tensor(
                    t2[:, 16:24], wdn[:, qc, :], -1e30, lgs[:, :],
                    ALU.mult, ALU.add),
                    reads=[S_lgs, S_wdn[qc]], writes=[S_t2])
                p.op("vector", lambda E: E.reduce_max(
                    t2[:, 9:10], t2[:, 16:24], axis=AX),
                    reads=[S_t2], writes=[S_t2])
                p.op("vector", lambda E: E.tensor_tensor(
                    t2[:, 24:32], t2[:, 16:24],
                    t2[:, 9:10].to_broadcast((128, 8)), ALU.is_ge),
                    reads=[S_t2], writes=[S_t2])
                p.op("vector", lambda E: E.tensor_sub(
                    t2[:, 10:11], t2[:, 8:9], t2[:, 9:10]),
                    reads=[S_t2], writes=[S_t2])
                p.op("scalar", lambda E: E.activation(
                    t2[:, 11:12], t2[:, 10:11], AF.Sigmoid),
                    reads=[S_t2], writes=[S_t2])
                p.op("scalar", lambda E: E.activation(
                    t2[:, 12:13], t2[:, 10:11], AF.Sigmoid, scale=-1.0),
                    reads=[S_t2], writes=[S_t2])
                p.op("vector", lambda E, qc=qc: E.tensor_scalar(
                    wdn[:, qc, :], wdn[:, qc, :], t2[:, 11:12], None, ALU.mult),
                    reads=[S_wdn[qc], S_t2], writes=[S_wdn[qc]])
                p.op("vector", lambda E, qc=qc: E.scalar_tensor_tensor(
                    wdn[:, qc, :], t2[:, 24:32], t2[:, 12:13], wdn[:, qc, :],
                    ALU.mult, ALU.add),
                    reads=[S_t2, S_wdn[qc]], writes=[S_wdn[qc]])
                for k in range(16):
                    p.op("tensor", lambda E, k=k, qc=qc: E.matmul(
                        pf[1][:, 0:128], xn3[:, k, qc * 128:(qc + 1) * 128],
                        lat3[:, k, :], start=(k == 0), stop=(k == 15)),
                        reads=[S_xn3[k][qc], S_lat3[k]], writes=[S_pf[1]],
                        defer=(k != 15))
                p.op("vector", lambda E, qc=qc: E.tensor_tensor(
                    xag[:, :].rearrange("p (e r) -> p e r", r=16),
                    pf[1][:, 0:128].rearrange("p (e r) -> p e r", r=16),
                    wdn[:, qc, :][:, :, None].to_broadcast((128, 8, 16)),
                    ALU.mult),
                    reads=[S_pf[1], S_wdn[qc]], writes=[S_xag])
                transpose128(xag[:, :], [S_xag], xafm[:, qc * 128:(qc + 1) * 128],
                             S_xafm[qc], ptb3[0], S_ptb3[0])


            for tc in range(NTC):
                for k in range(24):
                    transpose128(aown[:, tc, k * 128:(k + 1) * 128],
                                 [S_aownc[k // 3]],
                                 afm[:, k, tc * 128:(tc + 1) * 128],
                                 S_afm[k][tc], ptb3[k % 2], S_ptb3[k % 2])

            for k in range(4):
                srcap = wo_in[k * 128:(k + 1) * 128, 0:512]
                p.op("sync", lambda E, srcap=srcap, k=k: E.dma_start(
                    out=wst[k][:, :], in_=srcap),
                    writes=[S_wst[k]], sem=("wa", "wb", "wc", "wd")[k])
            p.barrier(exclude=("cc", "wa", "wb", "wc", "wd"))
            ph3a.close()
            wnt = [ph3.enter_context(nc.sbuf_tensor(f"wnt{i}", [128, 41, 512], BF16))
                   for i in range(2)]

            def emit_w(nt):
                wb = nt % 2
                wslot = S_wnt[wb]
                for k in range(41):
                    if k < 24:
                        srcap = wo_in[k * 128:(k + 1) * 128,
                                      nt * 512:(nt + 1) * 512]
                    elif k < 40:
                        srcap = wt_aug[(k - 24) * 128:(k - 23) * 128,
                                       nt * 512:(nt + 1) * 512]
                    else:
                        srcap = lora_bc[:, nt * 512:(nt + 1) * 512]
                    j = k % 4
                    if nt > 0 or k >= 4:
                        p.op("sync", lambda E, srcap=srcap, j=j: E.dma_start(
                            out=wst[j][:, :], in_=srcap),
                            writes=[S_wst[j]], sem=("wa", "wb", "wc", "wd")[j])
                    p.op("vector", lambda E, k=k, wb=wb, j=j: E.tensor_copy(
                        wnt[wb][:, k, :], wst[j][:, :]),
                        reads=[S_wst[j]], writes=[wslot])

            emit_w(0)
            for nt in range(4):
                wb = nt % 2
                wslot = S_wnt[wb]
                if nt + 1 < 4:
                    emit_w(nt + 1)
                for qc in range(8):
                    bank = 2 + qc % 2
                    for k in range(24):
                        p.op("tensor", lambda E, k=k, qc=qc, wb=wb, bank=bank:
                             E.matmul(pf[bank][:, :],
                                      afm[:, k, qc * 128:(qc + 1) * 128],
                                      wnt[wb][:, k, :],
                                      start=(k == 0), stop=False),
                             reads=[S_afm[k][qc], wslot], writes=[S_pf[bank]],
                             defer=True)
                    for k in range(16):
                        p.op("tensor", lambda E, k=k, qc=qc, wb=wb, bank=bank:
                             E.matmul(pf[bank][:, :],
                                      xn3[:, k, qc * 128:(qc + 1) * 128],
                                      wnt[wb][:, 24 + k, :],
                                      start=False, stop=False),
                             reads=[S_xn3[k][qc], wslot], writes=[S_pf[bank]],
                             defer=True)
                    p.op("tensor", lambda E, qc=qc, nt=nt, bank=bank: E.matmul(
                        pf[bank][:, :], ones_row[:, :],
                        bias_sb[:, nt * 512:(nt + 1) * 512],
                        start=False, stop=False),
                        reads=[S_ones, S_bias], writes=[S_pf[bank]], defer=True)
                    p.op("tensor", lambda E, qc=qc, wb=wb, bank=bank: E.matmul(
                        pf[bank][:, :], xafm[:, qc * 128:(qc + 1) * 128],
                        wnt[wb][:, 40, :], start=False, stop=True),
                        reads=[S_xafm[qc], wslot], writes=[S_pf[bank]])
                    i = qc % 2
                    p.op("sync", lambda E, qc=qc, nt=nt, i=i: E.dma_start(
                        out=xsl[i][:, :],
                        in_=x_own[qc * 128:(qc + 1) * 128,
                                  nt * 512:(nt + 1) * 512]),
                        writes=[S_xsl[i]], sem=("sa", "sb")[i])
                    p.op("vector", lambda E, i=i, bank=bank: E.tensor_add(
                        osb[i][:, :], pf[bank][:, :], xsl[i][:, :]),
                        reads=[S_pf[bank], S_xsl[i]], writes=[S_osb[i]])
                    p.op("sync", lambda E, qc=qc, nt=nt, i=i: E.dma_start(
                        out=out[qc * 128:(qc + 1) * 128,
                                nt * 512:(nt + 1) * 512],
                        in_=osb[i][:, :]),
                        reads=[S_osb[i]], writes=[S_out], sem=("st", "st2")[i])

            p.wait_raw("sync", "st", p.cnt["st"])
            p.wait_raw("sync", "st2", p.cnt["st2"])
            p.flush(block)

    return nc


_CACHE = {}


def _host_const_tables():
    inv = 1.0 / (10000.0 ** (np.arange(0, DR, 2, dtype=np.float64) / DR))
    t = np.arange(T, dtype=np.float64)
    fr = np.outer(inv, t)
    cos = np.cos(fr).astype(np.float32)
    sin = np.sin(fr).astype(np.float32)
    cdup = np.concatenate([cos, cos], axis=0).astype(ml_dtypes.bfloat16)
    sdup = np.concatenate([sin, sin], axis=0).astype(ml_dtypes.bfloat16)
    qf = np.arange(512)
    sl = np.arange(128)
    msk = np.concatenate(
        [(qf[None, :] >= (v * 128 + sl[:, None])) for v in range(4)],
        axis=1).astype(ml_dtypes.bfloat16)
    ident = np.eye(128, dtype=ml_dtypes.bfloat16)
    return cdup, sdup, msk, ident


def _prep_shared(I):
    W = np.asarray(I["L0_ffn_W"], dtype=np.float32)
    b = np.asarray(I["L0_ffn_b"], dtype=np.float32)
    wt_aug = np.ascontiguousarray(
        np.concatenate([W.T, b[None, :]], axis=0), dtype=np.float32)
    gw = np.asarray(I["L0_ffn_gate_w"], dtype=np.float32)
    gb = np.asarray(I["L0_ffn_gate_b"], dtype=np.float32)
    gate_aug = np.ascontiguousarray(
        np.concatenate([gw.T, gb[None, :]], axis=0), dtype=np.float32)
    lA = np.asarray(I["L0_ffn_lora_A"], dtype=np.float32)
    lora_at = np.ascontiguousarray(lA.transpose(2, 0, 1).reshape(C, 128))
    lB = np.asarray(I["L0_ffn_lora_B"], dtype=np.float32)
    lora_bc = np.ascontiguousarray(lB.transpose(0, 2, 1).reshape(128, C))
    return wt_aug, gate_aug, lora_at, lora_bc


def _head_cols_up(wn, wr, heads):
    cols = []
    for h in heads:
        cols.append(wn[:, h * DN:(h + 1) * DN])
    for pp in range(2):
        hA, hB = heads[2 * pp], heads[2 * pp + 1]
        evA = wr[:, hA * DR:(hA + 1) * DR][:, 0::2]
        evB = wr[:, hB * DR:(hB + 1) * DR][:, 0::2]
        odA = wr[:, hA * DR:(hA + 1) * DR][:, 1::2]
        odB = wr[:, hB * DR:(hB + 1) * DR][:, 1::2]
        cols.append(np.concatenate([evA, evB, odA, odB], axis=1))
    return np.ascontiguousarray(np.concatenate(cols, axis=1), dtype=np.float32)


def prep_in_maps(inputs):
    I = inputs
    x = np.asarray(I["L0_attn_x"], dtype=np.float32)
    wt_aug, gate_aug, lora_at, lora_bc = _prep_shared(I)
    cdup, sdup, msk, ident = _host_const_tables()
    wq_down = np.ascontiguousarray(np.asarray(I["L0_attn_wq_down"], dtype=np.float32))
    wkv_down = np.ascontiguousarray(np.asarray(I["L0_attn_wkv_down"], dtype=np.float32))
    wq_nope = np.asarray(I["L0_attn_wq_nope"], dtype=np.float32)
    wq_rope = np.asarray(I["L0_attn_wq_rope"], dtype=np.float32)
    wk_nope = np.asarray(I["L0_attn_wk_nope"], dtype=np.float32)
    wk_rope = np.asarray(I["L0_attn_wk_rope"], dtype=np.float32)
    wv_up = np.asarray(I["L0_attn_wv_up"], dtype=np.float32)
    wo = np.ascontiguousarray(np.asarray(I["L0_attn_wo"], dtype=np.float32))

    in_maps = []
    for c in range(NCORE):
        bidx, g = c // 4, c % 4
        heads = [4 * g + j for j in range(4)]
        wv_cols = np.ascontiguousarray(np.concatenate(
            [wv_up[:, h * D:(h + 1) * D] for h in heads], axis=1),
            dtype=np.float32)
        m = {
            "x_own": np.ascontiguousarray(x[bidx, g * TOK:(g + 1) * TOK, :]),
            "wq_down": wq_down,
            "wkv_down": wkv_down,
            "wq_up": _head_cols_up(wq_nope, wq_rope, heads),
            "wk_up": _head_cols_up(wk_nope, wk_rope, heads),
            "wv_up": wv_cols,
            "wo": wo,
            "wt_aug": wt_aug,
            "gate_aug": gate_aug,
            "lora_at": lora_at,
            "lora_bc": lora_bc,
            "cdup": cdup,
            "sdup": sdup,
            "maskc": msk,
            "ident": ident,
            "gidx": np.asarray([[r * T + g * TOK] for r in range(4)],
                               dtype=np.int32),
        }
        in_maps.append(m)
    return in_maps


def kernel(**inputs):
    if "nc" not in _CACHE:
        _CACHE["nc"] = build_program()
    nc = _CACHE["nc"]
    in_maps = prep_in_maps(inputs)
    res = run_bass_kernel_spmd(nc, in_maps, core_ids=list(range(NCORE)))
    outp = np.empty((B, T, C), dtype=np.float32)
    for c in range(NCORE):
        bidx, g = c // 4, c % 4
        outp[bidx, g * TOK:(g + 1) * TOK, :] = res.results[c]["out"]
    return outp


# version 16
# speedup vs baseline: 1.1628x; 1.0717x over previous
import math
import sys
import contextlib

for _p in ("/opt/trn_rl_repo",):
    if _p not in sys.path:
        sys.path.append(_p)

import numpy as np
import ml_dtypes
import concourse.bass as bass
import concourse.mybir as mybir
from concourse.bass_utils import run_bass_kernel_spmd

F32 = mybir.dt.float32
BF16 = mybir.dt.bfloat16
I32 = mybir.dt.int32
AF = mybir.ActivationFunctionType
ALU = mybir.AluOpType
AX = mybir.AxisListType.X

NCORE = 8
B, T, C = 2, 4096, 2048
H, DN, DR, D = 16, 128, 64, 192
QR, KVR = 1536, 512
TOK = 1024
NTC = TOK // 128
ISD = 1.0 / math.sqrt(D)
SHIFT = 12.0
EPS = 1e-5
GROUPS = [[0, 1, 2, 3], [4, 5, 6, 7]]
ENGS = ("sync", "scalar", "vector", "tensor", "gpsimd")
ENG_SEM = {"sync": "ld", "gpsimd": "ldg", "tensor": "tt", "vector": "vv",
           "scalar": "ss"}
ENG_INC = {"sync": 16, "gpsimd": 16, "tensor": 1, "vector": 1, "scalar": 1}


class Slot:
    __slots__ = ("w", "rs")
    def __init__(self):
        self.w = None
        self.rs = []


class Prog:

    def __init__(self):
        self.streams = {e: [] for e in ENGS}
        self.cnt = {"ld": 0, "st": 0, "st2": 0, "ldg": 0, "cc": 0,
                    "tt": 0, "vv": 0, "ss": 0,
                    "lb1": 0, "lw1": 0, "lb2": 0, "lw2": 0, "lw3": 0,
                    "lag": 0, "lag2": 0, "xa": 0, "xb": 0, "qa": 0, "qb": 0,
                    "wa": 0, "wb": 0, "wc": 0, "wd": 0, "sa": 0, "sb": 0,
                    "ka": 0, "kb": 0, "xc": 0, "xd": 0, "qc": 0, "qd": 0,
                    "la0": 0, "la1": 0, "sta": 0}
        self.waited = {e: {} for e in ENGS}
        self.sems = {}

    def op(self, eng, build, reads=(), writes=(), sem=None, inc=None,
           defer=False, raw=False):
        deps = {}
        for s in reads:
            if s.w:
                if deps.get(s.w[0], -1) < s.w[1]:
                    deps[s.w[0]] = s.w[1]
        for s in writes:
            toks = ([s.w] if s.w else []) + s.rs
            for (sm, v) in toks:
                if deps.get(sm, -1) < v:
                    deps[sm] = v
        if sem is None:
            sem = ENG_SEM[eng]
        if inc is None:
            inc = ENG_INC[eng]
        w = self.waited[eng]
        waits = []
        for sname, v in deps.items():
            if v > self.cnt[sname]:
                assert sname == sem and v <= self.cnt[sname] + inc, (
                    eng, sem, sname, v, self.cnt[sname], inc)
                continue
            if w.get(sname, 0) < v:
                waits.append((sname, v))
                w[sname] = v
        if defer:
            tok = (sem, self.cnt[sem] + inc)
            do_inc = False
        else:
            self.cnt[sem] += inc
            tok = (sem, self.cnt[sem])
            do_inc = True

        sems = self.sems

        def thunk(E, waits=waits, build=build, tok=tok, inc=inc,
                  do_inc=do_inc, raw=raw, sem=sem):
            for sname, v in waits:
                E.wait_ge(sems[sname], v)
            ins = build(E)
            if do_inc and not raw:
                ins.then_inc(sems[sem], inc)

        self.streams[eng].append(thunk)
        for s in reads:
            s.rs.append(tok)
        for s in writes:
            s.w = tok
            s.rs = []
        return tok

    def bulk_end(self, slots, sem):
        tok = (sem, self.cnt[sem])
        for s in slots:
            s.w = tok
            s.rs = []

    def wait_raw(self, eng, sem, v):
        w = self.waited[eng]
        if w.get(sem, 0) >= v:
            return
        w[eng if False else sem] = v
        sems = self.sems
        self.streams[eng].append(lambda E, s=sem, v=v: E.wait_ge(sems[s], v))

    def barrier(self, exclude=()):
        snap = dict(self.cnt)
        for e in ENGS:
            for s, v in snap.items():
                if v > 0 and s not in exclude:
                    self.wait_raw(e, s, v)

    def flush(self, block):
        for e in ENGS:
            fns = self.streams[e]
            if not fns:
                continue
            getattr(block, e)(lambda E, fns=tuple(fns): [f(E) for f in fns])
            self.streams[e] = []


def build_program(debug=False):
    nc = bass.Bass("TRN2", target_bir_lowering=False)
    p = Prog()

    def dram(name, shape, dtype, kind):
        return nc.dram_tensor(name, shape, dtype, kind=kind).ap()

    x_own = dram("x_own", [TOK, C], F32, "ExternalInput")
    wq_down = dram("wq_down", [C, QR], F32, "ExternalInput")
    wkv_down = dram("wkv_down", [C, KVR], F32, "ExternalInput")
    wq_up = dram("wq_up", [QR, 768], F32, "ExternalInput")
    wk_up = dram("wk_up", [KVR, 768], F32, "ExternalInput")
    wv_up = dram("wv_up", [KVR, 768], F32, "ExternalInput")
    wo_in = dram("wo", [H * D, C], F32, "ExternalInput")
    wt_aug = dram("wt_aug", [C + 1, C], F32, "ExternalInput")
    gate_aug = dram("gate_aug", [C + 1, 8], F32, "ExternalInput")
    lora_at = dram("lora_at", [C, 128], F32, "ExternalInput")
    lora_bc = dram("lora_bc", [128, C], F32, "ExternalInput")
    cdup_in = dram("cdup", [64, T], BF16, "ExternalInput")
    sdup_in = dram("sdup", [64, T], BF16, "ExternalInput")
    maskc = dram("maskc", [128, 4 * 512], BF16, "ExternalInput")
    ident_in = dram("ident", [128, 128], BF16, "ExternalInput")
    gidx = dram("gidx", [4, 1], I32, "ExternalInput")
    out = dram("out", [TOK, C], F32, "ExternalOutput")

    ikind = "ExternalOutput" if debug else "Internal"
    lat_in = dram("lat_in", [C, TOK], BF16, ikind)
    lat_out_kv = dram("lat_out_kv", [4 * KVR, TOK], BF16, ikind)
    lat_out_q = [dram(f"lat_out_q{h}", [4 * (QR // 2), TOK], BF16, ikind)
                 for h in range(2)]
    at_in = [dram(f"at_in{pp}", [T, 384], BF16, ikind) for pp in range(2)]
    at_out = [dram(f"at_out{pp}", [4 * T, 384], BF16, ikind) for pp in range(2)]

    ctx = contextlib.ExitStack()
    with ctx:
        for sname in p.cnt:
            p.sems[sname] = ctx.enter_context(nc.semaphore(sname))
        block = ctx.enter_context(nc.Block())

        csb = ctx.enter_context(nc.sbuf_tensor("csb", [128, 4], F32))
        gsb = ctx.enter_context(nc.sbuf_tensor("gsb", [128, 17, 8], BF16))
        lat3 = ctx.enter_context(nc.sbuf_tensor("lat3", [128, 16, 128], BF16))
        bias_sb = ctx.enter_context(nc.sbuf_tensor("bias_sb", [1, C], BF16))
        S_gsb = Slot()
        S_lat3 = [Slot() for _ in range(16)]
        S_bias = Slot()
        ident = ctx.enter_context(nc.sbuf_tensor("identsb", [128, 128], BF16))
        ones_row = ctx.enter_context(nc.sbuf_tensor("ones_row", [1, 128], BF16))
        S_csb, S_ident, S_ones = Slot(), Slot(), Slot()
        p.op("vector", lambda E: E.memset(csb[:, 0:1], ISD), writes=[S_csb],
             defer=True)
        p.op("vector", lambda E: E.memset(csb[:, 1:2], -SHIFT), writes=[S_csb])
        p.op("vector", lambda E: E.memset(ones_row[:, :], 1.0), writes=[S_ones])
        p.op("sync", lambda E: E.dma_start(out=ident[:, :], in_=ident_in[:, :]),
             writes=[S_ident], sem="lb1")
        p.bulk_end([S_ident], "lb1")

        def ln_chunk(xb, S_xb, st, S_st, xnb, S_xnb, xsq, S_xsq):
            p.op("vector", lambda E: E.scalar_tensor_tensor(
                xsq[:, :], xb[:, :], 1.0, xb[:, :], ALU.mult, ALU.mult,
                accum_out=st[:, 0:1]),
                reads=[S_xb], writes=[S_xsq, S_st[0]])
            p.op("vector", lambda E: E.reduce_sum(st[:, 1:2], xb[:, :], axis=AX),
                 reads=[S_xb], writes=[S_st[1]])
            p.op("vector", lambda E: E.tensor_scalar(
                st[:, 2:3], st[:, 1:2], 1.0 / C, None, ALU.mult),
                reads=[S_st[1]], writes=[S_st[2]])
            p.op("vector", lambda E: E.tensor_scalar(
                st[:, 3:4], st[:, 0:1], 1.0 / C, EPS, ALU.mult, ALU.add),
                reads=[S_st[0]], writes=[S_st[3]])
            p.op("vector", lambda E: E.scalar_tensor_tensor(
                st[:, 4:5], st[:, 2:3], -1.0, st[:, 2:3], ALU.mult, ALU.mult),
                reads=[S_st[2]], writes=[S_st[4]])
            p.op("vector", lambda E: E.tensor_add(
                st[:, 5:6], st[:, 3:4], st[:, 4:5]),
                reads=[S_st[3], S_st[4]], writes=[S_st[5]])
            p.op("scalar", lambda E: E.activation(st[:, 6:7], st[:, 5:6], AF.Sqrt),
                 reads=[S_st[5]], writes=[S_st[6]])
            p.op("vector", lambda E: E.reciprocal(st[:, 7:8], st[:, 6:7]),
                 reads=[S_st[6]], writes=[S_st[7]])
            p.op("vector", lambda E: E.tensor_mul(st[:, 4:5], st[:, 7:8], st[:, 7:8]),
                 reads=[S_st[7]], writes=[S_st[4]])
            p.op("vector", lambda E: E.tensor_mul(st[:, 4:5], st[:, 4:5], st[:, 5:6]),
                 reads=[S_st[4], S_st[5]], writes=[S_st[4]])
            p.op("vector", lambda E: E.tensor_scalar(
                st[:, 4:5], st[:, 4:5], -0.5, 1.5, ALU.mult, ALU.add),
                reads=[S_st[4]], writes=[S_st[4]])
            p.op("vector", lambda E: E.tensor_mul(st[:, 6:7], st[:, 7:8], st[:, 4:5]),
                 reads=[S_st[7], S_st[4]], writes=[S_st[6]])
            p.op("vector", lambda E: E.scalar_tensor_tensor(
                st[:, 3:4], st[:, 2:3], -1.0, st[:, 6:7], ALU.mult, ALU.mult),
                reads=[S_st[2], S_st[6]], writes=[S_st[3]])
            p.op("scalar", lambda E: E.activation(
                xnb[:, :], xb[:, :], AF.Identity, bias=st[:, 3:4],
                scale=st[:, 6:7]),
                reads=[S_xb, S_st[3], S_st[6]], writes=[S_xnb])

        def transpose128(src_ap, S_srcs, dst_ap, S_dst, ptb, S_ptb):
            p.op("tensor", lambda E: E.transpose(ptb[:, 0:128], src_ap,
                                                 ident[:, :]),
                 reads=list(S_srcs) + [S_ident], writes=[S_ptb])
            p.op("vector", lambda E: E.tensor_copy(dst_ap, ptb[:, 0:128]),
                 reads=[S_ptb], writes=[S_dst])

        S_lat_kv = Slot()
        S_lat_q = [Slot(), Slot()]
        with contextlib.ExitStack() as ph1:
            xb = [ph1.enter_context(nc.sbuf_tensor(f"xb{i}", [128, C], F32))
                  for i in range(4)]
            xsq = [ph1.enter_context(nc.sbuf_tensor(f"xsq{i}", [128, C], BF16))
                   for i in range(2)]
            stt = [ph1.enter_context(nc.sbuf_tensor(f"stt{i}", [128, 8], F32))
                   for i in range(2)]
            xnb = [ph1.enter_context(nc.sbuf_tensor(f"xnb{i}", [128, C], BF16))
                   for i in range(2)]
            xn_fm = ph1.enter_context(nc.sbuf_tensor("xn_fm", [128, 16, TOK], BF16))
            wqd = ph1.enter_context(nc.sbuf_tensor("wqd", [128, 16, QR], BF16))
            wkvd = ph1.enter_context(nc.sbuf_tensor("wkvd", [128, 16, KVR], BF16))
            lat_sb = [ph1.enter_context(nc.sbuf_tensor(f"lat_sb{i}", [128, 512], BF16))
                      for i in range(2)]
            ptb1 = [ph1.enter_context(nc.psum_tensor(f"ptb1_{i}", [128, 1024], BF16))
                    for i in range(2)]
            pl1 = [ph1.enter_context(nc.psum_tensor(f"pl1_{i}", [128, 512], F32))
                   for i in range(2)]

            S_xb = [Slot() for _ in range(4)]
            S_xsq = [Slot(), Slot()]
            S_stt = [[Slot() for _ in range(8)] for _ in range(2)]
            S_xnb = [Slot(), Slot()]
            S_xn = [[Slot() for _ in range(NTC)] for _ in range(16)]
            S_wqd = [Slot() for _ in range(16)]
            S_wkvd = [Slot() for _ in range(16)]
            S_latsb = [Slot(), Slot()]
            S_ptb1 = [Slot(), Slot()]
            S_pl1 = [Slot(), Slot()]

            for k in range(16):
                p.op("gpsimd", lambda E, k=k: E.dma_start(
                    out=wqd[:, k, :], in_=wq_down[k * 128:(k + 1) * 128, :]),
                    writes=[S_wqd[k]], sem="lw1")
                p.op("gpsimd", lambda E, k=k: E.dma_start(
                    out=wkvd[:, k, :], in_=wkv_down[k * 128:(k + 1) * 128, :]),
                    writes=[S_wkvd[k]], sem="lw1")
            p.bulk_end(S_wqd + S_wkvd, "lw1")
            for k in range(16):
                p.op("gpsimd", lambda E, k=k: E.dma_start(
                    out=gsb[:, k, :], in_=gate_aug[k * 128:(k + 1) * 128, :]),
                    writes=[S_gsb], sem="lw3")
            p.op("gpsimd", lambda E: E.dma_start(
                out=gsb[0:1, 16, :], in_=gate_aug[C:C + 1, :]),
                writes=[S_gsb], sem="lw3")
            p.op("gpsimd", lambda E: E.dma_start(
                out=lat3[:, :, :],
                in_=lora_at[:, :].rearrange("(kq p) r -> p kq r", p=128)),
                writes=S_lat3, sem="lw3")
            p.op("gpsimd", lambda E: E.dma_start(
                out=bias_sb[:, :], in_=wt_aug[C:C + 1, :]),
                writes=[S_bias], sem="lw3")
            p.bulk_end([S_gsb, S_bias] + S_lat3, "lw3")


            def emit_lat(mi_list, nt):
                tiles = []
                for mi in mi_list:
                    bank = mi % 2
                    for k in range(16):
                        if mi < 12:
                            lhs = wqd[:, k, mi * 128:(mi + 1) * 128]
                            rsl = S_wqd[k]
                        else:
                            lhs = wkvd[:, k, (mi - 12) * 128:(mi - 11) * 128]
                            rsl = S_wkvd[k]
                        p.op("tensor",
                             lambda E, lhs=lhs, k=k, nt=nt, bank=bank:
                             E.matmul(pl1[bank][:, :], lhs,
                                      xn_fm[:, k, nt * 512:(nt + 1) * 512],
                                      start=(k == 0), stop=(k == 15)),
                             reads=[rsl] + [S_xn[k][nt * 4 + j]
                                            for j in range(4)],
                             writes=[S_pl1[bank]], defer=(k != 15))
                    sbuf = lat_sb[bank]
                    p.op("vector", lambda E, sbuf=sbuf, bank=bank:
                         E.tensor_copy(sbuf[:, :], pl1[bank][:, :]),
                         reads=[S_pl1[bank]], writes=[S_latsb[bank]])
                    S_tile = Slot()
                    tiles.append(S_tile)
                    p.op("sync", lambda E, sbuf=sbuf, mi=mi, nt=nt:
                         E.dma_start(
                             out=lat_in[mi * 128:(mi + 1) * 128,
                                        nt * 512:(nt + 1) * 512],
                             in_=sbuf[:, :]),
                         reads=[S_latsb[bank]], writes=[S_tile],
                         sem=("la0", "la1")[bank])
                return tiles

            def emit_ln(tcs):
                for tc in tcs:
                    i = tc % 4
                    j = tc % 2
                    p.op("sync", lambda E, tc=tc, i=i: E.dma_start(
                        out=xb[i][:, :], in_=x_own[tc * 128:(tc + 1) * 128, :]),
                        writes=[S_xb[i]], sem=("xa", "xb", "xc", "xd")[i])
                    ln_chunk(xb[i], S_xb[i], stt[j], S_stt[j], xnb[j], S_xnb[j],
                             xsq[j], S_xsq[j])
                    for k in range(16):
                        transpose128(xnb[j][:, k * 128:(k + 1) * 128], [S_xnb[j]],
                                     xn_fm[:, k, tc * 128:(tc + 1) * 128],
                                     S_xn[k][tc], ptb1[k % 2], S_ptb1[k % 2])

            emit_ln(range(0, 4))
            kv_tiles = emit_lat([12, 13, 14, 15], 0)
            emit_ln(range(4, 8))
            kv_tiles += emit_lat([12, 13, 14, 15], 1)
            p.op("gpsimd", lambda E: E.collective_compute(
                "AllGather", ALU.bypass, replica_groups=GROUPS,
                ins=[lat_in[QR:C, :]], outs=[lat_out_kv[:, :]]),
                reads=kv_tiles, writes=[S_lat_kv], sem="cc", inc=1)
            q_tiles = emit_lat(list(range(6)), 0) + emit_lat(list(range(6)), 1)
            p.op("gpsimd", lambda E: E.collective_compute(
                "AllGather", ALU.bypass, replica_groups=GROUPS,
                ins=[lat_in[0:QR // 2, :]], outs=[lat_out_q[0][:, :]]),
                reads=q_tiles, writes=[S_lat_q[0]], sem="cc", inc=1)
            q_tiles = emit_lat(list(range(6, 12)), 0) + emit_lat(list(range(6, 12)), 1)
            p.op("gpsimd", lambda E: E.collective_compute(
                "AllGather", ALU.bypass, replica_groups=GROUPS,
                ins=[lat_in[QR // 2:QR, :]], outs=[lat_out_q[1][:, :]]),
                reads=q_tiles, writes=[S_lat_q[1]], sem="cc", inc=1)

            p.barrier(exclude=("cc",))
            p.flush(block)

        S_at_in = [Slot(), Slot()]
        S_at_out = [Slot(), Slot()]
        with contextlib.ExitStack() as ph2:
            kvt = [ph2.enter_context(nc.sbuf_tensor(f"kvt{i}", [128, 4, 512], BF16))
                   for i in range(2)]
            qlb = [ph2.enter_context(nc.sbuf_tensor(f"qlb{i}", [128, 12, 512], BF16))
                   for i in range(2)]
            qc2 = ph2.enter_context(nc.sbuf_tensor("qc2", [128, 2, T], BF16))
            qr2 = ph2.enter_context(nc.sbuf_tensor("qr2", [128, T], BF16))
            kc2 = ph2.enter_context(nc.sbuf_tensor("kc2", [128, 2, T], BF16))
            kr2 = ph2.enter_context(nc.sbuf_tensor("kr2", [128, 2, T], BF16))
            v2 = ph2.enter_context(nc.sbuf_tensor("v2", [128, 32, 386], BF16))
            ath = ph2.enter_context(nc.sbuf_tensor("ath", [128, 32, 384], BF16))
            cds = ph2.enter_context(nc.sbuf_tensor("cds", [64, T], BF16))
            sns = ph2.enter_context(nc.sbuf_tensor("sns", [64, T], BF16))
            msk = ph2.enter_context(nc.sbuf_tensor("msk", [128, 4 * 512], BF16))
            wqu = ph2.enter_context(nc.sbuf_tensor("wqu", [128, 12, 768], BF16))
            wku = ph2.enter_context(nc.sbuf_tensor("wku", [128, 4, 768], BF16))
            wvu = ph2.enter_context(nc.sbuf_tensor("wvu", [128, 4, 768], BF16))
            ptile = [ph2.enter_context(nc.sbuf_tensor(f"ptile{i}", [128, 512], BF16))
                     for i in range(3)]
            rA = ph2.enter_context(nc.sbuf_tensor("rA", [128, 512], F32))
            rB = ph2.enter_context(nc.sbuf_tensor("rB", [128, 512], F32))
            rec = ph2.enter_context(nc.sbuf_tensor("rec", [128, 4], F32))
            psT = [ph2.enter_context(nc.psum_tensor(f"ps2_{i}", [128, 512], F32))
                   for i in range(8)]
            ps = [t[:, :] for t in psT]

            S_kvt = [Slot(), Slot()]
            S_qlb = [[Slot(), Slot()] for _ in range(2)]
            S_qc = [[Slot() for _ in range(8)] for _ in range(2)]
            S_qr = [Slot() for _ in range(8)]
            S_kc = [[Slot() for _ in range(8)] for _ in range(2)]
            S_kr = [Slot() for _ in range(8)]
            S_v = [[Slot() for _ in range(2)] for _ in range(32)]
            S_vones = Slot()
            S_ath = [[Slot() for _ in range(2)] for _ in range(32)]
            S_cds, S_sns, S_msk = Slot(), Slot(), Slot()
            S_wqu = [Slot() for _ in range(12)]
            S_wku = [Slot() for _ in range(4)]
            S_wvu = [Slot() for _ in range(4)]
            S_pt = [Slot() for _ in range(3)]
            S_rA, S_rB, S_rec = Slot(), Slot(), Slot()
            S_ps = [Slot() for _ in range(8)]

            p.op("sync", lambda E: E.dma_start(out=cds[:, :], in_=cdup_in[:, :]),
                 writes=[S_cds], sem="lb2")
            p.op("sync", lambda E: E.dma_start(out=sns[:, :], in_=sdup_in[:, :]),
                 writes=[S_sns], sem="lb2")
            p.op("sync", lambda E: E.dma_start(out=msk[:, :], in_=maskc[:, :]),
                 writes=[S_msk], sem="lb2")
            for k in range(12):
                p.op("gpsimd", lambda E, k=k: E.dma_start(
                    out=wqu[:, k, :], in_=wq_up[k * 128:(k + 1) * 128, :]),
                    writes=[S_wqu[k]], sem="lw2")
            for k in range(4):
                p.op("gpsimd", lambda E, k=k: E.dma_start(
                    out=wku[:, k, :], in_=wk_up[k * 128:(k + 1) * 128, :]),
                    writes=[S_wku[k]], sem="lw2")
                p.op("gpsimd", lambda E, k=k: E.dma_start(
                    out=wvu[:, k, :], in_=wv_up[k * 128:(k + 1) * 128, :]),
                    writes=[S_wvu[k]], sem="lw2")
            p.bulk_end(S_wqu + S_wku + S_wvu, "lw2")
            p.bulk_end([S_cds, S_sns, S_msk], "lb2")
            p.op("vector", lambda E: E.memset(v2[:, :, 192:193], 1.0),
                 writes=[S_vones], defer=True)
            p.op("vector", lambda E: E.memset(v2[:, :, 385:386], 1.0),
                 writes=[S_vones])
            S_krz = Slot()
            p.op("vector", lambda E: E.memset(kr2[64:128, 0, :], 0.0),
                 writes=[S_krz], defer=True)
            p.op("vector", lambda E: E.memset(kr2[0:64, 1, :], 0.0),
                 writes=[S_krz])

            def rope_store(pbank, S_pbank, t0, dstq, S_dst, dsts=None):
                n = 512
                p.op("vector", lambda E: E.tensor_mul(
                    rA[0:64, :], pbank[0:64, :], cds[:, t0:t0 + n]),
                    reads=[S_pbank, S_cds], writes=[S_rA], defer=True)
                p.op("vector", lambda E: E.tensor_mul(
                    rA[64:128, :], pbank[64:128, :], cds[:, t0:t0 + n]),
                    reads=[S_pbank, S_cds], writes=[S_rA])
                p.op("vector", lambda E: E.tensor_mul(
                    rB[0:64, :], pbank[64:128, :], sns[:, t0:t0 + n]),
                    reads=[S_pbank, S_sns], writes=[S_rB], defer=True)
                p.op("vector", lambda E: E.tensor_mul(
                    rB[64:128, :], pbank[0:64, :], sns[:, t0:t0 + n]),
                    reads=[S_pbank, S_sns], writes=[S_rB])
                for hh in range(2):
                    d1 = (dstq[hh * 64:hh * 64 + 32, :] if dsts is None
                          else dsts[hh][0])
                    d2 = (dstq[hh * 64 + 32:hh * 64 + 64, :] if dsts is None
                          else dsts[hh][1])
                    p.op("vector", lambda E, hh=hh, d1=d1: E.tensor_sub(
                        d1, rA[hh * 32:hh * 32 + 32, :],
                        rB[hh * 32:hh * 32 + 32, :]),
                        reads=[S_rA, S_rB], writes=[S_dst], defer=True)
                    p.op("vector", lambda E, hh=hh, d2=d2: E.tensor_add(
                        d2, rB[64 + hh * 32:96 + hh * 32, :],
                        rA[64 + hh * 32:96 + hh * 32, :]),
                        reads=[S_rA, S_rB], writes=[S_dst],
                        defer=(hh == 0))

            for pp in range(2):
                for nt in range(8):
                    rr = nt // 2
                    ch = nt % 2
                    i = nt % 2
                    p.op("sync", lambda E, rr=rr, ch=ch, i=i: E.dma_start(
                        out=kvt[i][:, :, :],
                        in_=lat_out_kv[rr * KVR:(rr + 1) * KVR,
                                       ch * 512:(ch + 1) * 512].rearrange(
                            "(kk pk) n -> pk kk n", pk=128)),
                        reads=[S_lat_kv], writes=[S_kvt[i]],
                        sem=("ka", "kb")[i])
                    for mj, mi in enumerate((2 * pp, 2 * pp + 1, 4 + pp)):
                        for kk in range(4):
                            p.op("tensor",
                                 lambda E, mi=mi, kk=kk, i=i, mj=mj: E.matmul(
                                     ps[mj],
                                     wku[:, kk, mi * 128:(mi + 1) * 128],
                                     kvt[i][:, kk, :],
                                     start=(kk == 0), stop=(kk == 3)),
                                 reads=[S_wku[kk], S_kvt[i]],
                                 writes=[S_ps[mj]], defer=(kk != 3))
                    for mj in range(2):
                        p.op("vector", lambda E, mj=mj, nt=nt: E.tensor_copy(
                            kc2[:, mj, nt * 512:(nt + 1) * 512], ps[mj]),
                            reads=[S_ps[mj]], writes=[S_kc[mj][nt]])
                    rope_store(
                        ps[2], S_ps[2], nt * 512, None, S_kr[nt],
                        dsts=[(kr2[0:32, 0, nt * 512:(nt + 1) * 512],
                               kr2[32:64, 0, nt * 512:(nt + 1) * 512]),
                              (kr2[64:96, 1, nt * 512:(nt + 1) * 512],
                               kr2[96:128, 1, nt * 512:(nt + 1) * 512])])
                    for tcl in range(4):
                        tc = nt * 4 + tcl
                        bank = (3, 4, 5, 7)[tcl]
                        for kk in range(4):
                            p.op("tensor",
                                 lambda E, kk=kk, tcl=tcl, pp=pp, i=i, bank=bank:
                                 E.matmul(
                                     ps[bank][:, 0:384],
                                     kvt[i][:, kk, tcl * 128:(tcl + 1) * 128],
                                     wvu[:, kk, 2 * pp * 192:2 * pp * 192 + 384],
                                     start=(kk == 0), stop=(kk == 3)),
                                 reads=[S_wvu[kk], S_kvt[i]],
                                 writes=[S_ps[bank]], defer=(kk != 3))
                        p.op("vector",
                             lambda E, tc=tc, bank=bank: E.tensor_copy(
                                 v2[:, tc, 0:386].rearrange(
                                     "p (h x) -> p h x", h=2)[:, :, 0:192],
                                 ps[bank][:, 0:384].rearrange(
                                     "p (h x) -> p h x", h=2)),
                             reads=[S_ps[bank]],
                             writes=[S_v[tc][0], S_v[tc][1]])

                for r in range(4):
                    for nh in range(2):
                        i = (r * 2 + nh) % 2
                        for h in range(2):
                            p.op("sync", lambda E, r=r, nh=nh, i=i, h=h:
                                 E.dma_start(
                                     out=qlb[i][:, 6 * h:6 * h + 6, :],
                                     in_=lat_out_q[h][
                                         r * (QR // 2):(r + 1) * (QR // 2),
                                         nh * 512:(nh + 1) * 512].rearrange(
                                         "(kq pq) n -> pq kq n", pq=128)),
                                 reads=[S_lat_q[h]], writes=[S_qlb[i][h]],
                                 sem=("qa", "qb", "qc", "qd")[i * 2 + h])
                        t0 = r * TOK + nh * 512
                        for mj, mi in enumerate((2 * pp, 2 * pp + 1, 4 + pp)):
                            for k in range(12):
                                p.op("tensor",
                                     lambda E, mi=mi, k=k, i=i, mj=mj:
                                     E.matmul(ps[mj],
                                              wqu[:, k, mi * 128:(mi + 1) * 128],
                                              qlb[i][:, k, :],
                                              start=(k == 0), stop=(k == 11)),
                                     reads=[S_wqu[k], S_qlb[i][k // 6]],
                                     writes=[S_ps[mj]], defer=(k != 11))
                        for mj in range(2):
                            p.op("vector",
                                 lambda E, mj=mj, t0=t0: E.tensor_copy(
                                     qc2[:, mj, t0:t0 + 512], ps[mj]),
                                 reads=[S_ps[mj]], writes=[S_qc[mj][t0 // 512]])
                        rope_store(ps[2], S_ps[2], t0,
                                   qr2[:, t0:t0 + 512], S_qr[t0 // 512])

                for hl in range(2):
                    for qt in range(8):
                        nsc = (qt + 1) * 4
                        q0 = qt * 512

                        def emit_scores(sc, hl=hl, q0=q0):
                            sb = 4 + (sc % 3)
                            p.op("tensor",
                                 lambda E, hl=hl, sc=sc, q0=q0, sb=sb: E.matmul(
                                     ps[sb],
                                     kc2[:, hl, sc * 128:(sc + 1) * 128],
                                     qc2[:, hl, q0:q0 + 512],
                                     start=True, stop=False),
                                 reads=[S_kc[hl][sc // 4], S_qc[hl][q0 // 512]],
                                 writes=[S_ps[sb]], defer=True)
                            p.op("tensor",
                                 lambda E, hl=hl, sc=sc, q0=q0, sb=sb: E.matmul(
                                     ps[sb],
                                     kr2[:, hl, sc * 128:(sc + 1) * 128],
                                     qr2[:, q0:q0 + 512],
                                     start=False, stop=True),
                                 reads=[S_kr[sc // 4], S_qr[q0 // 512], S_krz],
                                 writes=[S_ps[sb]])

                        def emit_exp(sc, qt=qt):
                            sb = 4 + (sc % 3)
                            pt = ptile[sc % 3]
                            ipt = sc % 3
                            p.op("scalar", lambda E, sb=sb, pt=pt: E.activation(
                                pt[:, :], ps[sb], AF.Exp,
                                bias=csb[:, 1:2], scale=csb[:, 0:1]),
                                reads=[S_ps[sb], S_csb], writes=[S_pt[ipt]])
                            if sc >= qt * 4:
                                off = (sc - qt * 4) * 512
                                p.op("vector", lambda E, pt=pt, off=off:
                                     E.tensor_mul(pt[:, :], pt[:, :],
                                                  msk[:, off:off + 512]),
                                     reads=[S_pt[ipt], S_msk],
                                     writes=[S_pt[ipt]])

                        def emit_pv(sc, hl=hl, nsc=nsc):
                            pt = ptile[sc % 3]
                            ipt = sc % 3
                            for qc4 in range(4):
                                p.op("tensor",
                                     lambda E, pt=pt, qc4=qc4, sc=sc, hl=hl,
                                     nsc=nsc: E.matmul(
                                         ps[qc4][:, 0:193],
                                         pt[:, qc4 * 128:(qc4 + 1) * 128],
                                         v2[:, sc, hl * 193:(hl + 1) * 193],
                                         start=(sc == 0), stop=(sc == nsc - 1),
                                         skip_group_check=True),
                                     reads=[S_pt[ipt], S_v[sc][hl], S_vones],
                                     writes=[S_ps[qc4]], defer=(qc4 != 3))

                        emit_scores(0)
                        if nsc > 1:
                            emit_scores(1)
                        for sc in range(nsc):
                            emit_exp(sc)
                            if sc + 2 < nsc:
                                emit_scores(sc + 2)
                            emit_pv(sc)
                        for qc4 in range(4):
                            tc = qt * 4 + qc4
                            p.op("vector", lambda E, qc4=qc4: E.reciprocal(
                                rec[:, qc4:qc4 + 1], ps[qc4][:, 192:193]),
                                reads=[S_ps[qc4]], writes=[S_rec])
                            p.op("vector",
                                 lambda E, qc4=qc4, tc=tc, hl=hl: E.tensor_scalar(
                                     ath[:, tc, hl * 192:(hl + 1) * 192],
                                     ps[qc4][:, 0:192], rec[:, qc4:qc4 + 1],
                                     None, ALU.mult),
                                 reads=[S_ps[qc4], S_rec], writes=[S_ath[tc][hl]])

                ath_deps = [S_ath[tc][hl] for tc in range(32) for hl in range(2)]
                S_store = Slot()
                for q4 in range(4):
                    p.op("sync", lambda E, q4=q4, pp=pp: E.dma_start(
                        out=at_in[pp][q4 * 1024:(q4 + 1) * 1024, :].rearrange(
                            "(c p) f -> p c f", p=128),
                        in_=ath[:, q4 * 8:(q4 + 1) * 8, :]),
                        reads=ath_deps, writes=[S_store], sem="sta")
                p.op("gpsimd", lambda E, pp=pp: E.collective_compute(
                    "AllGather", ALU.bypass, replica_groups=GROUPS,
                    ins=[at_in[pp][:, :]], outs=[at_out[pp][:, :]]),
                    reads=[S_store, S_at_in[pp]], writes=[S_at_out[pp]],
                    sem="cc", inc=1)

            p.barrier(exclude=("cc",))
            p.flush(block)

        with contextlib.ExitStack() as ph3:
            xn3 = ph3.enter_context(nc.sbuf_tensor("xn3", [128, 16, TOK], BF16))
            afm = ph3.enter_context(nc.sbuf_tensor("afm", [128, 24, TOK], BF16))
            lgs = ph3.enter_context(nc.sbuf_tensor("lgs", [128, 8], F32))
            t2 = ph3.enter_context(nc.sbuf_tensor("t2", [128, 32], F32))
            wdn = ph3.enter_context(nc.sbuf_tensor("wdn", [128, 8, 8], F32))
            xag = ph3.enter_context(nc.sbuf_tensor("xag", [128, 128], BF16))
            xafm = ph3.enter_context(nc.sbuf_tensor("xafm", [128, TOK], BF16))
            xsl = [ph3.enter_context(nc.sbuf_tensor(f"xsl{i}", [128, 512], F32))
                   for i in range(2)]
            wst = [ph3.enter_context(nc.sbuf_tensor(f"wst{i}", [128, 512], F32))
                   for i in range(4)]
            osb = [ph3.enter_context(nc.sbuf_tensor(f"osb{i}", [128, 512], F32))
                   for i in range(2)]
            ptb3 = [ph3.enter_context(nc.psum_tensor(f"ptb3_{i}", [128, 1024], BF16))
                    for i in range(2)]
            pf = [ph3.enter_context(nc.psum_tensor(f"pf{i}", [128, 512], F32))
                  for i in range(4)]
            ph3a = contextlib.ExitStack()
            aown = ph3a.enter_context(nc.sbuf_tensor("aown", [128, 8, 3072], BF16))
            xb3 = [ph3a.enter_context(nc.sbuf_tensor(f"xb3_{i}", [128, C], F32))
                   for i in range(4)]
            xsq3 = [ph3a.enter_context(nc.sbuf_tensor(f"xsq3_{i}", [128, C], BF16))
                    for i in range(2)]
            st3 = [ph3a.enter_context(nc.sbuf_tensor(f"st3_{i}", [128, 8], F32))
                   for i in range(2)]
            xnb3 = [ph3a.enter_context(nc.sbuf_tensor(f"xnb3_{i}", [128, C], BF16))
                    for i in range(2)]

            S_aownc = [Slot() for _ in range(8)]
            S_xb3 = [Slot() for _ in range(4)]
            S_xsq3 = [Slot(), Slot()]
            S_st3 = [[Slot() for _ in range(8)] for _ in range(2)]
            S_xnb3 = [Slot(), Slot()]
            S_xn3 = [[Slot() for _ in range(NTC)] for _ in range(16)]
            S_afm = [[Slot() for _ in range(NTC)] for _ in range(24)]
            S_lgs = Slot()
            S_t2 = Slot()
            S_wdn = [Slot() for _ in range(8)]
            S_xag = Slot()
            S_xafm = [Slot() for _ in range(8)]
            S_wnt = [[Slot() for _ in range(41)] for _ in range(2)]
            S_wst = [Slot() for _ in range(4)]
            S_xsl = [Slot(), Slot()]
            S_osb = [Slot(), Slot()]
            S_ptb3 = [Slot(), Slot()]
            S_pf = [Slot() for _ in range(4)]
            S_out = Slot()



            cc_toks = [S_at_out[0].w, S_at_out[1].w]

            def aown_dmas(E):
                sems = p.sems
                last = None
                with E.register("goff") as rg:
                    offs = []
                    for r in range(4):
                        E.reg_load(rg, gidx[r:r + 1, 0:1])
                        offs.append(E.snap(rg, min_val=0,
                                           max_val=3 * T + 3 * TOK))
                    for ppx in range(2):
                        E.wait_ge(sems[cc_toks[ppx][0]], cc_toks[ppx][1])
                        for r in range(4):
                            col = r * 768 + ppx * 384
                            last = E.dma_start(
                                out=aown[:, :, col:col + 384],
                                in_=at_out[ppx][bass.ds(offs[r], TOK), :].rearrange(
                                    "(c p) f -> p c f", p=128))
                            last.then_inc(sems[("lag", "lag2")[ppx]], 16)
                return last

            p.op("gpsimd", aown_dmas, reads=[],
                 writes=S_aownc, sem="lag", inc=4 * 16, raw=True)
            p.cnt["lag2"] += 4 * 16
            for r in range(4):
                S_aownc[2 * r + 0].w = ("lag", p.cnt["lag"])
                S_aownc[2 * r + 1].w = ("lag2", p.cnt["lag2"])

            for tc in range(NTC):
                i = tc % 4
                j = tc % 2
                p.op("sync", lambda E, tc=tc, i=i: E.dma_start(
                    out=xb3[i][:, :], in_=x_own[tc * 128:(tc + 1) * 128, :]),
                    writes=[S_xb3[i]], sem=("xa", "xb", "xc", "xd")[i])
                ln_chunk(xb3[i], S_xb3[i], st3[j], S_st3[j], xnb3[j],
                         S_xnb3[j], xsq3[j], S_xsq3[j])
                for k in range(16):
                    transpose128(xnb3[j][:, k * 128:(k + 1) * 128], [S_xnb3[j]],
                                 xn3[:, k, tc * 128:(tc + 1) * 128],
                                 S_xn3[k][tc], ptb3[k % 2], S_ptb3[k % 2])

            for qc in range(8):
                for k in range(16):
                    p.op("tensor", lambda E, k=k, qc=qc: E.matmul(
                        pf[0][:, 0:8], xn3[:, k, qc * 128:(qc + 1) * 128],
                        gsb[:, k, :], start=(k == 0), stop=False),
                        reads=[S_xn3[k][qc], S_gsb], writes=[S_pf[0]],
                        defer=True)
                p.op("tensor", lambda E: E.matmul(
                    pf[0][:, 0:8], ones_row[:, :], gsb[0:1, 16, :],
                    start=False, stop=True),
                    reads=[S_ones, S_gsb], writes=[S_pf[0]])
                p.op("vector", lambda E: E.tensor_copy(lgs[:, :], pf[0][:, 0:8]),
                     reads=[S_pf[0]], writes=[S_lgs])
                p.op("vector", lambda E: E.reduce_max(t2[:, 8:9], lgs[:, :], axis=AX),
                     reads=[S_lgs], writes=[S_t2])
                p.op("vector", lambda E, qc=qc: E.tensor_tensor(
                    wdn[:, qc, :], lgs[:, :],
                    t2[:, 8:9].to_broadcast((128, 8)), ALU.is_ge),
                    reads=[S_lgs, S_t2], writes=[S_wdn[qc]])
                p.op("vector", lambda E, qc=qc: E.scalar_tensor_tensor(
                    t2[:, 16:24], wdn[:, qc, :], -1e30, lgs[:, :],
                    ALU.mult, ALU.add),
                    reads=[S_lgs, S_wdn[qc]], writes=[S_t2])
                p.op("vector", lambda E: E.reduce_max(
                    t2[:, 9:10], t2[:, 16:24], axis=AX),
                    reads=[S_t2], writes=[S_t2])
                p.op("vector", lambda E: E.tensor_tensor(
                    t2[:, 24:32], t2[:, 16:24],
                    t2[:, 9:10].to_broadcast((128, 8)), ALU.is_ge),
                    reads=[S_t2], writes=[S_t2])
                p.op("vector", lambda E: E.tensor_sub(
                    t2[:, 10:11], t2[:, 8:9], t2[:, 9:10]),
                    reads=[S_t2], writes=[S_t2])
                p.op("scalar", lambda E: E.activation(
                    t2[:, 11:12], t2[:, 10:11], AF.Sigmoid),
                    reads=[S_t2], writes=[S_t2])
                p.op("scalar", lambda E: E.activation(
                    t2[:, 12:13], t2[:, 10:11], AF.Sigmoid, scale=-1.0),
                    reads=[S_t2], writes=[S_t2])
                p.op("vector", lambda E, qc=qc: E.tensor_scalar(
                    wdn[:, qc, :], wdn[:, qc, :], t2[:, 11:12], None, ALU.mult),
                    reads=[S_wdn[qc], S_t2], writes=[S_wdn[qc]])
                p.op("vector", lambda E, qc=qc: E.scalar_tensor_tensor(
                    wdn[:, qc, :], t2[:, 24:32], t2[:, 12:13], wdn[:, qc, :],
                    ALU.mult, ALU.add),
                    reads=[S_t2, S_wdn[qc]], writes=[S_wdn[qc]])
                for k in range(16):
                    p.op("tensor", lambda E, k=k, qc=qc: E.matmul(
                        pf[1][:, 0:128], xn3[:, k, qc * 128:(qc + 1) * 128],
                        lat3[:, k, :], start=(k == 0), stop=(k == 15)),
                        reads=[S_xn3[k][qc], S_lat3[k]], writes=[S_pf[1]],
                        defer=(k != 15))
                p.op("vector", lambda E, qc=qc: E.tensor_tensor(
                    xag[:, :].rearrange("p (e r) -> p e r", r=16),
                    pf[1][:, 0:128].rearrange("p (e r) -> p e r", r=16),
                    wdn[:, qc, :][:, :, None].to_broadcast((128, 8, 16)),
                    ALU.mult),
                    reads=[S_pf[1], S_wdn[qc]], writes=[S_xag])
                transpose128(xag[:, :], [S_xag], xafm[:, qc * 128:(qc + 1) * 128],
                             S_xafm[qc], ptb3[0], S_ptb3[0])


            for tc in range(NTC):
                for k in range(24):
                    transpose128(aown[:, tc, k * 128:(k + 1) * 128],
                                 [S_aownc[k // 3]],
                                 afm[:, k, tc * 128:(tc + 1) * 128],
                                 S_afm[k][tc], ptb3[k % 2], S_ptb3[k % 2])

            for k in range(4):
                srcap = wo_in[k * 128:(k + 1) * 128, 0:512]
                p.op("sync", lambda E, srcap=srcap, k=k: E.dma_start(
                    out=wst[k][:, :], in_=srcap),
                    writes=[S_wst[k]], sem=("wa", "wb", "wc", "wd")[k])
            p.barrier(exclude=("cc", "wa", "wb", "wc", "wd"))
            ph3a.close()
            wnt = [ph3.enter_context(nc.sbuf_tensor(f"wnt{i}", [128, 41, 512], BF16))
                   for i in range(2)]

            def emit_w_loads(nt, ks):
                for k in ks:
                    if k < 24:
                        srcap = wo_in[k * 128:(k + 1) * 128,
                                      nt * 512:(nt + 1) * 512]
                    elif k < 40:
                        srcap = wt_aug[(k - 24) * 128:(k - 23) * 128,
                                       nt * 512:(nt + 1) * 512]
                    else:
                        srcap = lora_bc[:, nt * 512:(nt + 1) * 512]
                    j = k % 4
                    p.op("sync", lambda E, srcap=srcap, j=j: E.dma_start(
                        out=wst[j][:, :], in_=srcap),
                        writes=[S_wst[j]], sem=("wa", "wb", "wc", "wd")[j])

            def emit_w_casts(nt, ks):
                wb = nt % 2
                for k in ks:
                    j = k % 4
                    p.op("vector", lambda E, k=k, wb=wb, j=j: E.tensor_copy(
                        wnt[wb][:, k, :], wst[j][:, :]),
                        reads=[S_wst[j]], writes=[S_wnt[wb][k]])

            def emit_w_chunk(nt, ks):
                for k in ks:
                    emit_w_loads(nt, [k])
                    emit_w_casts(nt, [k])

            emit_w_casts(0, range(4))
            emit_w_chunk(0, range(4, 41))
            for nt in range(4):
                wb = nt % 2
                for qc in range(8):
                    if nt + 1 < 4:
                        emit_w_chunk(nt + 1,
                                     range(qc * 6, min((qc + 1) * 6, 41)))
                    bank = 2 + qc % 2
                    for k in range(24):
                        p.op("tensor", lambda E, k=k, qc=qc, wb=wb, bank=bank:
                             E.matmul(pf[bank][:, :],
                                      afm[:, k, qc * 128:(qc + 1) * 128],
                                      wnt[wb][:, k, :],
                                      start=(k == 0), stop=False),
                             reads=[S_afm[k][qc], S_wnt[wb][k]],
                             writes=[S_pf[bank]], defer=True)
                    for k in range(16):
                        p.op("tensor", lambda E, k=k, qc=qc, wb=wb, bank=bank:
                             E.matmul(pf[bank][:, :],
                                      xn3[:, k, qc * 128:(qc + 1) * 128],
                                      wnt[wb][:, 24 + k, :],
                                      start=False, stop=False),
                             reads=[S_xn3[k][qc], S_wnt[wb][24 + k]],
                             writes=[S_pf[bank]], defer=True)
                    p.op("tensor", lambda E, qc=qc, nt=nt, bank=bank: E.matmul(
                        pf[bank][:, :], ones_row[:, :],
                        bias_sb[:, nt * 512:(nt + 1) * 512],
                        start=False, stop=False),
                        reads=[S_ones, S_bias], writes=[S_pf[bank]], defer=True)
                    p.op("tensor", lambda E, qc=qc, wb=wb, bank=bank: E.matmul(
                        pf[bank][:, :], xafm[:, qc * 128:(qc + 1) * 128],
                        wnt[wb][:, 40, :], start=False, stop=True),
                        reads=[S_xafm[qc], S_wnt[wb][40]],
                        writes=[S_pf[bank]])
                    i = qc % 2
                    p.op("sync", lambda E, qc=qc, nt=nt, i=i: E.dma_start(
                        out=xsl[i][:, :],
                        in_=x_own[qc * 128:(qc + 1) * 128,
                                  nt * 512:(nt + 1) * 512]),
                        writes=[S_xsl[i]], sem=("sa", "sb")[i])
                    p.op("vector", lambda E, i=i, bank=bank: E.tensor_add(
                        osb[i][:, :], pf[bank][:, :], xsl[i][:, :]),
                        reads=[S_pf[bank], S_xsl[i]], writes=[S_osb[i]])
                    p.op("sync", lambda E, qc=qc, nt=nt, i=i: E.dma_start(
                        out=out[qc * 128:(qc + 1) * 128,
                                nt * 512:(nt + 1) * 512],
                        in_=osb[i][:, :]),
                        reads=[S_osb[i]], writes=[S_out], sem=("st", "st2")[i])

            p.wait_raw("sync", "st", p.cnt["st"])
            p.wait_raw("sync", "st2", p.cnt["st2"])
            p.flush(block)

    return nc


_CACHE = {}


def _host_const_tables():
    inv = 1.0 / (10000.0 ** (np.arange(0, DR, 2, dtype=np.float64) / DR))
    t = np.arange(T, dtype=np.float64)
    fr = np.outer(inv, t)
    cos = np.cos(fr).astype(np.float32)
    sin = np.sin(fr).astype(np.float32)
    cdup = np.concatenate([cos, cos], axis=0).astype(ml_dtypes.bfloat16)
    sdup = np.concatenate([sin, sin], axis=0).astype(ml_dtypes.bfloat16)
    qf = np.arange(512)
    sl = np.arange(128)
    msk = np.concatenate(
        [(qf[None, :] >= (v * 128 + sl[:, None])) for v in range(4)],
        axis=1).astype(ml_dtypes.bfloat16)
    ident = np.eye(128, dtype=ml_dtypes.bfloat16)
    return cdup, sdup, msk, ident


def _prep_shared(I):
    W = np.asarray(I["L0_ffn_W"], dtype=np.float32)
    b = np.asarray(I["L0_ffn_b"], dtype=np.float32)
    wt_aug = np.ascontiguousarray(
        np.concatenate([W.T, b[None, :]], axis=0), dtype=np.float32)
    gw = np.asarray(I["L0_ffn_gate_w"], dtype=np.float32)
    gb = np.asarray(I["L0_ffn_gate_b"], dtype=np.float32)
    gate_aug = np.ascontiguousarray(
        np.concatenate([gw.T, gb[None, :]], axis=0), dtype=np.float32)
    lA = np.asarray(I["L0_ffn_lora_A"], dtype=np.float32)
    lora_at = np.ascontiguousarray(lA.transpose(2, 0, 1).reshape(C, 128))
    lB = np.asarray(I["L0_ffn_lora_B"], dtype=np.float32)
    lora_bc = np.ascontiguousarray(lB.transpose(0, 2, 1).reshape(128, C))
    return wt_aug, gate_aug, lora_at, lora_bc


def _head_cols_up(wn, wr, heads):
    cols = []
    for h in heads:
        cols.append(wn[:, h * DN:(h + 1) * DN])
    for pp in range(2):
        hA, hB = heads[2 * pp], heads[2 * pp + 1]
        evA = wr[:, hA * DR:(hA + 1) * DR][:, 0::2]
        evB = wr[:, hB * DR:(hB + 1) * DR][:, 0::2]
        odA = wr[:, hA * DR:(hA + 1) * DR][:, 1::2]
        odB = wr[:, hB * DR:(hB + 1) * DR][:, 1::2]
        cols.append(np.concatenate([evA, evB, odA, odB], axis=1))
    return np.ascontiguousarray(np.concatenate(cols, axis=1), dtype=np.float32)


def prep_in_maps(inputs):
    I = inputs
    x = np.asarray(I["L0_attn_x"], dtype=np.float32)
    wt_aug, gate_aug, lora_at, lora_bc = _prep_shared(I)
    cdup, sdup, msk, ident = _host_const_tables()
    wq_down = np.ascontiguousarray(np.asarray(I["L0_attn_wq_down"], dtype=np.float32))
    wkv_down = np.ascontiguousarray(np.asarray(I["L0_attn_wkv_down"], dtype=np.float32))
    wq_nope = np.asarray(I["L0_attn_wq_nope"], dtype=np.float32)
    wq_rope = np.asarray(I["L0_attn_wq_rope"], dtype=np.float32)
    wk_nope = np.asarray(I["L0_attn_wk_nope"], dtype=np.float32)
    wk_rope = np.asarray(I["L0_attn_wk_rope"], dtype=np.float32)
    wv_up = np.asarray(I["L0_attn_wv_up"], dtype=np.float32)
    wo = np.ascontiguousarray(np.asarray(I["L0_attn_wo"], dtype=np.float32))

    in_maps = []
    for c in range(NCORE):
        bidx, g = c // 4, c % 4
        heads = [4 * g + j for j in range(4)]
        wv_cols = np.ascontiguousarray(np.concatenate(
            [wv_up[:, h * D:(h + 1) * D] for h in heads], axis=1),
            dtype=np.float32)
        m = {
            "x_own": np.ascontiguousarray(x[bidx, g * TOK:(g + 1) * TOK, :]),
            "wq_down": wq_down,
            "wkv_down": wkv_down,
            "wq_up": _head_cols_up(wq_nope, wq_rope, heads),
            "wk_up": _head_cols_up(wk_nope, wk_rope, heads),
            "wv_up": wv_cols,
            "wo": wo,
            "wt_aug": wt_aug,
            "gate_aug": gate_aug,
            "lora_at": lora_at,
            "lora_bc": lora_bc,
            "cdup": cdup,
            "sdup": sdup,
            "maskc": msk,
            "ident": ident,
            "gidx": np.asarray([[r * T + g * TOK] for r in range(4)],
                               dtype=np.int32),
        }
        in_maps.append(m)
    return in_maps


def kernel(**inputs):
    if "nc" not in _CACHE:
        _CACHE["nc"] = build_program()
    nc = _CACHE["nc"]
    in_maps = prep_in_maps(inputs)
    res = run_bass_kernel_spmd(nc, in_maps, core_ids=list(range(NCORE)))
    outp = np.empty((B, T, C), dtype=np.float32)
    for c in range(NCORE):
        bidx, g = c // 4, c % 4
        outp[bidx, g * TOK:(g + 1) * TOK, :] = res.results[c]["out"]
    return outp


# version 17
# speedup vs baseline: 1.1910x; 1.0242x over previous
import math
import sys
import contextlib

for _p in ("/opt/trn_rl_repo",):
    if _p not in sys.path:
        sys.path.append(_p)

import numpy as np
import ml_dtypes
import concourse.bass as bass
import concourse.mybir as mybir
from concourse.bass_utils import run_bass_kernel_spmd

F32 = mybir.dt.float32
BF16 = mybir.dt.bfloat16
I32 = mybir.dt.int32
AF = mybir.ActivationFunctionType
ALU = mybir.AluOpType
AX = mybir.AxisListType.X

NCORE = 8
B, T, C = 2, 4096, 2048
H, DN, DR, D = 16, 128, 64, 192
QR, KVR = 1536, 512
TOK = 1024
NTC = TOK // 128
ISD = 1.0 / math.sqrt(D)
SHIFT = 12.0
EPS = 1e-5
GROUPS = [[0, 1, 2, 3], [4, 5, 6, 7]]
ENGS = ("sync", "scalar", "vector", "tensor", "gpsimd")
ENG_SEM = {"sync": "ld", "gpsimd": "ldg", "tensor": "tt", "vector": "vv",
           "scalar": "ss"}
ENG_INC = {"sync": 16, "gpsimd": 16, "tensor": 1, "vector": 1, "scalar": 1}


class Slot:
    __slots__ = ("w", "rs")
    def __init__(self):
        self.w = None
        self.rs = []


class Prog:

    def __init__(self):
        self.streams = {e: [] for e in ENGS}
        self.cnt = {"ld": 0, "st": 0, "st2": 0, "ldg": 0, "cc": 0,
                    "tt": 0, "vv": 0, "ss": 0,
                    "lb1": 0, "lw1": 0, "lb2": 0, "lw2": 0, "lw3": 0,
                    "lag": 0, "lag2": 0, "xa": 0, "xb": 0, "qa": 0, "qb": 0,
                    "wa": 0, "wb": 0, "wc": 0, "wd": 0, "sa": 0, "sb": 0,
                    "ka": 0, "kb": 0, "xc": 0, "xd": 0, "qc": 0, "qd": 0,
                    "la0": 0, "la1": 0, "sta": 0}
        self.waited = {e: {} for e in ENGS}
        self.sems = {}

    def op(self, eng, build, reads=(), writes=(), sem=None, inc=None,
           defer=False, raw=False):
        deps = {}
        for s in reads:
            if s.w:
                if deps.get(s.w[0], -1) < s.w[1]:
                    deps[s.w[0]] = s.w[1]
        for s in writes:
            toks = ([s.w] if s.w else []) + s.rs
            for (sm, v) in toks:
                if deps.get(sm, -1) < v:
                    deps[sm] = v
        if sem is None:
            sem = ENG_SEM[eng]
        if inc is None:
            inc = ENG_INC[eng]
        w = self.waited[eng]
        waits = []
        for sname, v in deps.items():
            if v > self.cnt[sname]:
                assert sname == sem and v <= self.cnt[sname] + inc, (
                    eng, sem, sname, v, self.cnt[sname], inc)
                continue
            if w.get(sname, 0) < v:
                waits.append((sname, v))
                w[sname] = v
        if defer:
            tok = (sem, self.cnt[sem] + inc)
            do_inc = False
        else:
            self.cnt[sem] += inc
            tok = (sem, self.cnt[sem])
            do_inc = True

        sems = self.sems

        def thunk(E, waits=waits, build=build, tok=tok, inc=inc,
                  do_inc=do_inc, raw=raw, sem=sem):
            for sname, v in waits:
                E.wait_ge(sems[sname], v)
            ins = build(E)
            if do_inc and not raw:
                ins.then_inc(sems[sem], inc)

        self.streams[eng].append(thunk)
        for s in reads:
            s.rs.append(tok)
        for s in writes:
            s.w = tok
            s.rs = []
        return tok

    def bulk_end(self, slots, sem):
        tok = (sem, self.cnt[sem])
        for s in slots:
            s.w = tok
            s.rs = []

    def wait_raw(self, eng, sem, v):
        w = self.waited[eng]
        if w.get(sem, 0) >= v:
            return
        w[eng if False else sem] = v
        sems = self.sems
        self.streams[eng].append(lambda E, s=sem, v=v: E.wait_ge(sems[s], v))

    def barrier(self, exclude=()):
        snap = dict(self.cnt)
        for e in ENGS:
            for s, v in snap.items():
                if v > 0 and s not in exclude:
                    self.wait_raw(e, s, v)

    def flush(self, block):
        for e in ENGS:
            fns = self.streams[e]
            if not fns:
                continue
            getattr(block, e)(lambda E, fns=tuple(fns): [f(E) for f in fns])
            self.streams[e] = []


def build_program(debug=False):
    nc = bass.Bass("TRN2", target_bir_lowering=False)
    p = Prog()

    def dram(name, shape, dtype, kind):
        return nc.dram_tensor(name, shape, dtype, kind=kind).ap()

    x_own = dram("x_own", [TOK, C], F32, "ExternalInput")
    wq_down = dram("wq_down", [C, QR], F32, "ExternalInput")
    wkv_down = dram("wkv_down", [C, KVR], F32, "ExternalInput")
    wq_up = dram("wq_up", [QR, 768], F32, "ExternalInput")
    wk_up = dram("wk_up", [KVR, 768], F32, "ExternalInput")
    wv_up = dram("wv_up", [KVR, 768], F32, "ExternalInput")
    wo_in = dram("wo", [H * D, C], F32, "ExternalInput")
    wt_aug = dram("wt_aug", [C + 1, C], F32, "ExternalInput")
    gate_aug = dram("gate_aug", [C + 1, 8], F32, "ExternalInput")
    lora_at = dram("lora_at", [C, 128], F32, "ExternalInput")
    lora_bc = dram("lora_bc", [128, C], F32, "ExternalInput")
    cdup_in = dram("cdup", [64, T], BF16, "ExternalInput")
    sdup_in = dram("sdup", [64, T], BF16, "ExternalInput")
    maskc = dram("maskc", [128, 4 * 512], BF16, "ExternalInput")
    ident_in = dram("ident", [128, 128], BF16, "ExternalInput")
    gidx = dram("gidx", [4, 1], I32, "ExternalInput")
    out = dram("out", [TOK, C], F32, "ExternalOutput")

    ikind = "ExternalOutput" if debug else "Internal"
    lat_in = dram("lat_in", [C, TOK], BF16, ikind)
    lat_out_kv = dram("lat_out_kv", [4 * KVR, TOK], BF16, ikind)
    lat_out_q = [dram(f"lat_out_q{h}", [4 * (QR // 2), TOK], BF16, ikind)
                 for h in range(2)]
    at_in = [dram(f"at_in{pp}", [T, 384], BF16, ikind) for pp in range(2)]
    at_out = [dram(f"at_out{pp}", [4 * T, 384], BF16, ikind) for pp in range(2)]

    ctx = contextlib.ExitStack()
    with ctx:
        for sname in p.cnt:
            p.sems[sname] = ctx.enter_context(nc.semaphore(sname))
        block = ctx.enter_context(nc.Block())

        csb = ctx.enter_context(nc.sbuf_tensor("csb", [128, 4], F32))
        gsb = ctx.enter_context(nc.sbuf_tensor("gsb", [128, 17, 8], BF16))
        lat3 = ctx.enter_context(nc.sbuf_tensor("lat3", [128, 16, 128], BF16))
        bias_sb = ctx.enter_context(nc.sbuf_tensor("bias_sb", [1, C], BF16))
        S_gsb = Slot()
        S_lat3 = [Slot() for _ in range(16)]
        S_bias = Slot()
        ident = ctx.enter_context(nc.sbuf_tensor("identsb", [128, 128], BF16))
        ones_row = ctx.enter_context(nc.sbuf_tensor("ones_row", [1, 128], BF16))
        S_csb, S_ident, S_ones = Slot(), Slot(), Slot()
        p.op("vector", lambda E: E.memset(csb[:, 0:1], ISD), writes=[S_csb],
             defer=True)
        p.op("vector", lambda E: E.memset(csb[:, 1:2], -SHIFT), writes=[S_csb])
        p.op("vector", lambda E: E.memset(ones_row[:, :], 1.0), writes=[S_ones])
        p.op("sync", lambda E: E.dma_start(out=ident[:, :], in_=ident_in[:, :]),
             writes=[S_ident], sem="lb1")
        p.bulk_end([S_ident], "lb1")

        def ln_chunk(xb, S_xb, st, S_st, xnb, S_xnb, xsq, S_xsq):
            p.op("scalar", lambda E: E.activation(
                xsq[:, :], xb[:, :], AF.Square, accum_out=st[:, 0:1]),
                reads=[S_xb], writes=[S_xsq, S_st[0]])
            p.op("vector", lambda E: E.reduce_sum(st[:, 1:2], xb[:, :], axis=AX),
                 reads=[S_xb], writes=[S_st[1]])
            p.op("vector", lambda E: E.tensor_scalar(
                st[:, 2:3], st[:, 1:2], 1.0 / C, None, ALU.mult),
                reads=[S_st[1]], writes=[S_st[2]])
            p.op("vector", lambda E: E.tensor_scalar(
                st[:, 3:4], st[:, 0:1], 1.0 / C, EPS, ALU.mult, ALU.add),
                reads=[S_st[0]], writes=[S_st[3]])
            p.op("vector", lambda E: E.scalar_tensor_tensor(
                st[:, 4:5], st[:, 2:3], -1.0, st[:, 2:3], ALU.mult, ALU.mult),
                reads=[S_st[2]], writes=[S_st[4]])
            p.op("vector", lambda E: E.tensor_add(
                st[:, 5:6], st[:, 3:4], st[:, 4:5]),
                reads=[S_st[3], S_st[4]], writes=[S_st[5]])
            p.op("scalar", lambda E: E.activation(st[:, 6:7], st[:, 5:6], AF.Sqrt),
                 reads=[S_st[5]], writes=[S_st[6]])
            p.op("vector", lambda E: E.reciprocal(st[:, 7:8], st[:, 6:7]),
                 reads=[S_st[6]], writes=[S_st[7]])
            p.op("vector", lambda E: E.tensor_mul(st[:, 4:5], st[:, 7:8], st[:, 7:8]),
                 reads=[S_st[7]], writes=[S_st[4]])
            p.op("vector", lambda E: E.tensor_mul(st[:, 4:5], st[:, 4:5], st[:, 5:6]),
                 reads=[S_st[4], S_st[5]], writes=[S_st[4]])
            p.op("vector", lambda E: E.tensor_scalar(
                st[:, 4:5], st[:, 4:5], -0.5, 1.5, ALU.mult, ALU.add),
                reads=[S_st[4]], writes=[S_st[4]])
            p.op("vector", lambda E: E.tensor_mul(st[:, 6:7], st[:, 7:8], st[:, 4:5]),
                 reads=[S_st[7], S_st[4]], writes=[S_st[6]])
            p.op("vector", lambda E: E.scalar_tensor_tensor(
                st[:, 3:4], st[:, 2:3], -1.0, st[:, 6:7], ALU.mult, ALU.mult),
                reads=[S_st[2], S_st[6]], writes=[S_st[3]])
            p.op("scalar", lambda E: E.activation(
                xnb[:, :], xb[:, :], AF.Identity, bias=st[:, 3:4],
                scale=st[:, 6:7]),
                reads=[S_xb, S_st[3], S_st[6]], writes=[S_xnb])

        def transpose128(src_ap, S_srcs, dst_ap, S_dst, ptb, S_ptb):
            p.op("tensor", lambda E: E.transpose(ptb[:, 0:128], src_ap,
                                                 ident[:, :]),
                 reads=list(S_srcs) + [S_ident], writes=[S_ptb])
            p.op("vector", lambda E: E.tensor_copy(dst_ap, ptb[:, 0:128]),
                 reads=[S_ptb], writes=[S_dst])

        S_lat_kv = Slot()
        S_lat_q = [Slot(), Slot()]
        with contextlib.ExitStack() as ph1:
            xb = [ph1.enter_context(nc.sbuf_tensor(f"xb{i}", [128, C], F32))
                  for i in range(4)]
            xsq = [ph1.enter_context(nc.sbuf_tensor("xsq0", [128, C], BF16))] * 4
            stt = [ph1.enter_context(nc.sbuf_tensor(f"stt{i}", [128, 8], F32))
                   for i in range(4)]
            xnb = [ph1.enter_context(nc.sbuf_tensor(f"xnb{i}", [128, C], BF16))
                   for i in range(4)]
            xn_fm = ph1.enter_context(nc.sbuf_tensor("xn_fm", [128, 16, TOK], BF16))
            wqd = ph1.enter_context(nc.sbuf_tensor("wqd", [128, 16, QR], BF16))
            wkvd = ph1.enter_context(nc.sbuf_tensor("wkvd", [128, 16, KVR], BF16))
            lat_sb = [ph1.enter_context(nc.sbuf_tensor(f"lat_sb{i}", [128, 512], BF16))
                      for i in range(2)]
            ptb1 = [ph1.enter_context(nc.psum_tensor(f"ptb1_{i}", [128, 1024], BF16))
                    for i in range(2)]
            pl1 = [ph1.enter_context(nc.psum_tensor(f"pl1_{i}", [128, 512], F32))
                   for i in range(2)]

            S_xb = [Slot() for _ in range(4)]
            S_xsq0 = Slot()
            S_xsq = [S_xsq0] * 4
            S_stt = [[Slot() for _ in range(8)] for _ in range(4)]
            S_xnb = [Slot() for _ in range(4)]
            S_xn = [[Slot() for _ in range(NTC)] for _ in range(16)]
            S_wqd = [Slot() for _ in range(16)]
            S_wkvd = [Slot() for _ in range(16)]
            S_latsb = [Slot(), Slot()]
            S_ptb1 = [Slot(), Slot()]
            S_pl1 = [Slot(), Slot()]

            for k in range(16):
                p.op("gpsimd", lambda E, k=k: E.dma_start(
                    out=wqd[:, k, :], in_=wq_down[k * 128:(k + 1) * 128, :]),
                    writes=[S_wqd[k]], sem="lw1")
                p.op("gpsimd", lambda E, k=k: E.dma_start(
                    out=wkvd[:, k, :], in_=wkv_down[k * 128:(k + 1) * 128, :]),
                    writes=[S_wkvd[k]], sem="lw1")
            p.bulk_end(S_wqd + S_wkvd, "lw1")
            for k in range(16):
                p.op("gpsimd", lambda E, k=k: E.dma_start(
                    out=gsb[:, k, :], in_=gate_aug[k * 128:(k + 1) * 128, :]),
                    writes=[S_gsb], sem="lw3")
            p.op("gpsimd", lambda E: E.dma_start(
                out=gsb[0:1, 16, :], in_=gate_aug[C:C + 1, :]),
                writes=[S_gsb], sem="lw3")
            p.op("gpsimd", lambda E: E.dma_start(
                out=lat3[:, :, :],
                in_=lora_at[:, :].rearrange("(kq p) r -> p kq r", p=128)),
                writes=S_lat3, sem="lw3")
            p.op("gpsimd", lambda E: E.dma_start(
                out=bias_sb[:, :], in_=wt_aug[C:C + 1, :]),
                writes=[S_bias], sem="lw3")
            p.bulk_end([S_gsb, S_bias] + S_lat3, "lw3")


            def emit_lat(mi_list, nt):
                tiles = []
                for mi in mi_list:
                    bank = mi % 2
                    for k in range(16):
                        if mi < 12:
                            lhs = wqd[:, k, mi * 128:(mi + 1) * 128]
                            rsl = S_wqd[k]
                        else:
                            lhs = wkvd[:, k, (mi - 12) * 128:(mi - 11) * 128]
                            rsl = S_wkvd[k]
                        p.op("tensor",
                             lambda E, lhs=lhs, k=k, nt=nt, bank=bank:
                             E.matmul(pl1[bank][:, :], lhs,
                                      xn_fm[:, k, nt * 512:(nt + 1) * 512],
                                      start=(k == 0), stop=(k == 15)),
                             reads=[rsl] + [S_xn[k][nt * 4 + j]
                                            for j in range(4)],
                             writes=[S_pl1[bank]], defer=(k != 15))
                    sbuf = lat_sb[bank]
                    p.op("vector", lambda E, sbuf=sbuf, bank=bank:
                         E.tensor_copy(sbuf[:, :], pl1[bank][:, :]),
                         reads=[S_pl1[bank]], writes=[S_latsb[bank]])
                    S_tile = Slot()
                    tiles.append(S_tile)
                    p.op("sync", lambda E, sbuf=sbuf, mi=mi, nt=nt:
                         E.dma_start(
                             out=lat_in[mi * 128:(mi + 1) * 128,
                                        nt * 512:(nt + 1) * 512],
                             in_=sbuf[:, :]),
                         reads=[S_latsb[bank]], writes=[S_tile],
                         sem=("la0", "la1")[bank])
                return tiles

            def emit_ln(tcs):
                for tc in tcs:
                    i = tc % 4
                    p.op("sync", lambda E, tc=tc, i=i: E.dma_start(
                        out=xb[i][:, :], in_=x_own[tc * 128:(tc + 1) * 128, :]),
                        writes=[S_xb[i]], sem=("xa", "xb", "xc", "xd")[i])
                    ln_chunk(xb[i], S_xb[i], stt[i], S_stt[i], xnb[i], S_xnb[i],
                             xsq[i], S_xsq[i])
                    for k in range(16):
                        transpose128(xnb[i][:, k * 128:(k + 1) * 128], [S_xnb[i]],
                                     xn_fm[:, k, tc * 128:(tc + 1) * 128],
                                     S_xn[k][tc], ptb1[k % 2], S_ptb1[k % 2])

            emit_ln(range(0, 4))
            kv_tiles = emit_lat([12, 13, 14, 15], 0)
            emit_ln(range(4, 8))
            kv_tiles += emit_lat([12, 13, 14, 15], 1)
            p.op("gpsimd", lambda E: E.collective_compute(
                "AllGather", ALU.bypass, replica_groups=GROUPS,
                ins=[lat_in[QR:C, :]], outs=[lat_out_kv[:, :]]),
                reads=kv_tiles, writes=[S_lat_kv], sem="cc", inc=1)
            q_tiles = emit_lat(list(range(6)), 0) + emit_lat(list(range(6)), 1)
            p.op("gpsimd", lambda E: E.collective_compute(
                "AllGather", ALU.bypass, replica_groups=GROUPS,
                ins=[lat_in[0:QR // 2, :]], outs=[lat_out_q[0][:, :]]),
                reads=q_tiles, writes=[S_lat_q[0]], sem="cc", inc=1)
            q_tiles = emit_lat(list(range(6, 12)), 0) + emit_lat(list(range(6, 12)), 1)
            p.op("gpsimd", lambda E: E.collective_compute(
                "AllGather", ALU.bypass, replica_groups=GROUPS,
                ins=[lat_in[QR // 2:QR, :]], outs=[lat_out_q[1][:, :]]),
                reads=q_tiles, writes=[S_lat_q[1]], sem="cc", inc=1)

            p.barrier(exclude=("cc",))
            p.flush(block)

        S_at_in = [Slot(), Slot()]
        S_at_out = [Slot(), Slot()]
        with contextlib.ExitStack() as ph2:
            kvt = [ph2.enter_context(nc.sbuf_tensor(f"kvt{i}", [128, 4, 512], BF16))
                   for i in range(2)]
            qlb = [ph2.enter_context(nc.sbuf_tensor(f"qlb{i}", [128, 12, 512], BF16))
                   for i in range(2)]
            qc2 = ph2.enter_context(nc.sbuf_tensor("qc2", [128, 2, T], BF16))
            qr2 = ph2.enter_context(nc.sbuf_tensor("qr2", [128, T], BF16))
            kc2 = ph2.enter_context(nc.sbuf_tensor("kc2", [128, 2, T], BF16))
            kr2 = ph2.enter_context(nc.sbuf_tensor("kr2", [128, 2, T], BF16))
            v2 = ph2.enter_context(nc.sbuf_tensor("v2", [128, 32, 386], BF16))
            ath = ph2.enter_context(nc.sbuf_tensor("ath", [128, 32, 384], BF16))
            cds = ph2.enter_context(nc.sbuf_tensor("cds", [64, T], BF16))
            sns = ph2.enter_context(nc.sbuf_tensor("sns", [64, T], BF16))
            msk = ph2.enter_context(nc.sbuf_tensor("msk", [128, 4 * 512], BF16))
            wqu = ph2.enter_context(nc.sbuf_tensor("wqu", [128, 12, 768], BF16))
            wku = ph2.enter_context(nc.sbuf_tensor("wku", [128, 4, 768], BF16))
            wvu = ph2.enter_context(nc.sbuf_tensor("wvu", [128, 4, 768], BF16))
            ptile = [ph2.enter_context(nc.sbuf_tensor(f"ptile{i}", [128, 512], BF16))
                     for i in range(3)]
            rA = ph2.enter_context(nc.sbuf_tensor("rA", [128, 512], F32))
            rB = ph2.enter_context(nc.sbuf_tensor("rB", [128, 512], F32))
            rec = ph2.enter_context(nc.sbuf_tensor("rec", [128, 4], F32))
            psT = [ph2.enter_context(nc.psum_tensor(f"ps2_{i}", [128, 512], F32))
                   for i in range(8)]
            ps = [t[:, :] for t in psT]

            S_kvt = [Slot(), Slot()]
            S_qlb = [[Slot(), Slot()] for _ in range(2)]
            S_qc = [[Slot() for _ in range(8)] for _ in range(2)]
            S_qr = [Slot() for _ in range(8)]
            S_kc = [[Slot() for _ in range(8)] for _ in range(2)]
            S_kr = [Slot() for _ in range(8)]
            S_v = [[Slot() for _ in range(2)] for _ in range(32)]
            S_vones = Slot()
            S_ath = [[Slot() for _ in range(2)] for _ in range(32)]
            S_cds, S_sns, S_msk = Slot(), Slot(), Slot()
            S_wqu = [Slot() for _ in range(12)]
            S_wku = [Slot() for _ in range(4)]
            S_wvu = [Slot() for _ in range(4)]
            S_pt = [Slot() for _ in range(3)]
            S_rA, S_rB, S_rec = Slot(), Slot(), Slot()
            S_ps = [Slot() for _ in range(8)]

            p.op("sync", lambda E: E.dma_start(out=cds[:, :], in_=cdup_in[:, :]),
                 writes=[S_cds], sem="lb2")
            p.op("sync", lambda E: E.dma_start(out=sns[:, :], in_=sdup_in[:, :]),
                 writes=[S_sns], sem="lb2")
            p.op("sync", lambda E: E.dma_start(out=msk[:, :], in_=maskc[:, :]),
                 writes=[S_msk], sem="lb2")
            for k in range(12):
                p.op("gpsimd", lambda E, k=k: E.dma_start(
                    out=wqu[:, k, :], in_=wq_up[k * 128:(k + 1) * 128, :]),
                    writes=[S_wqu[k]], sem="lw2")
            for k in range(4):
                p.op("gpsimd", lambda E, k=k: E.dma_start(
                    out=wku[:, k, :], in_=wk_up[k * 128:(k + 1) * 128, :]),
                    writes=[S_wku[k]], sem="lw2")
                p.op("gpsimd", lambda E, k=k: E.dma_start(
                    out=wvu[:, k, :], in_=wv_up[k * 128:(k + 1) * 128, :]),
                    writes=[S_wvu[k]], sem="lw2")
            p.bulk_end(S_wqu + S_wku + S_wvu, "lw2")
            p.bulk_end([S_cds, S_sns, S_msk], "lb2")
            p.op("vector", lambda E: E.memset(v2[:, :, 192:193], 1.0),
                 writes=[S_vones], defer=True)
            p.op("vector", lambda E: E.memset(v2[:, :, 385:386], 1.0),
                 writes=[S_vones])
            S_krz = Slot()
            p.op("vector", lambda E: E.memset(kr2[64:128, 0, :], 0.0),
                 writes=[S_krz], defer=True)
            p.op("vector", lambda E: E.memset(kr2[0:64, 1, :], 0.0),
                 writes=[S_krz])

            def rope_store(pbank, S_pbank, t0, dstq, S_dst, dsts=None):
                n = 512
                p.op("vector", lambda E: E.tensor_mul(
                    rA[0:64, :], pbank[0:64, :], cds[:, t0:t0 + n]),
                    reads=[S_pbank, S_cds], writes=[S_rA], defer=True)
                p.op("vector", lambda E: E.tensor_mul(
                    rA[64:128, :], pbank[64:128, :], cds[:, t0:t0 + n]),
                    reads=[S_pbank, S_cds], writes=[S_rA])
                p.op("vector", lambda E: E.tensor_mul(
                    rB[0:64, :], pbank[64:128, :], sns[:, t0:t0 + n]),
                    reads=[S_pbank, S_sns], writes=[S_rB], defer=True)
                p.op("vector", lambda E: E.tensor_mul(
                    rB[64:128, :], pbank[0:64, :], sns[:, t0:t0 + n]),
                    reads=[S_pbank, S_sns], writes=[S_rB])
                for hh in range(2):
                    d1 = (dstq[hh * 64:hh * 64 + 32, :] if dsts is None
                          else dsts[hh][0])
                    d2 = (dstq[hh * 64 + 32:hh * 64 + 64, :] if dsts is None
                          else dsts[hh][1])
                    p.op("vector", lambda E, hh=hh, d1=d1: E.tensor_sub(
                        d1, rA[hh * 32:hh * 32 + 32, :],
                        rB[hh * 32:hh * 32 + 32, :]),
                        reads=[S_rA, S_rB], writes=[S_dst], defer=True)
                    p.op("vector", lambda E, hh=hh, d2=d2: E.tensor_add(
                        d2, rB[64 + hh * 32:96 + hh * 32, :],
                        rA[64 + hh * 32:96 + hh * 32, :]),
                        reads=[S_rA, S_rB], writes=[S_dst],
                        defer=(hh == 0))

            for pp in range(2):
                for nt in range(8):
                    rr = nt // 2
                    ch = nt % 2
                    i = nt % 2
                    p.op("scalar", lambda E, rr=rr, ch=ch, i=i: E.dma_start(
                        out=kvt[i][:, :, :],
                        in_=lat_out_kv[rr * KVR:(rr + 1) * KVR,
                                       ch * 512:(ch + 1) * 512].rearrange(
                            "(kk pk) n -> pk kk n", pk=128)),
                        reads=[S_lat_kv], writes=[S_kvt[i]],
                        sem=("ka", "kb")[i], inc=16)
                    for mj, mi in enumerate((2 * pp, 2 * pp + 1, 4 + pp)):
                        for kk in range(4):
                            p.op("tensor",
                                 lambda E, mi=mi, kk=kk, i=i, mj=mj: E.matmul(
                                     ps[mj],
                                     wku[:, kk, mi * 128:(mi + 1) * 128],
                                     kvt[i][:, kk, :],
                                     start=(kk == 0), stop=(kk == 3)),
                                 reads=[S_wku[kk], S_kvt[i]],
                                 writes=[S_ps[mj]], defer=(kk != 3))
                    for mj in range(2):
                        p.op("vector", lambda E, mj=mj, nt=nt: E.tensor_copy(
                            kc2[:, mj, nt * 512:(nt + 1) * 512], ps[mj]),
                            reads=[S_ps[mj]], writes=[S_kc[mj][nt]])
                    rope_store(
                        ps[2], S_ps[2], nt * 512, None, S_kr[nt],
                        dsts=[(kr2[0:32, 0, nt * 512:(nt + 1) * 512],
                               kr2[32:64, 0, nt * 512:(nt + 1) * 512]),
                              (kr2[64:96, 1, nt * 512:(nt + 1) * 512],
                               kr2[96:128, 1, nt * 512:(nt + 1) * 512])])
                    for tcl in range(4):
                        tc = nt * 4 + tcl
                        bank = (3, 4, 5, 7)[tcl]
                        for kk in range(4):
                            p.op("tensor",
                                 lambda E, kk=kk, tcl=tcl, pp=pp, i=i, bank=bank:
                                 E.matmul(
                                     ps[bank][:, 0:384],
                                     kvt[i][:, kk, tcl * 128:(tcl + 1) * 128],
                                     wvu[:, kk, 2 * pp * 192:2 * pp * 192 + 384],
                                     start=(kk == 0), stop=(kk == 3)),
                                 reads=[S_wvu[kk], S_kvt[i]],
                                 writes=[S_ps[bank]], defer=(kk != 3))
                        p.op("vector",
                             lambda E, tc=tc, bank=bank: E.tensor_copy(
                                 v2[:, tc, 0:386].rearrange(
                                     "p (h x) -> p h x", h=2)[:, :, 0:192],
                                 ps[bank][:, 0:384].rearrange(
                                     "p (h x) -> p h x", h=2)),
                             reads=[S_ps[bank]],
                             writes=[S_v[tc][0], S_v[tc][1]])

                for r in range(4):
                    for nh in range(2):
                        i = (r * 2 + nh) % 2
                        for h in range(2):
                            p.op("sync", lambda E, r=r, nh=nh, i=i, h=h:
                                 E.dma_start(
                                     out=qlb[i][:, 6 * h:6 * h + 6, :],
                                     in_=lat_out_q[h][
                                         r * (QR // 2):(r + 1) * (QR // 2),
                                         nh * 512:(nh + 1) * 512].rearrange(
                                         "(kq pq) n -> pq kq n", pq=128)),
                                 reads=[S_lat_q[h]], writes=[S_qlb[i][h]],
                                 sem=("qa", "qb", "qc", "qd")[i * 2 + h])
                        t0 = r * TOK + nh * 512
                        for mj, mi in enumerate((2 * pp, 2 * pp + 1, 4 + pp)):
                            for k in range(12):
                                p.op("tensor",
                                     lambda E, mi=mi, k=k, i=i, mj=mj:
                                     E.matmul(ps[mj],
                                              wqu[:, k, mi * 128:(mi + 1) * 128],
                                              qlb[i][:, k, :],
                                              start=(k == 0), stop=(k == 11)),
                                     reads=[S_wqu[k], S_qlb[i][k // 6]],
                                     writes=[S_ps[mj]], defer=(k != 11))
                        for mj in range(2):
                            p.op("vector",
                                 lambda E, mj=mj, t0=t0: E.tensor_copy(
                                     qc2[:, mj, t0:t0 + 512], ps[mj]),
                                 reads=[S_ps[mj]], writes=[S_qc[mj][t0 // 512]])
                        rope_store(ps[2], S_ps[2], t0,
                                   qr2[:, t0:t0 + 512], S_qr[t0 // 512])

                for hl in range(2):
                    for qt in range(8):
                        nsc = (qt + 1) * 4
                        q0 = qt * 512

                        def emit_scores(sc, hl=hl, q0=q0):
                            sb = 4 + (sc % 3)
                            p.op("tensor",
                                 lambda E, hl=hl, sc=sc, q0=q0, sb=sb: E.matmul(
                                     ps[sb],
                                     kc2[:, hl, sc * 128:(sc + 1) * 128],
                                     qc2[:, hl, q0:q0 + 512],
                                     start=True, stop=False),
                                 reads=[S_kc[hl][sc // 4], S_qc[hl][q0 // 512]],
                                 writes=[S_ps[sb]], defer=True)
                            p.op("tensor",
                                 lambda E, hl=hl, sc=sc, q0=q0, sb=sb: E.matmul(
                                     ps[sb],
                                     kr2[:, hl, sc * 128:(sc + 1) * 128],
                                     qr2[:, q0:q0 + 512],
                                     start=False, stop=True),
                                 reads=[S_kr[sc // 4], S_qr[q0 // 512], S_krz],
                                 writes=[S_ps[sb]])

                        def emit_exp(sc, qt=qt):
                            sb = 4 + (sc % 3)
                            pt = ptile[sc % 3]
                            ipt = sc % 3
                            p.op("scalar", lambda E, sb=sb, pt=pt: E.activation(
                                pt[:, :], ps[sb], AF.Exp,
                                bias=csb[:, 1:2], scale=csb[:, 0:1]),
                                reads=[S_ps[sb], S_csb], writes=[S_pt[ipt]])
                            if sc >= qt * 4:
                                off = (sc - qt * 4) * 512
                                p.op("vector", lambda E, pt=pt, off=off:
                                     E.tensor_mul(pt[:, :], pt[:, :],
                                                  msk[:, off:off + 512]),
                                     reads=[S_pt[ipt], S_msk],
                                     writes=[S_pt[ipt]])

                        def emit_pv(sc, hl=hl, nsc=nsc):
                            pt = ptile[sc % 3]
                            ipt = sc % 3
                            for qc4 in range(4):
                                p.op("tensor",
                                     lambda E, pt=pt, qc4=qc4, sc=sc, hl=hl,
                                     nsc=nsc: E.matmul(
                                         ps[qc4][:, 0:193],
                                         pt[:, qc4 * 128:(qc4 + 1) * 128],
                                         v2[:, sc, hl * 193:(hl + 1) * 193],
                                         start=(sc == 0), stop=(sc == nsc - 1),
                                         skip_group_check=True),
                                     reads=[S_pt[ipt], S_v[sc][hl], S_vones],
                                     writes=[S_ps[qc4]], defer=(qc4 != 3))

                        emit_scores(0)
                        if nsc > 1:
                            emit_scores(1)
                        for sc in range(nsc):
                            emit_exp(sc)
                            if sc + 2 < nsc:
                                emit_scores(sc + 2)
                            emit_pv(sc)
                        for qc4 in range(4):
                            tc = qt * 4 + qc4
                            p.op("vector", lambda E, qc4=qc4: E.reciprocal(
                                rec[:, qc4:qc4 + 1], ps[qc4][:, 192:193]),
                                reads=[S_ps[qc4]], writes=[S_rec])
                            p.op("vector",
                                 lambda E, qc4=qc4, tc=tc, hl=hl: E.tensor_scalar(
                                     ath[:, tc, hl * 192:(hl + 1) * 192],
                                     ps[qc4][:, 0:192], rec[:, qc4:qc4 + 1],
                                     None, ALU.mult),
                                 reads=[S_ps[qc4], S_rec], writes=[S_ath[tc][hl]])

                ath_deps = [S_ath[tc][hl] for tc in range(32) for hl in range(2)]
                S_store = Slot()
                for q4 in range(4):
                    p.op("sync", lambda E, q4=q4, pp=pp: E.dma_start(
                        out=at_in[pp][q4 * 1024:(q4 + 1) * 1024, :].rearrange(
                            "(c p) f -> p c f", p=128),
                        in_=ath[:, q4 * 8:(q4 + 1) * 8, :]),
                        reads=ath_deps, writes=[S_store], sem="sta")
                p.op("gpsimd", lambda E, pp=pp: E.collective_compute(
                    "AllGather", ALU.bypass, replica_groups=GROUPS,
                    ins=[at_in[pp][:, :]], outs=[at_out[pp][:, :]]),
                    reads=[S_store, S_at_in[pp]], writes=[S_at_out[pp]],
                    sem="cc", inc=1)

            p.barrier(exclude=("cc",))
            p.flush(block)

        with contextlib.ExitStack() as ph3:
            xn3 = ph3.enter_context(nc.sbuf_tensor("xn3", [128, 16, TOK], BF16))
            afm = ph3.enter_context(nc.sbuf_tensor("afm", [128, 24, TOK], BF16))
            lgs = ph3.enter_context(nc.sbuf_tensor("lgs", [128, 8], F32))
            t2 = ph3.enter_context(nc.sbuf_tensor("t2", [128, 32], F32))
            wdn = ph3.enter_context(nc.sbuf_tensor("wdn", [128, 8, 8], F32))
            xag = ph3.enter_context(nc.sbuf_tensor("xag", [128, 128], BF16))
            xafm = ph3.enter_context(nc.sbuf_tensor("xafm", [128, TOK], BF16))
            xsl = [ph3.enter_context(nc.sbuf_tensor(f"xsl{i}", [128, 512], F32))
                   for i in range(2)]
            wst = [ph3.enter_context(nc.sbuf_tensor(f"wst{i}", [128, 512], F32))
                   for i in range(4)]
            osb = [ph3.enter_context(nc.sbuf_tensor(f"osb{i}", [128, 512], F32))
                   for i in range(2)]
            ptb3 = [ph3.enter_context(nc.psum_tensor(f"ptb3_{i}", [128, 1024], BF16))
                    for i in range(2)]
            pf = [ph3.enter_context(nc.psum_tensor(f"pf{i}", [128, 512], F32))
                  for i in range(4)]
            ph3a = contextlib.ExitStack()
            aown = ph3a.enter_context(nc.sbuf_tensor("aown", [128, 8, 3072], BF16))
            xb3 = [ph3a.enter_context(nc.sbuf_tensor(f"xb3_{i}", [128, C], F32))
                   for i in range(4)]
            xsq3 = [ph3a.enter_context(nc.sbuf_tensor("xsq3_0", [128, C], BF16))] * 4
            st3 = [ph3a.enter_context(nc.sbuf_tensor(f"st3_{i}", [128, 8], F32))
                   for i in range(4)]
            xnb3 = [ph3a.enter_context(nc.sbuf_tensor(f"xnb3_{i}", [128, C], BF16))
                    for i in range(2)] * 2

            S_aownc = [Slot() for _ in range(8)]
            S_xb3 = [Slot() for _ in range(4)]
            S_xsq3_0 = Slot()
            S_xsq3 = [S_xsq3_0] * 4
            S_st3 = [[Slot() for _ in range(8)] for _ in range(4)]
            S_xnb3 = [Slot(), Slot()] * 2
            S_xn3 = [[Slot() for _ in range(NTC)] for _ in range(16)]
            S_afm = [[Slot() for _ in range(NTC)] for _ in range(24)]
            S_lgs = Slot()
            S_t2 = Slot()
            S_wdn = [Slot() for _ in range(8)]
            S_xag = Slot()
            S_xafm = [Slot() for _ in range(8)]
            S_wnt = [[Slot() for _ in range(41)] for _ in range(2)]
            S_wst = [Slot() for _ in range(4)]
            S_xsl = [Slot(), Slot()]
            S_osb = [Slot(), Slot()]
            S_ptb3 = [Slot(), Slot()]
            S_pf = [Slot() for _ in range(4)]
            S_out = Slot()



            cc_toks = [S_at_out[0].w, S_at_out[1].w]

            def aown_dmas(E):
                sems = p.sems
                last = None
                with E.register("goff") as rg:
                    offs = []
                    for r in range(4):
                        E.reg_load(rg, gidx[r:r + 1, 0:1])
                        offs.append(E.snap(rg, min_val=0,
                                           max_val=3 * T + 3 * TOK))
                    for ppx in range(2):
                        E.wait_ge(sems[cc_toks[ppx][0]], cc_toks[ppx][1])
                        for r in range(4):
                            col = r * 768 + ppx * 384
                            last = E.dma_start(
                                out=aown[:, :, col:col + 384],
                                in_=at_out[ppx][bass.ds(offs[r], TOK), :].rearrange(
                                    "(c p) f -> p c f", p=128))
                            last.then_inc(sems[("lag", "lag2")[ppx]], 16)
                return last

            p.op("gpsimd", aown_dmas, reads=[],
                 writes=S_aownc, sem="lag", inc=4 * 16, raw=True)
            p.cnt["lag2"] += 4 * 16
            for r in range(4):
                S_aownc[2 * r + 0].w = ("lag", p.cnt["lag"])
                S_aownc[2 * r + 1].w = ("lag2", p.cnt["lag2"])

            for tc in range(NTC):
                i = tc % 4
                p.op("sync", lambda E, tc=tc, i=i: E.dma_start(
                    out=xb3[i][:, :], in_=x_own[tc * 128:(tc + 1) * 128, :]),
                    writes=[S_xb3[i]], sem=("xa", "xb", "xc", "xd")[i])
                ln_chunk(xb3[i], S_xb3[i], st3[i], S_st3[i], xnb3[i],
                         S_xnb3[i], xsq3[i], S_xsq3[i])
                for k in range(16):
                    transpose128(xnb3[i][:, k * 128:(k + 1) * 128], [S_xnb3[i]],
                                 xn3[:, k, tc * 128:(tc + 1) * 128],
                                 S_xn3[k][tc], ptb3[k % 2], S_ptb3[k % 2])

            for qc in range(8):
                for k in range(16):
                    p.op("tensor", lambda E, k=k, qc=qc: E.matmul(
                        pf[0][:, 0:8], xn3[:, k, qc * 128:(qc + 1) * 128],
                        gsb[:, k, :], start=(k == 0), stop=False),
                        reads=[S_xn3[k][qc], S_gsb], writes=[S_pf[0]],
                        defer=True)
                p.op("tensor", lambda E: E.matmul(
                    pf[0][:, 0:8], ones_row[:, :], gsb[0:1, 16, :],
                    start=False, stop=True),
                    reads=[S_ones, S_gsb], writes=[S_pf[0]])
                p.op("vector", lambda E: E.tensor_copy(lgs[:, :], pf[0][:, 0:8]),
                     reads=[S_pf[0]], writes=[S_lgs])
                p.op("vector", lambda E: E.reduce_max(t2[:, 8:9], lgs[:, :], axis=AX),
                     reads=[S_lgs], writes=[S_t2])
                p.op("vector", lambda E, qc=qc: E.tensor_tensor(
                    wdn[:, qc, :], lgs[:, :],
                    t2[:, 8:9].to_broadcast((128, 8)), ALU.is_ge),
                    reads=[S_lgs, S_t2], writes=[S_wdn[qc]])
                p.op("vector", lambda E, qc=qc: E.scalar_tensor_tensor(
                    t2[:, 16:24], wdn[:, qc, :], -1e30, lgs[:, :],
                    ALU.mult, ALU.add),
                    reads=[S_lgs, S_wdn[qc]], writes=[S_t2])
                p.op("vector", lambda E: E.reduce_max(
                    t2[:, 9:10], t2[:, 16:24], axis=AX),
                    reads=[S_t2], writes=[S_t2])
                p.op("vector", lambda E: E.tensor_tensor(
                    t2[:, 24:32], t2[:, 16:24],
                    t2[:, 9:10].to_broadcast((128, 8)), ALU.is_ge),
                    reads=[S_t2], writes=[S_t2])
                p.op("vector", lambda E: E.tensor_sub(
                    t2[:, 10:11], t2[:, 8:9], t2[:, 9:10]),
                    reads=[S_t2], writes=[S_t2])
                p.op("scalar", lambda E: E.activation(
                    t2[:, 11:12], t2[:, 10:11], AF.Sigmoid),
                    reads=[S_t2], writes=[S_t2])
                p.op("scalar", lambda E: E.activation(
                    t2[:, 12:13], t2[:, 10:11], AF.Sigmoid, scale=-1.0),
                    reads=[S_t2], writes=[S_t2])
                p.op("vector", lambda E, qc=qc: E.tensor_scalar(
                    wdn[:, qc, :], wdn[:, qc, :], t2[:, 11:12], None, ALU.mult),
                    reads=[S_wdn[qc], S_t2], writes=[S_wdn[qc]])
                p.op("vector", lambda E, qc=qc: E.scalar_tensor_tensor(
                    wdn[:, qc, :], t2[:, 24:32], t2[:, 12:13], wdn[:, qc, :],
                    ALU.mult, ALU.add),
                    reads=[S_t2, S_wdn[qc]], writes=[S_wdn[qc]])
                for k in range(16):
                    p.op("tensor", lambda E, k=k, qc=qc: E.matmul(
                        pf[1][:, 0:128], xn3[:, k, qc * 128:(qc + 1) * 128],
                        lat3[:, k, :], start=(k == 0), stop=(k == 15)),
                        reads=[S_xn3[k][qc], S_lat3[k]], writes=[S_pf[1]],
                        defer=(k != 15))
                p.op("vector", lambda E, qc=qc: E.tensor_tensor(
                    xag[:, :].rearrange("p (e r) -> p e r", r=16),
                    pf[1][:, 0:128].rearrange("p (e r) -> p e r", r=16),
                    wdn[:, qc, :][:, :, None].to_broadcast((128, 8, 16)),
                    ALU.mult),
                    reads=[S_pf[1], S_wdn[qc]], writes=[S_xag])
                transpose128(xag[:, :], [S_xag], xafm[:, qc * 128:(qc + 1) * 128],
                             S_xafm[qc], ptb3[0], S_ptb3[0])


            for tc in range(NTC):
                for k in range(24):
                    transpose128(aown[:, tc, k * 128:(k + 1) * 128],
                                 [S_aownc[k // 3]],
                                 afm[:, k, tc * 128:(tc + 1) * 128],
                                 S_afm[k][tc], ptb3[k % 2], S_ptb3[k % 2])

            for k in range(4):
                srcap = wo_in[k * 128:(k + 1) * 128, 0:512]
                p.op("sync", lambda E, srcap=srcap, k=k: E.dma_start(
                    out=wst[k][:, :], in_=srcap),
                    writes=[S_wst[k]], sem=("wa", "wb", "wc", "wd")[k])
            p.barrier(exclude=("cc", "wa", "wb", "wc", "wd"))
            ph3a.close()
            wnt = [ph3.enter_context(nc.sbuf_tensor(f"wnt{i}", [128, 41, 512], BF16))
                   for i in range(2)]

            def emit_w_loads(nt, ks):
                for k in ks:
                    if k < 24:
                        srcap = wo_in[k * 128:(k + 1) * 128,
                                      nt * 512:(nt + 1) * 512]
                    elif k < 40:
                        srcap = wt_aug[(k - 24) * 128:(k - 23) * 128,
                                       nt * 512:(nt + 1) * 512]
                    else:
                        srcap = lora_bc[:, nt * 512:(nt + 1) * 512]
                    j = k % 4
                    p.op("sync", lambda E, srcap=srcap, j=j: E.dma_start(
                        out=wst[j][:, :], in_=srcap),
                        writes=[S_wst[j]], sem=("wa", "wb", "wc", "wd")[j])

            def emit_w_casts(nt, ks):
                wb = nt % 2
                for k in ks:
                    j = k % 4
                    p.op("vector", lambda E, k=k, wb=wb, j=j: E.tensor_copy(
                        wnt[wb][:, k, :], wst[j][:, :]),
                        reads=[S_wst[j]], writes=[S_wnt[wb][k]])

            def emit_w_chunk(nt, ks):
                for k in ks:
                    emit_w_loads(nt, [k])
                    emit_w_casts(nt, [k])

            emit_w_casts(0, range(4))
            emit_w_chunk(0, range(4, 41))
            for nt in range(4):
                wb = nt % 2
                for qc in range(8):
                    if nt + 1 < 4:
                        emit_w_chunk(nt + 1,
                                     range(qc * 6, min((qc + 1) * 6, 41)))
                    bank = 2 + qc % 2
                    for k in range(24):
                        p.op("tensor", lambda E, k=k, qc=qc, wb=wb, bank=bank:
                             E.matmul(pf[bank][:, :],
                                      afm[:, k, qc * 128:(qc + 1) * 128],
                                      wnt[wb][:, k, :],
                                      start=(k == 0), stop=False),
                             reads=[S_afm[k][qc], S_wnt[wb][k]],
                             writes=[S_pf[bank]], defer=True)
                    for k in range(16):
                        p.op("tensor", lambda E, k=k, qc=qc, wb=wb, bank=bank:
                             E.matmul(pf[bank][:, :],
                                      xn3[:, k, qc * 128:(qc + 1) * 128],
                                      wnt[wb][:, 24 + k, :],
                                      start=False, stop=False),
                             reads=[S_xn3[k][qc], S_wnt[wb][24 + k]],
                             writes=[S_pf[bank]], defer=True)
                    p.op("tensor", lambda E, qc=qc, nt=nt, bank=bank: E.matmul(
                        pf[bank][:, :], ones_row[:, :],
                        bias_sb[:, nt * 512:(nt + 1) * 512],
                        start=False, stop=False),
                        reads=[S_ones, S_bias], writes=[S_pf[bank]], defer=True)
                    p.op("tensor", lambda E, qc=qc, wb=wb, bank=bank: E.matmul(
                        pf[bank][:, :], xafm[:, qc * 128:(qc + 1) * 128],
                        wnt[wb][:, 40, :], start=False, stop=True),
                        reads=[S_xafm[qc], S_wnt[wb][40]],
                        writes=[S_pf[bank]])
                    i = qc % 2
                    p.op("sync", lambda E, qc=qc, nt=nt, i=i: E.dma_start(
                        out=xsl[i][:, :],
                        in_=x_own[qc * 128:(qc + 1) * 128,
                                  nt * 512:(nt + 1) * 512]),
                        writes=[S_xsl[i]], sem=("sa", "sb")[i])
                    p.op("vector", lambda E, i=i, bank=bank: E.tensor_add(
                        osb[i][:, :], pf[bank][:, :], xsl[i][:, :]),
                        reads=[S_pf[bank], S_xsl[i]], writes=[S_osb[i]])
                    p.op("sync", lambda E, qc=qc, nt=nt, i=i: E.dma_start(
                        out=out[qc * 128:(qc + 1) * 128,
                                nt * 512:(nt + 1) * 512],
                        in_=osb[i][:, :]),
                        reads=[S_osb[i]], writes=[S_out], sem=("st", "st2")[i])

            p.wait_raw("sync", "st", p.cnt["st"])
            p.wait_raw("sync", "st2", p.cnt["st2"])
            p.flush(block)

    return nc


_CACHE = {}


def _host_const_tables():
    inv = 1.0 / (10000.0 ** (np.arange(0, DR, 2, dtype=np.float64) / DR))
    t = np.arange(T, dtype=np.float64)
    fr = np.outer(inv, t)
    cos = np.cos(fr).astype(np.float32)
    sin = np.sin(fr).astype(np.float32)
    cdup = np.concatenate([cos, cos], axis=0).astype(ml_dtypes.bfloat16)
    sdup = np.concatenate([sin, sin], axis=0).astype(ml_dtypes.bfloat16)
    qf = np.arange(512)
    sl = np.arange(128)
    msk = np.concatenate(
        [(qf[None, :] >= (v * 128 + sl[:, None])) for v in range(4)],
        axis=1).astype(ml_dtypes.bfloat16)
    ident = np.eye(128, dtype=ml_dtypes.bfloat16)
    return cdup, sdup, msk, ident


def _prep_shared(I):
    W = np.asarray(I["L0_ffn_W"], dtype=np.float32)
    b = np.asarray(I["L0_ffn_b"], dtype=np.float32)
    wt_aug = np.ascontiguousarray(
        np.concatenate([W.T, b[None, :]], axis=0), dtype=np.float32)
    gw = np.asarray(I["L0_ffn_gate_w"], dtype=np.float32)
    gb = np.asarray(I["L0_ffn_gate_b"], dtype=np.float32)
    gate_aug = np.ascontiguousarray(
        np.concatenate([gw.T, gb[None, :]], axis=0), dtype=np.float32)
    lA = np.asarray(I["L0_ffn_lora_A"], dtype=np.float32)
    lora_at = np.ascontiguousarray(lA.transpose(2, 0, 1).reshape(C, 128))
    lB = np.asarray(I["L0_ffn_lora_B"], dtype=np.float32)
    lora_bc = np.ascontiguousarray(lB.transpose(0, 2, 1).reshape(128, C))
    return wt_aug, gate_aug, lora_at, lora_bc


def _head_cols_up(wn, wr, heads):
    cols = []
    for h in heads:
        cols.append(wn[:, h * DN:(h + 1) * DN])
    for pp in range(2):
        hA, hB = heads[2 * pp], heads[2 * pp + 1]
        evA = wr[:, hA * DR:(hA + 1) * DR][:, 0::2]
        evB = wr[:, hB * DR:(hB + 1) * DR][:, 0::2]
        odA = wr[:, hA * DR:(hA + 1) * DR][:, 1::2]
        odB = wr[:, hB * DR:(hB + 1) * DR][:, 1::2]
        cols.append(np.concatenate([evA, evB, odA, odB], axis=1))
    return np.ascontiguousarray(np.concatenate(cols, axis=1), dtype=np.float32)


def prep_in_maps(inputs):
    I = inputs
    x = np.asarray(I["L0_attn_x"], dtype=np.float32)
    wt_aug, gate_aug, lora_at, lora_bc = _prep_shared(I)
    cdup, sdup, msk, ident = _host_const_tables()
    wq_down = np.ascontiguousarray(np.asarray(I["L0_attn_wq_down"], dtype=np.float32))
    wkv_down = np.ascontiguousarray(np.asarray(I["L0_attn_wkv_down"], dtype=np.float32))
    wq_nope = np.asarray(I["L0_attn_wq_nope"], dtype=np.float32)
    wq_rope = np.asarray(I["L0_attn_wq_rope"], dtype=np.float32)
    wk_nope = np.asarray(I["L0_attn_wk_nope"], dtype=np.float32)
    wk_rope = np.asarray(I["L0_attn_wk_rope"], dtype=np.float32)
    wv_up = np.asarray(I["L0_attn_wv_up"], dtype=np.float32)
    wo = np.ascontiguousarray(np.asarray(I["L0_attn_wo"], dtype=np.float32))

    in_maps = []
    for c in range(NCORE):
        bidx, g = c // 4, c % 4
        heads = [4 * g + j for j in range(4)]
        wv_cols = np.ascontiguousarray(np.concatenate(
            [wv_up[:, h * D:(h + 1) * D] for h in heads], axis=1),
            dtype=np.float32)
        m = {
            "x_own": np.ascontiguousarray(x[bidx, g * TOK:(g + 1) * TOK, :]),
            "wq_down": wq_down,
            "wkv_down": wkv_down,
            "wq_up": _head_cols_up(wq_nope, wq_rope, heads),
            "wk_up": _head_cols_up(wk_nope, wk_rope, heads),
            "wv_up": wv_cols,
            "wo": wo,
            "wt_aug": wt_aug,
            "gate_aug": gate_aug,
            "lora_at": lora_at,
            "lora_bc": lora_bc,
            "cdup": cdup,
            "sdup": sdup,
            "maskc": msk,
            "ident": ident,
            "gidx": np.asarray([[r * T + g * TOK] for r in range(4)],
                               dtype=np.int32),
        }
        in_maps.append(m)
    return in_maps


def kernel(**inputs):
    if "nc" not in _CACHE:
        _CACHE["nc"] = build_program()
    nc = _CACHE["nc"]
    in_maps = prep_in_maps(inputs)
    res = run_bass_kernel_spmd(nc, in_maps, core_ids=list(range(NCORE)))
    outp = np.empty((B, T, C), dtype=np.float32)
    for c in range(NCORE):
        bidx, g = c // 4, c % 4
        outp[bidx, g * TOK:(g + 1) * TOK, :] = res.results[c]["out"]
    return outp
